# version 01
# baseline (speedup 1.0000x reference)
from contextlib import ExitStack

import numpy as np
import ml_dtypes

import concourse.bass as bass
import concourse.mybir as mybir
from concourse.bass_utils import run_bass_kernel_spmd

F32 = mybir.dt.float32
BF16 = mybir.dt.bfloat16
AF = mybir.ActivationFunctionType
ALU = mybir.AluOpType
BF = ml_dtypes.bfloat16


class Cfg:
    def __init__(self, T=2048, D=1024, HF=4096, Er=7, E=8, group=4,
                 ln_eps=1e-5, attn_eps=1e-6):
        assert T % 512 == 0 and D % 128 == 0 and HF % 512 == 0
        self.T, self.D, self.HF, self.Er, self.E = T, D, HF, Er, E
        self.DC = D // 128
        self.TC = T // 128
        self.FT = T // 512
        self.HB = HF // 512
        self.NH = D // 64
        self.NP = D // 128
        self.group = group
        self.ln_eps, self.attn_eps = ln_eps, attn_eps


class Sem:
    def __init__(self, nc, name):
        self.h = nc.alloc_semaphore(name)
        self.n = 0

    def bump(self, amount=1):
        self.n += amount
        return self.n


class Builder:
    ENGS = ("t", "v", "s", "g", "y")

    def __init__(self, nc):
        self.nc = nc
        self.q = {e: [] for e in self.ENGS}
        self.sem = {k: Sem(nc, k) for k in
                    ("st", "sv", "ss", "scc", "sgd", "syd", "su", "sg2", "sd2",
                     "sw", "swb", "sx", "sconst", "sconsg", "se16", "sci", "syo")}
        self.cons = {}

    def wait(self, eng, sem, val):
        if val is None or val <= 0:
            return
        h = sem.h
        self.q[eng].append(lambda M: M.wait_ge(h, val))

    def wait_slot(self, eng, key):
        c = self.cons.get(key)
        if c is not None:
            self.wait(eng, c[0], c[1])

    def consumed(self, key, sem, val):
        self.cons[key] = (sem, val)

    def emit(self, eng, fn, inc=None, amount=1):
        if inc is None:
            self.q[eng].append(fn)
            return None
        val = inc.bump(amount)
        h = inc.h

        def L(M, fn=fn, h=h, amount=amount):
            fn(M).then_inc(h, amount)
        self.q[eng].append(L)
        return val

    def finalize(self):
        with self.nc.Block() as block:
            @block.tensor
            def _(E):
                for f in self.q["t"]:
                    f(E)

            @block.vector
            def _(E):
                for f in self.q["v"]:
                    f(E)

            @block.scalar
            def _(E):
                for f in self.q["s"]:
                    f(E)

            @block.gpsimd
            def _(E):
                for f in self.q["g"]:
                    f(E)

            @block.sync
            def _(E):
                for f in self.q["y"]:
                    f(E)


def build_nc(cfg: Cfg):
    nc = bass.Bass("TRN2", target_bir_lowering=False)
    T, D, HF, Er, E = cfg.T, cfg.D, cfg.HF, cfg.Er, cfg.E
    DC, TC, FT, HB, NH, NP = cfg.DC, cfg.TC, cfg.FT, cfg.HB, cfg.NH, cfg.NP
    WH = min(1024, T)
    NHALF = T // WH
    NQ = WH // 512
    OSW = min(512, D)
    NOS = D // OSW

    dx = nc.dram_tensor("x", [D, T], F32, kind="ExternalInput")
    dy = nc.dram_tensor("y", [D, T], F32, kind="ExternalOutput")
    dwq = nc.dram_tensor("wq", [D, D], F32, kind="ExternalInput")
    dwk = nc.dram_tensor("wk", [D, D], F32, kind="ExternalInput")
    dwv = nc.dram_tensor("wv", [D, D], F32, kind="ExternalInput")
    dwo = nc.dram_tensor("wo", [D, D], F32, kind="ExternalInput")
    dbq = nc.dram_tensor("bq", [1, D], F32, kind="ExternalInput")
    dbk = nc.dram_tensor("bk", [1, D], F32, kind="ExternalInput")
    dbv = nc.dram_tensor("bv", [1, D], F32, kind="ExternalInput")
    dbo = nc.dram_tensor("bo", [1, D], F32, kind="ExternalInput")
    drw = nc.dram_tensor("rw", [D, E], F32, kind="ExternalInput")
    dup = nc.dram_tensor("upw", [Er, D, HF], F32, kind="ExternalInput")
    dgate = nc.dram_tensor("gatew", [Er, D, HF], F32, kind="ExternalInput")
    ddown = nc.dram_tensor("downw", [Er, HF, D], F32, kind="ExternalInput")
    dsup = nc.dram_tensor("sup", [D, HF], F32, kind="ExternalInput")
    dsgate = nc.dram_tensor("sgate", [D, HF], F32, kind="ExternalInput")
    dsdown = nc.dram_tensor("sdown", [HF, D], F32, kind="ExternalInput")
    dsel8 = nc.dram_tensor("sel8", [E, Er * 128], BF16, kind="ExternalInput")
    di128 = nc.dram_tensor("i128", [128, 128], BF16, kind="ExternalInput")
    de16 = nc.dram_tensor("e16", [NH, NP * 128], BF16, kind="ExternalInput")
    dccin = nc.dram_tensor("cc_in", [128, NP * 64 + NP], F32)
    dccout = nc.dram_tensor("cc_out", [128, NP * 64 + NP], F32)

    with ExitStack() as top:
        ec = top.enter_context
        x_sb = ec(nc.sbuf_tensor("x_sb", [128, DC * T], F32))
        xln_sb = ec(nc.sbuf_tensor("xln_sb", [128, DC * T], BF16))
        rows = ec(nc.sbuf_tensor("rows", [128, T], F32))
        rstd_row = ec(nc.sbuf_tensor("rstd_row", [1, T], BF16))
        onesf = ec(nc.sbuf_tensor("onesf", [1, 128], F32))
        onesbf = ec(nc.sbuf_tensor("onesbf", [1, 512], BF16))
        onescol = ec(nc.sbuf_tensor("onescol", [128, 1], BF16))
        statw = ec(nc.sbuf_tensor("statw", [128, 1], BF16))
        ones128f = ec(nc.sbuf_tensor("ones128f", [128, 1], F32))
        rw_sb = ec(nc.sbuf_tensor("rw_sb", [128, DC * E], F32))
        srow = ec(nc.sbuf_tensor("srow", [1, E], F32))
        sb8 = ec(nc.sbuf_tensor("sb8", [128, E], F32))
        cols = ec(nc.sbuf_tensor("cols", [128, 4 * TC], F32))
        PS_A = ec(nc.psum_tensor("PS_A", [128, 512], F32))
        PS_B = ec(nc.psum_tensor("PS_B", [128, 512], F32))
        PS_C = ec(nc.psum_tensor("PS_C", [128, 512], F32))
        PS_D = ec(nc.psum_tensor("PS_D", [128, 512], F32))
        PS_W = ec(nc.psum_tensor("PS_W", [128, 2048], F32))
        PS = {"A": PS_A, "B": PS_B, "C": PS_C, "D": PS_D}

        B = Builder(nc)
        st, sv, ss = B.sem["st"], B.sem["sv"], B.sem["ss"]
        sgd, syd, scc = B.sem["sgd"], B.sem["syd"], B.sem["scc"]
        su, sg2, sd2, sw = B.sem["su"], B.sem["sg2"], B.sem["sd2"], B.sem["sw"]
        sx, sconst, se16 = B.sem["sx"], B.sem["sconst"], B.sem["se16"]
        sconsg = B.sem["sconsg"]
        sci, syo = B.sem["sci"], B.sem["syo"]

        def xs(dc, a, b):
            return x_sb[:, dc * T + a: dc * T + b]

        def xlns(dc, a, b):
            return xln_sb[:, dc * T + a: dc * T + b]

        def vq(fn):
            B.wait("v", sv, sv.n)
            return B.emit("v", fn, inc=sv)

        B.emit("v", lambda M: M.memset(onesf[:, :], 1.0))
        B.emit("v", lambda M: M.memset(onesbf[:, :], 1.0))
        B.emit("v", lambda M: M.memset(onescol[:, :], 1.0))
        B.emit("v", lambda M: M.memset(statw[:, :], 1.0 / D))
        v_consts = B.emit("v", lambda M: M.memset(ones128f[:, :], 1.0), inc=sv)

        src = dx.ap().rearrange("(a p) j -> p a j", p=128)
        dst = x_sb.ap().rearrange("p (a j) -> p a j", a=DC)
        y_xload = B.emit("y", lambda M, o=dst, i=src: M.dma_start(out=o, in_=i),
                         inc=sx, amount=16)
        rsrc = drw.ap().rearrange("(a p) j -> p a j", p=128)
        rdst = rw_sb.ap().rearrange("p (a j) -> p a j", a=DC)
        B.emit("g", lambda M, o=rdst, i=rsrc: M.dma_start(out=o, in_=i),
                      inc=sconsg, amount=16)
        CONSG_TOTAL = sconsg.n

        B.wait("t", sconsg, CONSG_TOTAL)
        B.wait("t", sv, v_consts)
        t_cs = None
        for dc in range(DC):
            t_cs = B.emit("t", lambda M, dc=dc: M.matmul(
                PS_B[0:1, 0:E], ones128f[:, 0:1], rw_sb[:, dc * E:(dc + 1) * E],
                start=(dc == 0), stop=(dc == DC - 1)),
                inc=(st if dc == DC - 1 else None))
        B.wait("v", st, t_cs)
        v_srow = B.emit("v", lambda M: M.tensor_copy(srow[:, :], PS_B[0:1, 0:E]), inc=sv)
        B.wait("t", sv, v_srow)
        t_sb8 = B.emit("t", lambda M: M.matmul(
            PS_B[:, 32:32 + E], onesf[0:1, 0:128], srow[0:1, :],
            start=True, stop=True), inc=st)
        B.wait("v", st, t_sb8)
        v_sb8 = B.emit("v", lambda M: M.tensor_copy(sb8[:, :], PS_B[:, 32:32 + E]), inc=sv)
        B.consumed("B", sv, v_sb8)

        def emit_ln(scr_sb, scr_off, sq_sb, sq_off):
            t_dc = 0
            for dc in range(DC):
                B.emit("s", lambda M, dc=dc: M.copy(xlns(dc, 0, T), xs(dc, 0, T)))
                B.wait("s", st, t_dc)
                v = B.emit("s", lambda M, dc=dc: M.activation(
                    sq_sb[:, sq_off: sq_off + T], xs(dc, 0, T), AF.Square), inc=ss)
                B.wait("t", ss, v)
                if dc == 0:
                    B.wait_slot("t", "Wstat")
                for ft in range(FT):
                    B.emit("t", lambda M, dc=dc, ft=ft: M.matmul(
                        PS_W[0:1, ft * 512:(ft + 1) * 512], statw[:, 0:1],
                        xlns(dc, ft * 512, (ft + 1) * 512),
                        start=(dc == 0), stop=(dc == DC - 1)))
                    t_dc = B.emit("t", lambda M, dc=dc, ft=ft: M.matmul(
                        PS_W[32:33, ft * 512:(ft + 1) * 512], statw[:, 0:1],
                        sq_sb[:, sq_off + ft * 512: sq_off + (ft + 1) * 512],
                        start=(dc == 0), stop=(dc == DC - 1)),
                        inc=(st if ft == FT - 1 else None))
            t_stats = t_dc
            B.wait("v", st, t_stats)
            vq(lambda M: M.tensor_copy(rows[0:1, :], PS_W[0:1, 0:T]))
            vq(lambda M: M.tensor_mul(rows[96:97, :], rows[0:1, :],
                                      rows[0:1, :]))
            v_var = vq(lambda M: M.scalar_tensor_tensor(
                rows[64:65, :], PS_W[32:33, 0:T], float(cfg.ln_eps),
                rows[96:97, :], ALU.add, ALU.subtract))
            B.consumed("Wstat", sv, v_var)
            B.wait("s", sv, v_var)
            s_sd = B.emit("s", lambda M: M.activation(
                rows[96:97, :], rows[64:65, :], AF.Sqrt), inc=ss)
            B.wait("v", ss, s_sd)
            def _rstd_lp(M):
                with nc.allow_low_precision(reason="bf16 rstd row"):
                    return M.reciprocal(rstd_row[0:1, :], rows[96:97, :])
            v_rstd = vq(_rstd_lp)
            B.wait("t", sv, v_rstd)
            names = ["A", "B", "C", "D"]
            for i in range(2 * FT):
                ft, which = i // 2, i % 2
                key = names[i % 4]
                ps = PS[key]
                B.wait_slot("t", key)
                if which == 0:
                    tv = B.emit("t", lambda M, ps=ps, ft=ft: M.matmul(
                        ps[:, 0:512], onesf[0:1, :],
                        rows[0:1, ft * 512:(ft + 1) * 512],
                        start=True, stop=True), inc=st)
                else:
                    tv = B.emit("t", lambda M, ps=ps, ft=ft: M.matmul(
                        ps[:, 0:512], onesbf[0:1, 0:128],
                        rstd_row[0:1, ft * 512:(ft + 1) * 512],
                        start=True, stop=True), inc=st)
                B.wait("v", st, tv)
                off = scr_off + T + which * T + ft * 512
                vv = B.emit("v", lambda M, ps=ps, off=off: M.tensor_copy(
                    scr_sb[:, off: off + 512], ps[:, 0:512]), inc=sv)
                B.consumed(key, sv, vv)
            v_ln = None
            for dc in range(DC):
                vq(lambda M, dc=dc: M.tensor_sub(
                    scr_sb[:, scr_off: scr_off + T], xs(dc, 0, T),
                    scr_sb[:, scr_off + T: scr_off + 2 * T]))
                v_ln = vq(lambda M, dc=dc: M.tensor_mul(
                    xlns(dc, 0, T), scr_sb[:, scr_off: scr_off + T],
                    scr_sb[:, scr_off + 2 * T: scr_off + 3 * T]))
            return v_ln

        def emit_router(mrows_sb, v_ln, logits_sb, mtok_sb, oh_sb, i128_sb,
                        c_tot):
            B.wait("t", sv, v_ln)
            B.wait_slot("t", "B")
            for tc in range(TC):
                B.emit("t", lambda M, tc=tc: M.matmul(
                    PS_B[:, tc:tc + 1], rows[0:1, tc * 128:(tc + 1) * 128],
                    onesf[0:1, 0:1], start=True, stop=True))
                t_cols = B.emit("t", lambda M, tc=tc: M.matmul(
                    PS_B[:, TC + tc: TC + tc + 1],
                    rstd_row[0:1, tc * 128:(tc + 1) * 128],
                    onesbf[0:1, 0:1], start=True, stop=True),
                    inc=(st if tc == TC - 1 else None))
            B.wait("v", st, t_cols)
            v_cols = vq(lambda M: M.tensor_copy(
                cols[:, 0:2 * TC], PS_B[:, 0:2 * TC]))
            B.consumed("B", sv, v_cols)
            B.wait_slot("t", "Wstat")
            for tc in range(TC):
                for dc in range(DC):
                    t_raw = B.emit("t", lambda M, tc=tc, dc=dc: M.matmul(
                        PS_W[:, tc * E:(tc + 1) * E],
                        xs(dc, tc * 128, (tc + 1) * 128),
                        rw_sb[:, dc * E:(dc + 1) * E],
                        start=(dc == 0), stop=(dc == DC - 1)),
                        inc=(st if (tc == TC - 1 and dc == DC - 1) else None))
            B.wait("v", st, t_raw)
            B.wait("v", sv, v_cols)
            for tc in range(TC):
                vq(lambda M, tc=tc: M.tensor_scalar(
                    logits_sb[:, tc * E:(tc + 1) * E], sb8[:, :],
                    cols[:, tc:tc + 1], None, ALU.mult))
                vq(lambda M, tc=tc: M.tensor_sub(
                    logits_sb[:, tc * E:(tc + 1) * E], PS_W[:, tc * E:(tc + 1) * E],
                    logits_sb[:, tc * E:(tc + 1) * E]))
                vq(lambda M, tc=tc: M.tensor_scalar(
                    logits_sb[:, tc * E:(tc + 1) * E],
                    logits_sb[:, tc * E:(tc + 1) * E],
                    cols[:, TC + tc: TC + tc + 1], None, ALU.mult))
                vq(lambda M, tc=tc: M.tensor_reduce(
                    cols[:, 2 * TC + tc: 2 * TC + tc + 1],
                    logits_sb[:, tc * E:(tc + 1) * E], mybir.AxisListType.X, ALU.max))
            v_logit = vq(lambda M: M.tensor_scalar_mul(
                cols[:, 2 * TC: 3 * TC], cols[:, 2 * TC: 3 * TC], -1.0))
            B.wait("s", sv, v_logit)
            for tc in range(TC):
                s_exp = B.emit("s", lambda M, tc=tc: M.activation(
                    PS_W[:, tc * E:(tc + 1) * E], logits_sb[:, tc * E:(tc + 1) * E],
                    AF.Exp, bias=cols[:, 2 * TC + tc: 2 * TC + tc + 1], scale=1.0),
                    inc=(ss if tc == TC - 1 else None))
            B.wait("v", ss, s_exp)
            for tc in range(TC):
                vq(lambda M, tc=tc: M.tensor_reduce(
                    cols[:, 3 * TC + tc: 3 * TC + tc + 1],
                    PS_W[:, tc * E:(tc + 1) * E], mybir.AxisListType.X, ALU.add))
            v_se = vq(lambda M: M.reciprocal(
                cols[:, 3 * TC: 4 * TC], cols[:, 3 * TC: 4 * TC]))
            for tc in range(TC):
                vq(lambda M, tc=tc: M.tensor_scalar(
                    oh_sb[:, 0:E], logits_sb[:, tc * E:(tc + 1) * E],
                    cols[:, 2 * TC + tc: 2 * TC + tc + 1], 0.0, ALU.add, ALU.is_ge))
                v_mtok = vq(lambda M, tc=tc: M.tensor_scalar(
                    mtok_sb[:, tc * E:(tc + 1) * E], oh_sb[:, 0:E],
                    cols[:, 3 * TC + tc: 3 * TC + tc + 1], None, ALU.mult))
            B.wait("t", sv, v_mtok)
            B.wait("t", sconst, c_tot)
            for tc in range(TC):
                t_tr = B.emit("t", lambda M, tc=tc: M.matmul(
                    PS_W[0:E, tc * 128:(tc + 1) * 128],
                    mtok_sb[:, tc * E:(tc + 1) * E], i128_sb[:, :],
                    start=True, stop=True),
                    inc=(st if tc == TC - 1 else None))
            B.wait("v", st, t_tr)
            v_mrows = vq(lambda M, m=mrows_sb: M.tensor_copy(
                m[:, :], PS_W[0:E, 0:T]))
            B.consumed("Wstat", sv, v_mrows)
            return v_mrows

        def emit_ffn(v_gate, tag):
            with ExitStack() as ffn:
                fc = ffn.enter_context
                xup_sb = fc(nc.sbuf_tensor("xup_sb" + tag, [128, DC * T], BF16))
                hid_sb = fc(nc.sbuf_tensor("hid_sb" + tag, [128, 4 * T], BF16))
                U_sb = fc(nc.sbuf_tensor("U_sb" + tag, [128, DC * 512], BF16))
                G_sb = fc(nc.sbuf_tensor("G_sb" + tag, [128, DC * 512], BF16))
                D_sb = fc(nc.sbuf_tensor("D_sb" + tag, [128, 4 * D], BF16))
                mtb_sb = fc(nc.sbuf_tensor("mtb_sb" + tag, [128, T], BF16))
                mrows_sb = fc(nc.sbuf_tensor("mrows_sb" + tag, [E, T], BF16))
                silu_sb = fc(nc.sbuf_tensor("silu_sb" + tag, [128, 2 * 512], BF16))
                logits_sb = fc(nc.sbuf_tensor("logits" + tag, [128, TC * E], F32))
                oh_sb = fc(nc.sbuf_tensor("oh" + tag, [128, E], F32))
                mtok_sb = fc(nc.sbuf_tensor("mtok" + tag, [128, TC * E], BF16))
                sel8_sb = fc(nc.sbuf_tensor("sel8" + tag, [E, Er * 128], BF16))
                i128_sb = fc(nc.sbuf_tensor("i128" + tag, [128, 128], BF16))

                B.wait("y", sv, v_gate)
                B.emit("y", lambda M: M.dma_start(
                    out=sel8_sb[:, :], in_=dsel8[:, :]), inc=sconst, amount=16)
                B.emit("y", lambda M: M.dma_start(
                    out=i128_sb[:, :], in_=di128[:, :]), inc=sconst, amount=16)
                c_tot = sconst.n

                v_ln = emit_ln(hid_sb, 0, mtb_sb, 0)
                v_mrows = emit_router(mrows_sb, v_ln, logits_sb, mtok_sb,
                                      oh_sb, i128_sb, c_tot)

                def wsrc(e):
                    if e < Er:
                        return (dup.ap()[e], dgate.ap()[e], ddown.ap()[e])
                    return (dsup.ap(), dsgate.ap(), dsdown.ap())

                def load_ug(e, hb):
                    up, gate, _ = wsrc(e)
                    s_u = up[:, hb * 512:(hb + 1) * 512].rearrange(
                        "(a p) j -> p a j", p=128)
                    d_u = U_sb.ap().rearrange("p (a j) -> p a j", a=DC)
                    vu = B.emit("g", lambda M, o=d_u, i=s_u: M.dma_start(out=o, in_=i),
                                inc=su, amount=16)
                    s_g = gate[:, hb * 512:(hb + 1) * 512].rearrange(
                        "(a p) j -> p a j", p=128)
                    d_g = G_sb.ap().rearrange("p (a j) -> p a j", a=DC)
                    vg = B.emit("g", lambda M, o=d_g, i=s_g: M.dma_start(out=o, in_=i),
                                inc=sg2, amount=16)
                    return (vu, vg)

                def load_d(e, hb):
                    _, _, down = wsrc(e)
                    s_d = down[hb * 512:(hb + 1) * 512, :].rearrange(
                        "(a p) j -> p a j", p=128)
                    d_d = D_sb.ap().rearrange("p (a j) -> p a j", a=4)
                    return B.emit("g", lambda M, o=d_d, i=s_d: M.dma_start(out=o, in_=i),
                                  inc=sd2, amount=16)

                def emit_xup(e, t_prev_A):
                    if e < Er:
                        B.wait("t", sv, v_mrows)
                        B.wait("t", sconst, c_tot)
                        for ft in range(FT):
                            key = ["B", "D"][ft % 2]
                            ps = PS[key]
                            B.wait_slot("t", key)
                            tv = B.emit("t", lambda M, ps=ps, ft=ft, e=e: M.matmul(
                                ps[:, 0:512], sel8_sb[:, e * 128:(e + 1) * 128],
                                mrows_sb[:, ft * 512:(ft + 1) * 512],
                                start=True, stop=True), inc=st)
                            B.wait("v", st, tv)
                            vv = B.emit("v", lambda M, ps=ps, ft=ft: M.tensor_copy(
                                mtb_sb[:, ft * 512:(ft + 1) * 512], ps[:, 0:512]),
                                inc=sv)
                            B.consumed(key, sv, vv)
                        vx = None
                        for dc in range(DC):
                            vx = vq(lambda M, dc=dc: M.tensor_mul(
                                xup_sb[:, dc * T:(dc + 1) * T], xlns(dc, 0, T),
                                mtb_sb[:, :]))
                        return (sv, vx)
                    else:
                        B.wait("s", st, t_prev_A)
                        B.wait("s", sv, v_ln)
                        vv = None
                        for dc in range(DC):
                            vv = B.emit("s", lambda M, dc=dc: M.activation(
                                xup_sb[:, dc * T:(dc + 1) * T], xlns(dc, 0, T),
                                AF.Identity, scale=0.5),
                                inc=(ss if dc == DC - 1 else None))
                        return (ss, vv)

                order = list(range(Er + 1))
                B.wait("g", sv, v_gate)
                ug_ready = load_ug(order[0], 0)
                d_ready = load_d(order[0], 0)
                t_lastA = 0

                for ei, e in enumerate(order):
                    v_xup = emit_xup(e, t_lastA)
                    for hb in range(HB):
                        B.wait("t", su, ug_ready[0])
                        B.wait("t", sg2, ug_ready[1])
                        B.wait("t", v_xup[0], v_xup[1])
                        tiles = [(h, f) for h in range(4) for f in range(FT)]
                        for i, (hc, ft) in enumerate(tiles):
                            kU = ["A", "C"][i % 2]
                            kG = ["B", "D"][i % 2]
                            psU, psG = PS[kU], PS[kG]
                            sl = i % 2
                            B.wait_slot("t", kU)
                            B.wait_slot("t", kG)
                            for dc in range(DC):
                                B.emit("t", lambda M, psU=psU, dc=dc, hc=hc, ft=ft:
                                       M.matmul(
                                    psU[:, 0:512],
                                    U_sb[:, dc * 512 + hc * 128: dc * 512 + (hc + 1) * 128],
                                    xup_sb[:, dc * T + ft * 512: dc * T + (ft + 1) * 512],
                                    start=(dc == 0), stop=(dc == DC - 1)))
                            for dc in range(DC):
                                tA = B.emit("t", lambda M, psG=psG, dc=dc, hc=hc, ft=ft:
                                       M.matmul(
                                    psG[:, 0:512],
                                    G_sb[:, dc * 512 + hc * 128: dc * 512 + (hc + 1) * 128],
                                    xlns(dc, ft * 512, (ft + 1) * 512),
                                    start=(dc == 0), stop=(dc == DC - 1)),
                                    inc=(st if dc == DC - 1 else None))
                            B.wait("s", st, tA)
                            sS = B.emit("s", lambda M, psG=psG, sl=sl: M.activation(
                                silu_sb[:, sl * 512:(sl + 1) * 512], psG[:, 0:512],
                                AF.Silu), inc=ss)
                            B.wait("v", ss, sS)
                            vv = B.emit("v",
                                        lambda M, psU=psU, sl=sl, hc=hc, ft=ft:
                                        M.tensor_mul(
                                hid_sb[:, hc * T + ft * 512: hc * T + (ft + 1) * 512],
                                psU[:, 0:512],
                                silu_sb[:, sl * 512:(sl + 1) * 512]), inc=sv)
                            B.consumed(kU, sv, vv)
                            B.consumed(kG, sv, vv)
                            v_hid = vv
                        t_lastA = tA
                        nxt = (e, hb + 1) if hb + 1 < HB else (
                            (order[ei + 1], 0) if ei + 1 < len(order) else None)
                        if nxt is not None:
                            B.wait("g", st, t_lastA)
                            ug_next = load_ug(*nxt)
                        B.wait("t", sd2, d_ready)
                        B.wait("t", sv, v_hid)
                        tB = 0
                        for dco in range(DC):
                            for half in range(NHALF):
                                B.wait_slot("t", "W%d" % half)
                                for hc in range(4):
                                    for q in range(NQ):
                                        c0 = half * WH + q * 512
                                        tB = B.emit("t",
                                               lambda M, dco=dco, hc=hc, c0=c0:
                                               M.matmul(
                                            PS_W[:, c0:c0 + 512],
                                            D_sb[:, hc * D + dco * 128: hc * D + (dco + 1) * 128],
                                            hid_sb[:, hc * T + c0: hc * T + c0 + 512],
                                            start=(hc == 0), stop=(hc == 3)),
                                            inc=(st if (hc == 3 and q == NQ - 1) else None))
                                B.wait("v", st, tB)
                                vv = B.emit("v",
                                            lambda M, dco=dco, half=half:
                                            M.tensor_add(
                                    xs(dco, half * WH, (half + 1) * WH),
                                    xs(dco, half * WH, (half + 1) * WH),
                                    PS_W[:, half * WH:(half + 1) * WH]), inc=sv)
                                B.consumed("W%d" % half, sv, vv)
                        if nxt is not None:
                            B.wait("g", st, tB)
                            d_ready = load_d(*nxt)
                            ug_ready = ug_next
                B.consumed("Wstat", sv, sv.n)
                return sv.n

        def emit_attn(v_gate):
            with ExitStack() as att:
                ac = att.enter_context
                W0 = ac(nc.sbuf_tensor("W0", [128, DC * D], BF16))
                W1 = ac(nc.sbuf_tensor("W1", [128, DC * D], BF16))
                K_tile = ac(nc.sbuf_tensor("K_tile", [128, 2 * D], BF16))
                V_tile = ac(nc.sbuf_tensor("V_tile", [128, 2 * D], BF16))
                ascr = ac(nc.sbuf_tensor("ascr", [128, DC * T], BF16))
                recip_sb = ac(nc.sbuf_tensor("recip_sb", [NH, T], BF16))
                cc_sb = ac(nc.sbuf_tensor("cc_sb", [128, NP * 64 + NP], F32))
                kvbf_sb = ac(nc.sbuf_tensor("kvbf_sb", [128, NP * 64], BF16))
                zbd_sb = ac(nc.sbuf_tensor("zbd_sb", [128, DC * NH], BF16))
                e16_sb = ac(nc.sbuf_tensor("e16_sb", [NH, NP * 128], BF16))
                bias_at = ac(nc.sbuf_tensor("bias_at", [1, 2 * D], BF16))
                sc1 = ac(nc.sbuf_tensor("sc1a", [128, 512], BF16))
                sc2 = ac(nc.sbuf_tensor("sc2a", [128, 512], BF16))

                def load_w(dw, W_t, sem):
                    s_w = dw.ap().rearrange("(a p) j -> p a j", p=128)
                    d_w = W_t.ap().rearrange("p (a j) -> p a j", a=DC)
                    return B.emit("g", lambda M, o=d_w, i=s_w: M.dma_start(
                        out=o, in_=i), inc=sem, amount=16)

                B.wait("y", sv, v_gate)
                y_e16 = B.emit("y", lambda M: M.dma_start(
                    out=e16_sb[:, :], in_=de16[:, :]), inc=se16, amount=16)
                B.wait("g", sv, v_gate)
                w_k = load_w(dwk, W0, sw)
                w_v = load_w(dwv, W1, B.sem["swb"])
                B.emit("g", lambda M: M.dma_start(
                    out=bias_at[0:1, 0:D], in_=dbk[:, :]), inc=sconsg, amount=16)
                B.emit("g", lambda M: M.dma_start(
                    out=bias_at[0:1, D:2 * D], in_=dbv[:, :]), inc=sconsg, amount=16)
                bias_kv = sconsg.n

                v_ln2 = emit_ln(ascr, 0, ascr, 3 * T)
                v_ccz = vq(lambda M: M.memset(cc_sb[:, :], 0.0))

                B.wait("t", sw, w_k)
                B.wait("t", B.sem["swb"], w_v)
                B.wait("t", sconsg, bias_kv)
                B.wait("t", sv, v_ln2)
                B.wait_slot("t", "A")
                B.wait_slot("t", "B")
                B.wait_slot("t", "C")
                B.wait_slot("t", "Wstat")
                t_kvtc = [0] * TC
                for tc in range(TC):
                    half = tc % 2
                    B.wait_slot("t", "WpK")
                    tvk = 0
                    for os in range(NOS):
                        c0 = os * OSW
                        for dc in range(DC):
                            B.emit("t", lambda M, dc=dc, tc=tc, c0=c0: M.matmul(
                                PS_W[:, c0:c0 + OSW],
                                xlns(dc, tc * 128, (tc + 1) * 128),
                                W0[:, dc * D + c0: dc * D + c0 + OSW],
                                start=(dc == 0), stop=False))
                        tvk = B.emit("t", lambda M, c0=c0: M.matmul(
                            PS_W[:, c0:c0 + OSW], onesbf[0:1, 0:128],
                            bias_at[0:1, c0:c0 + OSW],
                            start=False, stop=True),
                            inc=(st if os == NOS - 1 else None))
                    vphi = 0
                    for os in range(NOS):
                        c0 = os * OSW
                        dk = K_tile[:, half * D + c0: half * D + c0 + OSW]
                        B.wait("v", st, tvk)
                        if tc >= 2:
                            B.wait("v", st, t_kvtc[tc - 2])
                        v1 = vq(lambda M, c0=c0: M.tensor_scalar_min(
                            sc1[:, 0:OSW], PS_W[:, c0:c0 + OSW], 0.0))
                        B.wait("s", sv, v1)
                        s1 = B.emit("s", lambda M: M.activation(
                            sc2[:, 0:OSW], sc1[:, 0:OSW], AF.Exp), inc=ss)
                        B.wait("v", ss, s1)
                        vphi = vq(lambda M, c0=c0, dk=dk: M.scalar_tensor_tensor(
                            dk, PS_W[:, c0:c0 + OSW], 0.0,
                            sc2[:, 0:OSW], ALU.max, ALU.add))
                    B.consumed("WpK", sv, vphi)
                    B.wait_slot("t", "WpV")
                    tvv = 0
                    for os in range(NOS):
                        c0 = D + os * OSW
                        for dc in range(DC):
                            B.emit("t", lambda M, dc=dc, tc=tc, c0=c0: M.matmul(
                                PS_W[:, c0:c0 + OSW],
                                xlns(dc, tc * 128, (tc + 1) * 128),
                                W1[:, dc * D + c0 - D: dc * D + c0 - D + OSW],
                                start=(dc == 0), stop=False))
                        tvv = B.emit("t", lambda M, c0=c0: M.matmul(
                            PS_W[:, c0:c0 + OSW], onesbf[0:1, 0:128],
                            bias_at[0:1, c0:c0 + OSW],
                            start=False, stop=True),
                            inc=(st if os == NOS - 1 else None))
                    vcv = 0
                    for os in range(NOS):
                        c0 = D + os * OSW
                        dv = V_tile[:, half * D + os * OSW: half * D + (os + 1) * OSW]
                        B.wait("v", st, tvv)
                        if tc >= 2:
                            B.wait("v", st, t_kvtc[tc - 2])
                        vcv = vq(lambda M, c0=c0, dv=dv: M.tensor_copy(
                            dv, PS_W[:, c0:c0 + OSW]))
                    B.consumed("WpV", sv, vcv)
                    B.wait("t", sv, vcv)
                    if tc > 0:
                        B.wait("t", sv, v_kvadd)
                    for p in range(NP):
                        b0 = half * D + p * 128
                        B.emit("t", lambda M, p=p, b0=b0: M.matmul(
                            PS_A[0:64, p * 64:(p + 1) * 64],
                            K_tile[:, b0: b0 + 64], V_tile[:, b0: b0 + 64],
                            start=True, stop=True))
                        B.emit("t", lambda M, p=p, b0=b0: M.matmul(
                            PS_B[64:128, p * 64:(p + 1) * 64],
                            K_tile[:, b0 + 64: b0 + 128],
                            V_tile[:, b0 + 64: b0 + 128],
                            start=True, stop=True))
                        t_kvtc[tc] = B.emit("t", lambda M, p=p, b0=b0: M.matmul(
                            PS_C[:, p:p + 1],
                            K_tile[:, b0: b0 + 128], onescol[:, 0:1],
                            start=True, stop=True),
                            inc=(st if p == NP - 1 else None))
                    B.wait("v", st, t_kvtc[tc])
                    if tc == 0:
                        B.wait("v", sv, v_ccz)
                    vq(lambda M: M.tensor_add(
                        cc_sb[0:64, 0:NP * 64], cc_sb[0:64, 0:NP * 64],
                        PS_A[0:64, 0:NP * 64]))
                    vq(lambda M: M.tensor_add(
                        cc_sb[64:128, 0:NP * 64], cc_sb[64:128, 0:NP * 64],
                        PS_B[64:128, 0:NP * 64]))
                    v_kvadd = vq(lambda M: M.tensor_add(
                        cc_sb[:, NP * 64: NP * 64 + NP],
                        cc_sb[:, NP * 64: NP * 64 + NP], PS_C[:, 0:NP]))
                t_kv = t_kvtc[TC - 1]
                v_cc = v_kvadd

                B.wait("g", st, t_kv)
                w_q = load_w(dwq, W0, sw)
                w_o = load_w(dwo, W1, B.sem["swb"])
                B.emit("g", lambda M: M.dma_start(
                    out=bias_at[0:1, 0:D], in_=dbq[:, :]), inc=sconsg, amount=16)
                B.emit("g", lambda M: M.dma_start(
                    out=bias_at[0:1, D:2 * D], in_=dbo[:, :]), inc=sconsg, amount=16)
                bias_qo = sconsg.n

                B.consumed("A", sv, v_cc)
                B.consumed("B", sv, v_cc)
                B.consumed("C", sv, v_cc)
                B.wait("y", sv, v_cc)
                y_ccin = B.emit("y", lambda M: M.dma_start(
                    out=dccin[:, :], in_=cc_sb[:, :]), inc=sci, amount=16)
                B.wait("g", sci, y_ccin)
                g_cc = B.emit("g", lambda M: M.collective_compute(
                    "AllReduce", ALU.add,
                    replica_groups=[list(range(0, cfg.group)),
                                    list(range(cfg.group, 2 * cfg.group))],
                    ins=[dccin.ap().opt()], outs=[dccout.ap().opt()]), inc=scc)
                B.wait("g", scc, g_cc)
                g_ccr = B.emit("g", lambda M: M.dma_start(
                    out=cc_sb[:, :], in_=dccout[:, :]), inc=sgd, amount=16)

                phiQ = ascr
                B.wait("t", sw, w_q)
                B.wait("t", sconsg, bias_qo)
                t_q = 0
                v_q = 0
                for i, (oc, ft) in enumerate(
                        [(o, f) for o in range(DC) for f in range(FT)]):
                    key = ["C", "D"][i % 2]
                    ps = PS[key]
                    B.wait_slot("t", key)
                    for dc in range(DC):
                        B.emit("t", lambda M, ps=ps, dc=dc, oc=oc, ft=ft: M.matmul(
                            ps[:, 0:512],
                            W0[:, dc * D + oc * 128: dc * D + (oc + 1) * 128],
                            xlns(dc, ft * 512, (ft + 1) * 512),
                            start=(dc == 0), stop=False))
                    t_q = B.emit("t", lambda M, ps=ps, oc=oc: M.matmul(
                        ps[:, 0:512], bias_at[0:1, oc * 128:(oc + 1) * 128],
                        onesbf[0:1, :], start=False, stop=True), inc=st)
                    B.wait("v", st, t_q)
                    v1 = vq(lambda M, ps=ps: M.tensor_scalar_min(
                        sc1[:, :], ps[:, 0:512], 0.0))
                    B.wait("s", sv, v1)
                    s1 = B.emit("s", lambda M: M.activation(
                        sc2[:, :], sc1[:, :], AF.Exp), inc=ss)
                    B.wait("v", ss, s1)
                    v_q = vq(lambda M, ps=ps, oc=oc, ft=ft: M.scalar_tensor_tensor(
                        phiQ[:, oc * T + ft * 512: oc * T + (ft + 1) * 512],
                        ps[:, 0:512], 0.0, sc2[:, :], ALU.max, ALU.add))
                    B.consumed(key, sv, v_q)

                B.wait("v", sgd, g_ccr)
                vq(lambda M: M.tensor_copy(
                    kvbf_sb[:, :], cc_sb[:, 0:NP * 64]))
                vq(lambda M: M.memset(zbd_sb[:, :], 0.0))
                v_kvz = 0
                for dc in range(DC):
                    vq(lambda M, dc=dc: M.tensor_copy(
                        zbd_sb[0:64, dc * NH + 2 * dc: dc * NH + 2 * dc + 1],
                        cc_sb[0:64, NP * 64 + dc: NP * 64 + dc + 1]))
                    v_kvz = vq(lambda M, dc=dc: M.tensor_copy(
                        zbd_sb[64:128, dc * NH + 2 * dc + 1: dc * NH + 2 * dc + 2],
                        cc_sb[64:128, NP * 64 + dc: NP * 64 + dc + 1]))

                B.wait("t", sv, max(v_kvz, v_q))
                B.wait_slot("t", "Wstat")
                t_nrm = 0
                for qq in range(FT):
                    for dc in range(DC):
                        t_nrm = B.emit("t", lambda M, qq=qq, dc=dc: M.matmul(
                            PS_W[0:NH, qq * 512:(qq + 1) * 512],
                            zbd_sb[:, dc * NH:(dc + 1) * NH],
                            phiQ[:, dc * T + qq * 512: dc * T + (qq + 1) * 512],
                            start=(dc == 0), stop=(dc == DC - 1)),
                            inc=(st if (qq == FT - 1 and dc == DC - 1) else None))
                B.wait("v", st, t_nrm)
                vq(lambda M: M.tensor_scalar(
                    PS_W[32:32 + NH, 0:T], PS_W[0:NH, 0:T],
                    float(cfg.attn_eps), None, ALU.add))

                def _recip_lp(M):
                    with nc.allow_low_precision(reason="bf16 attn norm recip"):
                        return M.reciprocal(recip_sb[:, :], PS_W[32:32 + NH, 0:T])
                v_rec = vq(_recip_lp)
                B.consumed("Wstat", sv, v_rec)

                B.wait("t", sv, v_rec)
                B.wait("t", se16, y_e16)
                okeys = [("A", "B"), ("C", "D")]
                v_fm = 0
                for i, (p, ft) in enumerate(
                        [(p, f) for p in range(NP) for f in range(FT)]):
                    kO, kR = okeys[i % 2]
                    psO, psR = PS[kO], PS[kR]
                    B.wait_slot("t", kO)
                    B.wait_slot("t", kR)
                    B.emit("t", lambda M, psO=psO, p=p, ft=ft: M.matmul(
                        psO[0:64, 0:512], kvbf_sb[0:64, p * 64:(p + 1) * 64],
                        phiQ[0:64, p * T + ft * 512: p * T + (ft + 1) * 512],
                        start=True, stop=True))
                    B.emit("t", lambda M, psO=psO, p=p, ft=ft: M.matmul(
                        psO[64:128, 0:512], kvbf_sb[64:128, p * 64:(p + 1) * 64],
                        phiQ[64:128, p * T + ft * 512: p * T + (ft + 1) * 512],
                        start=True, stop=True))
                    tv = B.emit("t", lambda M, psR=psR, p=p, ft=ft: M.matmul(
                        psR[:, 0:512], e16_sb[:, p * 128:(p + 1) * 128],
                        recip_sb[:, ft * 512:(ft + 1) * 512],
                        start=True, stop=True), inc=st)
                    B.wait("v", st, tv)
                    vq(lambda M, psR=psR: M.tensor_copy(
                        sc1[:, :], psR[:, 0:512]))
                    v_fm = vq(lambda M, psO=psO, p=p, ft=ft: M.tensor_mul(
                        xlns(p, ft * 512, (ft + 1) * 512), psO[:, 0:512],
                        sc1[:, :]))
                    B.consumed(kO, sv, v_fm)
                    B.consumed(kR, sv, v_fm)

                B.wait("t", B.sem["swb"], w_o)
                B.wait("t", sv, v_fm)
                v_out = 0
                for i, (oc, ft) in enumerate(
                        [(o, f) for o in range(DC) for f in range(FT)]):
                    key = ["C", "D"][i % 2]
                    ps = PS[key]
                    B.wait_slot("t", key)
                    for dc in range(DC):
                        B.emit("t", lambda M, ps=ps, dc=dc, oc=oc, ft=ft: M.matmul(
                            ps[:, 0:512],
                            W1[:, dc * D + oc * 128: dc * D + (oc + 1) * 128],
                            xlns(dc, ft * 512, (ft + 1) * 512),
                            start=(dc == 0), stop=False))
                    tv = B.emit("t", lambda M, ps=ps, oc=oc: M.matmul(
                        ps[:, 0:512], bias_at[0:1, D + oc * 128: D + (oc + 1) * 128],
                        onesbf[0:1, :], start=False, stop=True), inc=st)
                    B.wait("v", st, tv)
                    v_out = B.emit("v", lambda M, ps=ps, oc=oc, ft=ft: M.tensor_add(
                        xs(oc, ft * 512, (ft + 1) * 512),
                        xs(oc, ft * 512, (ft + 1) * 512), ps[:, 0:512]), inc=sv)
                    B.consumed(key, sv, v_out)
                return v_out

        B.wait("s", sx, y_xload)
        B.wait("t", sx, y_xload)
        v1 = emit_ffn(v_sb8, "_f1")
        B.wait("s", sv, v1)
        v2 = emit_attn(v1)
        B.wait("s", sv, v2)
        v3 = emit_ffn(v2, "_f2")

        B.wait("y", sv, v3)
        osrc = x_sb.ap().rearrange("p (a j) -> p a j", a=DC)
        odst = dy.ap().rearrange("(a p) j -> p a j", p=128)
        y_out = B.emit("y", lambda M, o=odst, i=osrc: M.dma_start(out=o, in_=i),
                       inc=syo, amount=16)
        B.wait("y", syo, y_out)

        B.finalize()
    return nc


def _host_constants(cfg):
    E, Er, NH, NP = cfg.E, cfg.Er, cfg.NH, cfg.NP
    sel8 = np.zeros((E, Er * 128), dtype=BF)
    for e in range(Er):
        sel8[e, e * 128:(e + 1) * 128] = 1.0
    i128 = np.eye(128, dtype=np.float32).astype(BF)
    e16 = np.zeros((NH, NP * 128), dtype=BF)
    for p in range(NP):
        e16[2 * p, p * 128: p * 128 + 64] = 1.0
        e16[2 * p + 1, p * 128 + 64: (p + 1) * 128] = 1.0
    return sel8, i128, e16


def stage_inputs(cfg, x, Wq, bq, Wk, bk, Wv, bv, Wo, bo, rw, upw, gatew, downw,
                 sup, sgate, sdown, n_cores=8):
    f32 = np.float32
    x = np.asarray(x, f32)
    Bb = x.shape[0]
    per_batch = n_cores // Bb
    sel8, i128, e16 = _host_constants(cfg)
    common = {
        "wq": np.ascontiguousarray(np.asarray(Wq, f32).T),
        "wk": np.ascontiguousarray(np.asarray(Wk, f32).T),
        "wv": np.ascontiguousarray(np.asarray(Wv, f32).T),
        "wo": np.ascontiguousarray(np.asarray(Wo, f32).T),
        "bq": np.asarray(bq, f32).reshape(1, -1),
        "bk": np.asarray(bk, f32).reshape(1, -1),
        "bv": np.asarray(bv, f32).reshape(1, -1),
        "bo": np.asarray(bo, f32).reshape(1, -1),
        "rw": np.ascontiguousarray(np.asarray(rw, f32)),
        "upw": np.ascontiguousarray(np.asarray(upw, f32)),
        "gatew": np.ascontiguousarray(np.asarray(gatew, f32)),
        "downw": np.ascontiguousarray(np.asarray(downw, f32)),
        "sup": np.ascontiguousarray(np.asarray(sup, f32)),
        "sgate": np.ascontiguousarray(np.asarray(sgate, f32)),
        "sdown": np.ascontiguousarray(np.asarray(sdown, f32)),
        "sel8": sel8, "i128": i128, "e16": e16,
    }
    in_maps = []
    for c in range(n_cores):
        b = c // per_batch
        t0 = (c % per_batch) * cfg.T
        m = dict(common)
        m["x"] = np.ascontiguousarray(x[b, t0:t0 + cfg.T, :].T)
        in_maps.append(m)
    return in_maps


def gather_out(cfg, results, Bb, S, D, n_cores=8):
    per_batch = n_cores // Bb
    out = np.empty((Bb, S, D), dtype=np.float32)
    for c in range(n_cores):
        b = c // per_batch
        t0 = (c % per_batch) * cfg.T
        out[b, t0:t0 + cfg.T, :] = results[c]["y"].T
    return out


def kernel(L0_attn_x, L0_attn_Wq, L0_attn_bq, L0_attn_Wk, L0_attn_bk,
           L0_attn_Wv, L0_attn_bv, L0_attn_Wo, L0_attn_bo,
           L0_ffn_router_w, L0_ffn_up_w, L0_ffn_gate_w, L0_ffn_down_w,
           L0_ffn_shared_up_w, L0_ffn_shared_gate_w, L0_ffn_shared_down_w):
    x = np.asarray(L0_attn_x, dtype=np.float32)
    Bb, S, D = x.shape
    upw = np.asarray(L0_ffn_up_w)
    cfg = Cfg(T=(Bb * S) // 8, D=D, HF=upw.shape[2], Er=upw.shape[0],
              E=np.asarray(L0_ffn_router_w).shape[1], group=8 // Bb)
    in_maps = stage_inputs(cfg, x, L0_attn_Wq, L0_attn_bq, L0_attn_Wk,
                           L0_attn_bk, L0_attn_Wv, L0_attn_bv, L0_attn_Wo,
                           L0_attn_bo, L0_ffn_router_w, L0_ffn_up_w,
                           L0_ffn_gate_w, L0_ffn_down_w, L0_ffn_shared_up_w,
                           L0_ffn_shared_gate_w, L0_ffn_shared_down_w)
    nc = build_nc(cfg)
    res = run_bass_kernel_spmd(nc, in_maps, core_ids=list(range(8)))
    return gather_out(cfg, res.results, Bb, S, D)


# version 02
# speedup vs baseline: 1.0053x; 1.0053x over previous
from contextlib import ExitStack

import numpy as np
import ml_dtypes

import concourse.bass as bass
import concourse.mybir as mybir
from concourse.bass_utils import run_bass_kernel_spmd

F32 = mybir.dt.float32
BF16 = mybir.dt.bfloat16
AF = mybir.ActivationFunctionType
ALU = mybir.AluOpType
BF = ml_dtypes.bfloat16


class Cfg:
    def __init__(self, T=2048, D=1024, HF=4096, Er=7, E=8, group=4,
                 ln_eps=1e-5, attn_eps=1e-6):
        assert T % 512 == 0 and D % 128 == 0 and HF % 512 == 0
        self.T, self.D, self.HF, self.Er, self.E = T, D, HF, Er, E
        self.DC = D // 128
        self.TC = T // 128
        self.FT = T // 512
        self.HB = HF // 512
        self.NH = D // 64
        self.NP = D // 128
        self.group = group
        self.ln_eps, self.attn_eps = ln_eps, attn_eps


class Sem:
    def __init__(self, nc, name):
        self.h = nc.alloc_semaphore(name)
        self.n = 0

    def bump(self, amount=1):
        self.n += amount
        return self.n


class Builder:
    ENGS = ("t", "v", "s", "g", "y")

    def __init__(self, nc):
        self.nc = nc
        self.q = {e: [] for e in self.ENGS}
        self.sem = {k: Sem(nc, k) for k in
                    ("st", "sv", "ss", "scc", "sgd", "syd", "su", "sg2", "sd2",
                     "sw", "swb", "sx", "sconst", "sconsg", "se16", "sci", "syo")}
        self.cons = {}

    def wait(self, eng, sem, val):
        if val is None or val <= 0:
            return
        h = sem.h
        self.q[eng].append(lambda M: M.wait_ge(h, val))

    def wait_slot(self, eng, key):
        c = self.cons.get(key)
        if c is not None:
            self.wait(eng, c[0], c[1])

    def consumed(self, key, sem, val):
        self.cons[key] = (sem, val)

    def emit(self, eng, fn, inc=None, amount=1):
        if inc is None:
            self.q[eng].append(fn)
            return None
        val = inc.bump(amount)
        h = inc.h

        def L(M, fn=fn, h=h, amount=amount):
            fn(M).then_inc(h, amount)
        self.q[eng].append(L)
        return val

    def finalize(self):
        with self.nc.Block() as block:
            @block.tensor
            def _(E):
                for f in self.q["t"]:
                    f(E)

            @block.vector
            def _(E):
                for f in self.q["v"]:
                    f(E)

            @block.scalar
            def _(E):
                for f in self.q["s"]:
                    f(E)

            @block.gpsimd
            def _(E):
                for f in self.q["g"]:
                    f(E)

            @block.sync
            def _(E):
                for f in self.q["y"]:
                    f(E)


def build_nc(cfg: Cfg):
    nc = bass.Bass("TRN2", target_bir_lowering=False)
    T, D, HF, Er, E = cfg.T, cfg.D, cfg.HF, cfg.Er, cfg.E
    DC, TC, FT, HB, NH, NP = cfg.DC, cfg.TC, cfg.FT, cfg.HB, cfg.NH, cfg.NP
    WH = min(1024, T)
    NHALF = T // WH
    NQ = WH // 512
    OSW = min(512, D)
    NOS = D // OSW

    dx = nc.dram_tensor("x", [D, T], F32, kind="ExternalInput")
    dy = nc.dram_tensor("y", [D, T], F32, kind="ExternalOutput")
    dwq = nc.dram_tensor("wq", [D, D], F32, kind="ExternalInput")
    dwk = nc.dram_tensor("wk", [D, D], F32, kind="ExternalInput")
    dwv = nc.dram_tensor("wv", [D, D], F32, kind="ExternalInput")
    dwo = nc.dram_tensor("wo", [D, D], F32, kind="ExternalInput")
    dbq = nc.dram_tensor("bq", [1, D], F32, kind="ExternalInput")
    dbk = nc.dram_tensor("bk", [1, D], F32, kind="ExternalInput")
    dbv = nc.dram_tensor("bv", [1, D], F32, kind="ExternalInput")
    dbo = nc.dram_tensor("bo", [1, D], F32, kind="ExternalInput")
    drw = nc.dram_tensor("rw", [D, E], F32, kind="ExternalInput")
    dup = nc.dram_tensor("upw", [Er, D, HF], F32, kind="ExternalInput")
    dgate = nc.dram_tensor("gatew", [Er, D, HF], F32, kind="ExternalInput")
    ddown = nc.dram_tensor("downw", [Er, HF, D], F32, kind="ExternalInput")
    dsup = nc.dram_tensor("sup", [D, HF], F32, kind="ExternalInput")
    dsgate = nc.dram_tensor("sgate", [D, HF], F32, kind="ExternalInput")
    dsdown = nc.dram_tensor("sdown", [HF, D], F32, kind="ExternalInput")
    dsel8 = nc.dram_tensor("sel8", [E, Er * 128], BF16, kind="ExternalInput")
    di128 = nc.dram_tensor("i128", [128, 128], BF16, kind="ExternalInput")
    de16 = nc.dram_tensor("e16", [NH, NP * 128], BF16, kind="ExternalInput")
    dccin = nc.dram_tensor("cc_in", [128, NP * 64 + NP], F32)
    dccout = nc.dram_tensor("cc_out", [128, NP * 64 + NP], F32)

    with ExitStack() as top:
        ec = top.enter_context
        x_sb = ec(nc.sbuf_tensor("x_sb", [128, DC * T], F32))
        xln_sb = ec(nc.sbuf_tensor("xln_sb", [128, DC * T], BF16))
        rows = ec(nc.sbuf_tensor("rows", [128, T], F32))
        rstd_row = ec(nc.sbuf_tensor("rstd_row", [1, T], BF16))
        onesf = ec(nc.sbuf_tensor("onesf", [1, 128], F32))
        onesbf = ec(nc.sbuf_tensor("onesbf", [1, 512], BF16))
        onescol = ec(nc.sbuf_tensor("onescol", [128, 1], BF16))
        statw = ec(nc.sbuf_tensor("statw", [128, 1], BF16))
        ones128f = ec(nc.sbuf_tensor("ones128f", [128, 1], F32))
        rw_sb = ec(nc.sbuf_tensor("rw_sb", [128, DC * E], F32))
        srow = ec(nc.sbuf_tensor("srow", [1, E], F32))
        sb8 = ec(nc.sbuf_tensor("sb8", [128, E], F32))
        cols = ec(nc.sbuf_tensor("cols", [128, 4 * TC], F32))
        PS_A = ec(nc.psum_tensor("PS_A", [128, 512], F32))
        PS_B = ec(nc.psum_tensor("PS_B", [128, 512], F32))
        PS_C = ec(nc.psum_tensor("PS_C", [128, 512], F32))
        PS_D = ec(nc.psum_tensor("PS_D", [128, 512], F32))
        PS_W = ec(nc.psum_tensor("PS_W", [128, 2048], F32))
        PS = {"A": PS_A, "B": PS_B, "C": PS_C, "D": PS_D}

        B = Builder(nc)
        st, sv, ss = B.sem["st"], B.sem["sv"], B.sem["ss"]
        sgd, syd, scc = B.sem["sgd"], B.sem["syd"], B.sem["scc"]
        su, sg2, sd2, sw = B.sem["su"], B.sem["sg2"], B.sem["sd2"], B.sem["sw"]
        sx, sconst, se16 = B.sem["sx"], B.sem["sconst"], B.sem["se16"]
        sconsg = B.sem["sconsg"]
        sci, syo = B.sem["sci"], B.sem["syo"]

        def xs(dc, a, b):
            return x_sb[:, dc * T + a: dc * T + b]

        def xlns(dc, a, b):
            return xln_sb[:, dc * T + a: dc * T + b]

        def vq(fn):
            B.wait("v", sv, sv.n)
            return B.emit("v", fn, inc=sv)

        B.emit("v", lambda M: M.memset(onesf[:, :], 1.0))
        B.emit("v", lambda M: M.memset(onesbf[:, :], 1.0))
        B.emit("v", lambda M: M.memset(onescol[:, :], 1.0))
        B.emit("v", lambda M: M.memset(statw[:, :], 1.0 / D))
        v_consts = B.emit("v", lambda M: M.memset(ones128f[:, :], 1.0), inc=sv)

        src = dx.ap().rearrange("(a p) j -> p a j", p=128)
        dst = x_sb.ap().rearrange("p (a j) -> p a j", a=DC)
        y_xload = B.emit("y", lambda M, o=dst, i=src: M.dma_start(out=o, in_=i),
                         inc=sx, amount=16)
        rsrc = drw.ap().rearrange("(a p) j -> p a j", p=128)
        rdst = rw_sb.ap().rearrange("p (a j) -> p a j", a=DC)
        B.emit("g", lambda M, o=rdst, i=rsrc: M.dma_start(out=o, in_=i),
                      inc=sconsg, amount=16)
        CONSG_TOTAL = sconsg.n

        B.wait("t", sconsg, CONSG_TOTAL)
        B.wait("t", sv, v_consts)
        t_cs = None
        for dc in range(DC):
            t_cs = B.emit("t", lambda M, dc=dc: M.matmul(
                PS_B[0:1, 0:E], ones128f[:, 0:1], rw_sb[:, dc * E:(dc + 1) * E],
                start=(dc == 0), stop=(dc == DC - 1)),
                inc=(st if dc == DC - 1 else None))
        B.wait("v", st, t_cs)
        v_srow = B.emit("v", lambda M: M.tensor_copy(srow[:, :], PS_B[0:1, 0:E]), inc=sv)
        B.wait("t", sv, v_srow)
        t_sb8 = B.emit("t", lambda M: M.matmul(
            PS_B[:, 32:32 + E], onesf[0:1, 0:128], srow[0:1, :],
            start=True, stop=True), inc=st)
        B.wait("v", st, t_sb8)
        v_sb8 = B.emit("v", lambda M: M.tensor_copy(sb8[:, :], PS_B[:, 32:32 + E]), inc=sv)
        B.consumed("B", sv, v_sb8)

        def emit_ln(scr_sb, scr_off, sq_sb, sq_off):
            t_dc = 0
            for dc in range(DC):
                B.emit("s", lambda M, dc=dc: M.copy(xlns(dc, 0, T), xs(dc, 0, T)))
                B.wait("s", st, t_dc)
                v = B.emit("s", lambda M, dc=dc: M.activation(
                    sq_sb[:, sq_off: sq_off + T], xs(dc, 0, T), AF.Square), inc=ss)
                B.wait("t", ss, v)
                if dc == 0:
                    B.wait_slot("t", "Wstat")
                for ft in range(FT):
                    B.emit("t", lambda M, dc=dc, ft=ft: M.matmul(
                        PS_W[0:1, ft * 512:(ft + 1) * 512], statw[:, 0:1],
                        xlns(dc, ft * 512, (ft + 1) * 512),
                        start=(dc == 0), stop=(dc == DC - 1)))
                    t_dc = B.emit("t", lambda M, dc=dc, ft=ft: M.matmul(
                        PS_W[32:33, ft * 512:(ft + 1) * 512], statw[:, 0:1],
                        sq_sb[:, sq_off + ft * 512: sq_off + (ft + 1) * 512],
                        start=(dc == 0), stop=(dc == DC - 1)),
                        inc=(st if ft == FT - 1 else None))
            t_stats = t_dc
            B.wait("v", st, t_stats)
            vq(lambda M: M.tensor_copy(rows[0:1, :], PS_W[0:1, 0:T]))
            vq(lambda M: M.tensor_mul(rows[96:97, :], rows[0:1, :],
                                      rows[0:1, :]))
            v_var = vq(lambda M: M.scalar_tensor_tensor(
                rows[64:65, :], PS_W[32:33, 0:T], float(cfg.ln_eps),
                rows[96:97, :], ALU.add, ALU.subtract))
            B.consumed("Wstat", sv, v_var)
            B.wait("s", sv, v_var)
            s_sd = B.emit("s", lambda M: M.activation(
                rows[96:97, :], rows[64:65, :], AF.Sqrt), inc=ss)
            B.wait("v", ss, s_sd)
            def _rstd_lp(M):
                with nc.allow_low_precision(reason="bf16 rstd row"):
                    return M.reciprocal(rstd_row[0:1, :], rows[96:97, :])
            v_rstd = vq(_rstd_lp)
            B.wait("t", sv, v_rstd)
            names = ["A", "B", "C", "D"]
            for i in range(2 * FT):
                ft, which = i // 2, i % 2
                key = names[i % 4]
                ps = PS[key]
                B.wait_slot("t", key)
                if which == 0:
                    tv = B.emit("t", lambda M, ps=ps, ft=ft: M.matmul(
                        ps[:, 0:512], onesf[0:1, :],
                        rows[0:1, ft * 512:(ft + 1) * 512],
                        start=True, stop=True), inc=st)
                else:
                    tv = B.emit("t", lambda M, ps=ps, ft=ft: M.matmul(
                        ps[:, 0:512], onesbf[0:1, 0:128],
                        rstd_row[0:1, ft * 512:(ft + 1) * 512],
                        start=True, stop=True), inc=st)
                B.wait("v", st, tv)
                off = scr_off + T + which * T + ft * 512
                vv = B.emit("v", lambda M, ps=ps, off=off: M.tensor_copy(
                    scr_sb[:, off: off + 512], ps[:, 0:512]), inc=sv)
                B.consumed(key, sv, vv)
            v_ln = None
            for dc in range(DC):
                vq(lambda M, dc=dc: M.tensor_sub(
                    scr_sb[:, scr_off: scr_off + T], xs(dc, 0, T),
                    scr_sb[:, scr_off + T: scr_off + 2 * T]))
                v_ln = vq(lambda M, dc=dc: M.tensor_mul(
                    xlns(dc, 0, T), scr_sb[:, scr_off: scr_off + T],
                    scr_sb[:, scr_off + 2 * T: scr_off + 3 * T]))
            return v_ln, v_rstd

        def emit_router(mrows_sb, v_rstd, logits_sb, mtok_sb, oh_sb, i128_sb,
                        c_tot):
            B.wait("t", sv, v_rstd)
            B.wait_slot("t", "B")
            for tc in range(TC):
                B.emit("t", lambda M, tc=tc: M.matmul(
                    PS_B[:, tc:tc + 1], rows[0:1, tc * 128:(tc + 1) * 128],
                    onesf[0:1, 0:1], start=True, stop=True))
                t_cols = B.emit("t", lambda M, tc=tc: M.matmul(
                    PS_B[:, TC + tc: TC + tc + 1],
                    rstd_row[0:1, tc * 128:(tc + 1) * 128],
                    onesbf[0:1, 0:1], start=True, stop=True),
                    inc=(st if tc == TC - 1 else None))
            B.wait("v", st, t_cols)
            v_cols = vq(lambda M: M.tensor_copy(
                cols[:, 0:2 * TC], PS_B[:, 0:2 * TC]))
            B.consumed("B", sv, v_cols)
            B.wait_slot("t", "Wstat")
            for tc in range(TC):
                for dc in range(DC):
                    t_raw = B.emit("t", lambda M, tc=tc, dc=dc: M.matmul(
                        PS_W[:, tc * E:(tc + 1) * E],
                        xs(dc, tc * 128, (tc + 1) * 128),
                        rw_sb[:, dc * E:(dc + 1) * E],
                        start=(dc == 0), stop=(dc == DC - 1)),
                        inc=(st if (tc == TC - 1 and dc == DC - 1) else None))
            B.wait("v", st, t_raw)
            B.wait("v", sv, v_cols)
            for tc in range(TC):
                vq(lambda M, tc=tc: M.tensor_scalar(
                    logits_sb[:, tc * E:(tc + 1) * E], sb8[:, :],
                    cols[:, tc:tc + 1], None, ALU.mult))
                vq(lambda M, tc=tc: M.tensor_sub(
                    logits_sb[:, tc * E:(tc + 1) * E], PS_W[:, tc * E:(tc + 1) * E],
                    logits_sb[:, tc * E:(tc + 1) * E]))
                vq(lambda M, tc=tc: M.tensor_scalar(
                    logits_sb[:, tc * E:(tc + 1) * E],
                    logits_sb[:, tc * E:(tc + 1) * E],
                    cols[:, TC + tc: TC + tc + 1], None, ALU.mult))
                vq(lambda M, tc=tc: M.tensor_reduce(
                    cols[:, 2 * TC + tc: 2 * TC + tc + 1],
                    logits_sb[:, tc * E:(tc + 1) * E], mybir.AxisListType.X, ALU.max))
            v_logit = vq(lambda M: M.tensor_scalar_mul(
                cols[:, 2 * TC: 3 * TC], cols[:, 2 * TC: 3 * TC], -1.0))
            B.wait("s", sv, v_logit)
            for tc in range(TC):
                s_exp = B.emit("s", lambda M, tc=tc: M.activation(
                    PS_W[:, tc * E:(tc + 1) * E], logits_sb[:, tc * E:(tc + 1) * E],
                    AF.Exp, bias=cols[:, 2 * TC + tc: 2 * TC + tc + 1], scale=1.0),
                    inc=(ss if tc == TC - 1 else None))
            B.wait("v", ss, s_exp)
            for tc in range(TC):
                vq(lambda M, tc=tc: M.tensor_reduce(
                    cols[:, 3 * TC + tc: 3 * TC + tc + 1],
                    PS_W[:, tc * E:(tc + 1) * E], mybir.AxisListType.X, ALU.add))
            v_se = vq(lambda M: M.reciprocal(
                cols[:, 3 * TC: 4 * TC], cols[:, 3 * TC: 4 * TC]))
            for tc in range(TC):
                vq(lambda M, tc=tc: M.tensor_scalar(
                    oh_sb[:, 0:E], logits_sb[:, tc * E:(tc + 1) * E],
                    cols[:, 2 * TC + tc: 2 * TC + tc + 1], 0.0, ALU.add, ALU.is_ge))
                v_mtok = vq(lambda M, tc=tc: M.tensor_scalar(
                    mtok_sb[:, tc * E:(tc + 1) * E], oh_sb[:, 0:E],
                    cols[:, 3 * TC + tc: 3 * TC + tc + 1], None, ALU.mult))
            B.wait("t", sv, v_mtok)
            B.wait("t", sconst, c_tot)
            for tc in range(TC):
                t_tr = B.emit("t", lambda M, tc=tc: M.matmul(
                    PS_W[0:E, tc * 128:(tc + 1) * 128],
                    mtok_sb[:, tc * E:(tc + 1) * E], i128_sb[:, :],
                    start=True, stop=True),
                    inc=(st if tc == TC - 1 else None))
            B.wait("v", st, t_tr)
            v_mrows = vq(lambda M, m=mrows_sb: M.tensor_copy(
                m[:, :], PS_W[0:E, 0:T]))
            B.consumed("Wstat", sv, v_mrows)
            return v_mrows

        def emit_ffn(v_gate, tag):
            with ExitStack() as ffn:
                fc = ffn.enter_context
                xup_sb = fc(nc.sbuf_tensor("xup_sb" + tag, [128, DC * T], BF16))
                hid_sb = fc(nc.sbuf_tensor("hid_sb" + tag, [128, 4 * T], BF16))
                U_sb = fc(nc.sbuf_tensor("U_sb" + tag, [128, DC * 512], BF16))
                G_sb = fc(nc.sbuf_tensor("G_sb" + tag, [128, DC * 512], BF16))
                D_sb = fc(nc.sbuf_tensor("D_sb" + tag, [128, 4 * D], BF16))
                mtb_sb = fc(nc.sbuf_tensor("mtb_sb" + tag, [128, T], BF16))
                mrows_sb = fc(nc.sbuf_tensor("mrows_sb" + tag, [E, T], BF16))
                silu_sb = fc(nc.sbuf_tensor("silu_sb" + tag, [128, 2 * 512], BF16))
                logits_sb = fc(nc.sbuf_tensor("logits" + tag, [128, TC * E], F32))
                oh_sb = fc(nc.sbuf_tensor("oh" + tag, [128, E], F32))
                mtok_sb = fc(nc.sbuf_tensor("mtok" + tag, [128, TC * E], BF16))
                sel8_sb = fc(nc.sbuf_tensor("sel8" + tag, [E, Er * 128], BF16))
                i128_sb = fc(nc.sbuf_tensor("i128" + tag, [128, 128], BF16))

                B.wait("y", sv, v_gate)
                B.emit("y", lambda M: M.dma_start(
                    out=sel8_sb[:, :], in_=dsel8[:, :]), inc=sconst, amount=16)
                B.emit("y", lambda M: M.dma_start(
                    out=i128_sb[:, :], in_=di128[:, :]), inc=sconst, amount=16)
                c_tot = sconst.n

                v_ln, v_rstd = emit_ln(hid_sb, 0, mtb_sb, 0)
                v_mrows = emit_router(mrows_sb, v_rstd, logits_sb, mtok_sb,
                                      oh_sb, i128_sb, c_tot)

                def wsrc(e):
                    if e < Er:
                        return (dup.ap()[e], dgate.ap()[e], ddown.ap()[e])
                    return (dsup.ap(), dsgate.ap(), dsdown.ap())

                def load_ug(e, hb):
                    up, gate, _ = wsrc(e)
                    s_u = up[:, hb * 512:(hb + 1) * 512].rearrange(
                        "(a p) j -> p a j", p=128)
                    d_u = U_sb.ap().rearrange("p (a j) -> p a j", a=DC)
                    vu = B.emit("g", lambda M, o=d_u, i=s_u: M.dma_start(out=o, in_=i),
                                inc=su, amount=16)
                    s_g = gate[:, hb * 512:(hb + 1) * 512].rearrange(
                        "(a p) j -> p a j", p=128)
                    d_g = G_sb.ap().rearrange("p (a j) -> p a j", a=DC)
                    vg = B.emit("g", lambda M, o=d_g, i=s_g: M.dma_start(out=o, in_=i),
                                inc=sg2, amount=16)
                    return (vu, vg)

                def load_d(e, hb):
                    _, _, down = wsrc(e)
                    s_d = down[hb * 512:(hb + 1) * 512, :].rearrange(
                        "(a p) j -> p a j", p=128)
                    d_d = D_sb.ap().rearrange("p (a j) -> p a j", a=4)
                    return B.emit("g", lambda M, o=d_d, i=s_d: M.dma_start(out=o, in_=i),
                                  inc=sd2, amount=16)

                def emit_xup(e, t_prev_A):
                    if e < Er:
                        B.wait("t", sv, v_mrows)
                        B.wait("t", sconst, c_tot)
                        for ft in range(FT):
                            key = ["B", "D"][ft % 2]
                            ps = PS[key]
                            B.wait_slot("t", key)
                            tv = B.emit("t", lambda M, ps=ps, ft=ft, e=e: M.matmul(
                                ps[:, 0:512], sel8_sb[:, e * 128:(e + 1) * 128],
                                mrows_sb[:, ft * 512:(ft + 1) * 512],
                                start=True, stop=True), inc=st)
                            B.wait("v", st, tv)
                            vv = B.emit("v", lambda M, ps=ps, ft=ft: M.tensor_copy(
                                mtb_sb[:, ft * 512:(ft + 1) * 512], ps[:, 0:512]),
                                inc=sv)
                            B.consumed(key, sv, vv)
                        vx = None
                        for dc in range(DC):
                            vx = vq(lambda M, dc=dc: M.tensor_mul(
                                xup_sb[:, dc * T:(dc + 1) * T], xlns(dc, 0, T),
                                mtb_sb[:, :]))
                        return (sv, vx)
                    else:
                        B.wait("s", st, t_prev_A)
                        B.wait("s", sv, v_ln)
                        vv = None
                        for dc in range(DC):
                            vv = B.emit("s", lambda M, dc=dc: M.activation(
                                xup_sb[:, dc * T:(dc + 1) * T], xlns(dc, 0, T),
                                AF.Identity, scale=0.5),
                                inc=(ss if dc == DC - 1 else None))
                        return (ss, vv)

                order = [Er] + list(range(Er))
                B.wait("g", sv, v_gate)
                ug_ready = load_ug(order[0], 0)
                d_ready = load_d(order[0], 0)
                t_lastA = 0

                v_xup = emit_xup(order[0], 0)
                v_xup_next = None
                for ei, e in enumerate(order):
                    if v_xup_next is not None:
                        v_xup = v_xup_next
                    v_xup_next = None
                    for hb in range(HB):
                        B.wait("t", su, ug_ready[0])
                        B.wait("t", sg2, ug_ready[1])
                        B.wait("t", v_xup[0], v_xup[1])
                        tiles = [(h, f) for h in range(4) for f in range(FT)]
                        for i, (hc, ft) in enumerate(tiles):
                            kU = ["A", "C"][i % 2]
                            kG = ["B", "D"][i % 2]
                            psU, psG = PS[kU], PS[kG]
                            sl = i % 2
                            B.wait_slot("t", kU)
                            B.wait_slot("t", kG)
                            for dc in range(DC):
                                B.emit("t", lambda M, psU=psU, dc=dc, hc=hc, ft=ft:
                                       M.matmul(
                                    psU[:, 0:512],
                                    U_sb[:, dc * 512 + hc * 128: dc * 512 + (hc + 1) * 128],
                                    xup_sb[:, dc * T + ft * 512: dc * T + (ft + 1) * 512],
                                    start=(dc == 0), stop=(dc == DC - 1)))
                            for dc in range(DC):
                                tA = B.emit("t", lambda M, psG=psG, dc=dc, hc=hc, ft=ft:
                                       M.matmul(
                                    psG[:, 0:512],
                                    G_sb[:, dc * 512 + hc * 128: dc * 512 + (hc + 1) * 128],
                                    xlns(dc, ft * 512, (ft + 1) * 512),
                                    start=(dc == 0), stop=(dc == DC - 1)),
                                    inc=(st if dc == DC - 1 else None))
                            B.wait("s", st, tA)
                            sS = B.emit("s", lambda M, psG=psG, sl=sl: M.activation(
                                silu_sb[:, sl * 512:(sl + 1) * 512], psG[:, 0:512],
                                AF.Silu), inc=ss)
                            B.wait("v", ss, sS)
                            vv = B.emit("v",
                                        lambda M, psU=psU, sl=sl, hc=hc, ft=ft:
                                        M.tensor_mul(
                                hid_sb[:, hc * T + ft * 512: hc * T + (ft + 1) * 512],
                                psU[:, 0:512],
                                silu_sb[:, sl * 512:(sl + 1) * 512]), inc=sv)
                            B.consumed(kU, sv, vv)
                            B.consumed(kG, sv, vv)
                            v_hid = vv
                        t_lastA = tA
                        nxt = (e, hb + 1) if hb + 1 < HB else (
                            (order[ei + 1], 0) if ei + 1 < len(order) else None)
                        if nxt is not None:
                            B.wait("g", st, t_lastA)
                            ug_next = load_ug(*nxt)
                        if hb == HB - 1 and ei + 1 < len(order):
                            v_xup_next = emit_xup(order[ei + 1], t_lastA)
                        B.wait("t", sd2, d_ready)
                        B.wait("t", sv, v_hid)
                        tB = 0
                        for dco in range(DC):
                            for half in range(NHALF):
                                B.wait_slot("t", "W%d" % half)
                                for hc in range(4):
                                    for q in range(NQ):
                                        c0 = half * WH + q * 512
                                        tB = B.emit("t",
                                               lambda M, dco=dco, hc=hc, c0=c0:
                                               M.matmul(
                                            PS_W[:, c0:c0 + 512],
                                            D_sb[:, hc * D + dco * 128: hc * D + (dco + 1) * 128],
                                            hid_sb[:, hc * T + c0: hc * T + c0 + 512],
                                            start=(hc == 0), stop=(hc == 3)),
                                            inc=(st if (hc == 3 and q == NQ - 1) else None))
                                B.wait("v", st, tB)
                                vv = B.emit("v",
                                            lambda M, dco=dco, half=half:
                                            M.tensor_add(
                                    xs(dco, half * WH, (half + 1) * WH),
                                    xs(dco, half * WH, (half + 1) * WH),
                                    PS_W[:, half * WH:(half + 1) * WH]), inc=sv)
                                B.consumed("W%d" % half, sv, vv)
                        if nxt is not None:
                            B.wait("g", st, tB)
                            d_ready = load_d(*nxt)
                            ug_ready = ug_next
                B.consumed("Wstat", sv, sv.n)
                return sv.n

        def emit_attn(v_gate):
            with ExitStack() as att:
                ac = att.enter_context
                W0 = ac(nc.sbuf_tensor("W0", [128, DC * D], BF16))
                W1 = ac(nc.sbuf_tensor("W1", [128, DC * D], BF16))
                K_tile = ac(nc.sbuf_tensor("K_tile", [128, 2 * D], BF16))
                V_tile = ac(nc.sbuf_tensor("V_tile", [128, 2 * D], BF16))
                ascr = ac(nc.sbuf_tensor("ascr", [128, DC * T], BF16))
                recip_sb = ac(nc.sbuf_tensor("recip_sb", [NH, T], BF16))
                cc_sb = ac(nc.sbuf_tensor("cc_sb", [128, NP * 64 + NP], F32))
                kvbf_sb = ac(nc.sbuf_tensor("kvbf_sb", [128, NP * 64], BF16))
                zbd_sb = ac(nc.sbuf_tensor("zbd_sb", [128, DC * NH], BF16))
                e16_sb = ac(nc.sbuf_tensor("e16_sb", [NH, NP * 128], BF16))
                bias_at = ac(nc.sbuf_tensor("bias_at", [1, 2 * D], BF16))
                sc1 = ac(nc.sbuf_tensor("sc1a", [128, 512], BF16))
                sc2 = ac(nc.sbuf_tensor("sc2a", [128, 512], BF16))

                def load_w(dw, W_t, sem):
                    s_w = dw.ap().rearrange("(a p) j -> p a j", p=128)
                    d_w = W_t.ap().rearrange("p (a j) -> p a j", a=DC)
                    return B.emit("g", lambda M, o=d_w, i=s_w: M.dma_start(
                        out=o, in_=i), inc=sem, amount=16)

                B.wait("y", sv, v_gate)
                y_e16 = B.emit("y", lambda M: M.dma_start(
                    out=e16_sb[:, :], in_=de16[:, :]), inc=se16, amount=16)
                B.wait("g", sv, v_gate)
                w_k = load_w(dwk, W0, sw)
                w_v = load_w(dwv, W1, B.sem["swb"])
                B.emit("g", lambda M: M.dma_start(
                    out=bias_at[0:1, 0:D], in_=dbk[:, :]), inc=sconsg, amount=16)
                B.emit("g", lambda M: M.dma_start(
                    out=bias_at[0:1, D:2 * D], in_=dbv[:, :]), inc=sconsg, amount=16)
                bias_kv = sconsg.n

                v_ln2, _ = emit_ln(ascr, 0, ascr, 3 * T)
                v_ccz = vq(lambda M: M.memset(cc_sb[:, :], 0.0))

                B.wait("t", sw, w_k)
                B.wait("t", B.sem["swb"], w_v)
                B.wait("t", sconsg, bias_kv)
                B.wait("t", sv, v_ln2)
                B.wait_slot("t", "A")
                B.wait_slot("t", "B")
                B.wait_slot("t", "C")
                B.wait_slot("t", "Wstat")
                t_kvtc = [0] * TC
                for tc in range(TC):
                    half = tc % 2
                    B.wait_slot("t", "WpK")
                    tvk = 0
                    for os in range(NOS):
                        c0 = os * OSW
                        for dc in range(DC):
                            B.emit("t", lambda M, dc=dc, tc=tc, c0=c0: M.matmul(
                                PS_W[:, c0:c0 + OSW],
                                xlns(dc, tc * 128, (tc + 1) * 128),
                                W0[:, dc * D + c0: dc * D + c0 + OSW],
                                start=(dc == 0), stop=False))
                        tvk = B.emit("t", lambda M, c0=c0: M.matmul(
                            PS_W[:, c0:c0 + OSW], onesbf[0:1, 0:128],
                            bias_at[0:1, c0:c0 + OSW],
                            start=False, stop=True),
                            inc=(st if os == NOS - 1 else None))
                    vphi = 0
                    for os in range(NOS):
                        c0 = os * OSW
                        dk = K_tile[:, half * D + c0: half * D + c0 + OSW]
                        B.wait("v", st, tvk)
                        if tc >= 2:
                            B.wait("v", st, t_kvtc[tc - 2])
                        v1 = vq(lambda M, c0=c0: M.tensor_scalar_min(
                            sc1[:, 0:OSW], PS_W[:, c0:c0 + OSW], 0.0))
                        B.wait("s", sv, v1)
                        s1 = B.emit("s", lambda M: M.activation(
                            sc2[:, 0:OSW], sc1[:, 0:OSW], AF.Exp), inc=ss)
                        B.wait("v", ss, s1)
                        vphi = vq(lambda M, c0=c0, dk=dk: M.scalar_tensor_tensor(
                            dk, PS_W[:, c0:c0 + OSW], 0.0,
                            sc2[:, 0:OSW], ALU.max, ALU.add))
                    B.consumed("WpK", sv, vphi)
                    B.wait_slot("t", "WpV")
                    tvv = 0
                    for os in range(NOS):
                        c0 = D + os * OSW
                        for dc in range(DC):
                            B.emit("t", lambda M, dc=dc, tc=tc, c0=c0: M.matmul(
                                PS_W[:, c0:c0 + OSW],
                                xlns(dc, tc * 128, (tc + 1) * 128),
                                W1[:, dc * D + c0 - D: dc * D + c0 - D + OSW],
                                start=(dc == 0), stop=False))
                        tvv = B.emit("t", lambda M, c0=c0: M.matmul(
                            PS_W[:, c0:c0 + OSW], onesbf[0:1, 0:128],
                            bias_at[0:1, c0:c0 + OSW],
                            start=False, stop=True),
                            inc=(st if os == NOS - 1 else None))
                    vcv = 0
                    for os in range(NOS):
                        c0 = D + os * OSW
                        dv = V_tile[:, half * D + os * OSW: half * D + (os + 1) * OSW]
                        B.wait("v", st, tvv)
                        if tc >= 2:
                            B.wait("v", st, t_kvtc[tc - 2])
                        vcv = vq(lambda M, c0=c0, dv=dv: M.tensor_copy(
                            dv, PS_W[:, c0:c0 + OSW]))
                    B.consumed("WpV", sv, vcv)
                    B.wait("t", sv, vcv)
                    if tc > 0:
                        B.wait("t", sv, v_kvadd)
                    for p in range(NP):
                        b0 = half * D + p * 128
                        B.emit("t", lambda M, p=p, b0=b0: M.matmul(
                            PS_A[0:64, p * 64:(p + 1) * 64],
                            K_tile[:, b0: b0 + 64], V_tile[:, b0: b0 + 64],
                            start=True, stop=True))
                        B.emit("t", lambda M, p=p, b0=b0: M.matmul(
                            PS_B[64:128, p * 64:(p + 1) * 64],
                            K_tile[:, b0 + 64: b0 + 128],
                            V_tile[:, b0 + 64: b0 + 128],
                            start=True, stop=True))
                        t_kvtc[tc] = B.emit("t", lambda M, p=p, b0=b0: M.matmul(
                            PS_C[:, p:p + 1],
                            K_tile[:, b0: b0 + 128], onescol[:, 0:1],
                            start=True, stop=True),
                            inc=(st if p == NP - 1 else None))
                    B.wait("v", st, t_kvtc[tc])
                    if tc == 0:
                        B.wait("v", sv, v_ccz)
                    vq(lambda M: M.tensor_add(
                        cc_sb[0:64, 0:NP * 64], cc_sb[0:64, 0:NP * 64],
                        PS_A[0:64, 0:NP * 64]))
                    vq(lambda M: M.tensor_add(
                        cc_sb[64:128, 0:NP * 64], cc_sb[64:128, 0:NP * 64],
                        PS_B[64:128, 0:NP * 64]))
                    v_kvadd = vq(lambda M: M.tensor_add(
                        cc_sb[:, NP * 64: NP * 64 + NP],
                        cc_sb[:, NP * 64: NP * 64 + NP], PS_C[:, 0:NP]))
                t_kv = t_kvtc[TC - 1]
                v_cc = v_kvadd

                B.wait("g", st, t_kv)
                w_q = load_w(dwq, W0, sw)
                w_o = load_w(dwo, W1, B.sem["swb"])
                B.emit("g", lambda M: M.dma_start(
                    out=bias_at[0:1, 0:D], in_=dbq[:, :]), inc=sconsg, amount=16)
                B.emit("g", lambda M: M.dma_start(
                    out=bias_at[0:1, D:2 * D], in_=dbo[:, :]), inc=sconsg, amount=16)
                bias_qo = sconsg.n

                B.consumed("A", sv, v_cc)
                B.consumed("B", sv, v_cc)
                B.consumed("C", sv, v_cc)
                B.wait("y", sv, v_cc)
                y_ccin = B.emit("y", lambda M: M.dma_start(
                    out=dccin[:, :], in_=cc_sb[:, :]), inc=sci, amount=16)
                B.wait("g", sci, y_ccin)
                g_cc = B.emit("g", lambda M: M.collective_compute(
                    "AllReduce", ALU.add,
                    replica_groups=[list(range(0, cfg.group)),
                                    list(range(cfg.group, 2 * cfg.group))],
                    ins=[dccin.ap().opt()], outs=[dccout.ap().opt()]), inc=scc)
                B.wait("g", scc, g_cc)
                g_ccr = B.emit("g", lambda M: M.dma_start(
                    out=cc_sb[:, :], in_=dccout[:, :]), inc=sgd, amount=16)

                phiQ = ascr
                B.wait("t", sw, w_q)
                B.wait("t", sconsg, bias_qo)
                t_q = 0
                v_q = 0
                for i, (oc, ft) in enumerate(
                        [(o, f) for o in range(DC) for f in range(FT)]):
                    key = ["C", "D"][i % 2]
                    ps = PS[key]
                    B.wait_slot("t", key)
                    for dc in range(DC):
                        B.emit("t", lambda M, ps=ps, dc=dc, oc=oc, ft=ft: M.matmul(
                            ps[:, 0:512],
                            W0[:, dc * D + oc * 128: dc * D + (oc + 1) * 128],
                            xlns(dc, ft * 512, (ft + 1) * 512),
                            start=(dc == 0), stop=False))
                    t_q = B.emit("t", lambda M, ps=ps, oc=oc: M.matmul(
                        ps[:, 0:512], bias_at[0:1, oc * 128:(oc + 1) * 128],
                        onesbf[0:1, :], start=False, stop=True), inc=st)
                    B.wait("v", st, t_q)
                    v1 = vq(lambda M, ps=ps: M.tensor_scalar_min(
                        sc1[:, :], ps[:, 0:512], 0.0))
                    B.wait("s", sv, v1)
                    s1 = B.emit("s", lambda M: M.activation(
                        sc2[:, :], sc1[:, :], AF.Exp), inc=ss)
                    B.wait("v", ss, s1)
                    v_q = vq(lambda M, ps=ps, oc=oc, ft=ft: M.scalar_tensor_tensor(
                        phiQ[:, oc * T + ft * 512: oc * T + (ft + 1) * 512],
                        ps[:, 0:512], 0.0, sc2[:, :], ALU.max, ALU.add))
                    B.consumed(key, sv, v_q)

                B.wait("v", sgd, g_ccr)
                vq(lambda M: M.tensor_copy(
                    kvbf_sb[:, :], cc_sb[:, 0:NP * 64]))
                vq(lambda M: M.memset(zbd_sb[:, :], 0.0))
                v_kvz = 0
                for dc in range(DC):
                    vq(lambda M, dc=dc: M.tensor_copy(
                        zbd_sb[0:64, dc * NH + 2 * dc: dc * NH + 2 * dc + 1],
                        cc_sb[0:64, NP * 64 + dc: NP * 64 + dc + 1]))
                    v_kvz = vq(lambda M, dc=dc: M.tensor_copy(
                        zbd_sb[64:128, dc * NH + 2 * dc + 1: dc * NH + 2 * dc + 2],
                        cc_sb[64:128, NP * 64 + dc: NP * 64 + dc + 1]))

                B.wait("t", sv, max(v_kvz, v_q))
                B.wait_slot("t", "Wstat")
                t_nrm = 0
                for qq in range(FT):
                    for dc in range(DC):
                        t_nrm = B.emit("t", lambda M, qq=qq, dc=dc: M.matmul(
                            PS_W[0:NH, qq * 512:(qq + 1) * 512],
                            zbd_sb[:, dc * NH:(dc + 1) * NH],
                            phiQ[:, dc * T + qq * 512: dc * T + (qq + 1) * 512],
                            start=(dc == 0), stop=(dc == DC - 1)),
                            inc=(st if (qq == FT - 1 and dc == DC - 1) else None))
                B.wait("v", st, t_nrm)
                vq(lambda M: M.tensor_scalar(
                    PS_W[32:32 + NH, 0:T], PS_W[0:NH, 0:T],
                    float(cfg.attn_eps), None, ALU.add))

                def _recip_lp(M):
                    with nc.allow_low_precision(reason="bf16 attn norm recip"):
                        return M.reciprocal(recip_sb[:, :], PS_W[32:32 + NH, 0:T])
                v_rec = vq(_recip_lp)
                B.consumed("Wstat", sv, v_rec)

                B.wait("t", sv, v_rec)
                B.wait("t", se16, y_e16)
                okeys = [("A", "B"), ("C", "D")]
                v_fm = 0
                for i, (p, ft) in enumerate(
                        [(p, f) for p in range(NP) for f in range(FT)]):
                    kO, kR = okeys[i % 2]
                    psO, psR = PS[kO], PS[kR]
                    B.wait_slot("t", kO)
                    B.wait_slot("t", kR)
                    B.emit("t", lambda M, psO=psO, p=p, ft=ft: M.matmul(
                        psO[0:64, 0:512], kvbf_sb[0:64, p * 64:(p + 1) * 64],
                        phiQ[0:64, p * T + ft * 512: p * T + (ft + 1) * 512],
                        start=True, stop=True))
                    B.emit("t", lambda M, psO=psO, p=p, ft=ft: M.matmul(
                        psO[64:128, 0:512], kvbf_sb[64:128, p * 64:(p + 1) * 64],
                        phiQ[64:128, p * T + ft * 512: p * T + (ft + 1) * 512],
                        start=True, stop=True))
                    tv = B.emit("t", lambda M, psR=psR, p=p, ft=ft: M.matmul(
                        psR[:, 0:512], e16_sb[:, p * 128:(p + 1) * 128],
                        recip_sb[:, ft * 512:(ft + 1) * 512],
                        start=True, stop=True), inc=st)
                    B.wait("v", st, tv)
                    vq(lambda M, psR=psR: M.tensor_copy(
                        sc1[:, :], psR[:, 0:512]))
                    v_fm = vq(lambda M, psO=psO, p=p, ft=ft: M.tensor_mul(
                        xlns(p, ft * 512, (ft + 1) * 512), psO[:, 0:512],
                        sc1[:, :]))
                    B.consumed(kO, sv, v_fm)
                    B.consumed(kR, sv, v_fm)

                B.wait("t", B.sem["swb"], w_o)
                B.wait("t", sv, v_fm)
                v_out = 0
                for i, (oc, ft) in enumerate(
                        [(o, f) for o in range(DC) for f in range(FT)]):
                    key = ["C", "D"][i % 2]
                    ps = PS[key]
                    B.wait_slot("t", key)
                    for dc in range(DC):
                        B.emit("t", lambda M, ps=ps, dc=dc, oc=oc, ft=ft: M.matmul(
                            ps[:, 0:512],
                            W1[:, dc * D + oc * 128: dc * D + (oc + 1) * 128],
                            xlns(dc, ft * 512, (ft + 1) * 512),
                            start=(dc == 0), stop=False))
                    tv = B.emit("t", lambda M, ps=ps, oc=oc: M.matmul(
                        ps[:, 0:512], bias_at[0:1, D + oc * 128: D + (oc + 1) * 128],
                        onesbf[0:1, :], start=False, stop=True), inc=st)
                    B.wait("v", st, tv)
                    v_out = B.emit("v", lambda M, ps=ps, oc=oc, ft=ft: M.tensor_add(
                        xs(oc, ft * 512, (ft + 1) * 512),
                        xs(oc, ft * 512, (ft + 1) * 512), ps[:, 0:512]), inc=sv)
                    B.consumed(key, sv, v_out)
                return v_out

        B.wait("s", sx, y_xload)
        B.wait("t", sx, y_xload)
        v1 = emit_ffn(v_sb8, "_f1")
        B.wait("s", sv, v1)
        v2 = emit_attn(v1)
        B.wait("s", sv, v2)
        v3 = emit_ffn(v2, "_f2")

        B.wait("y", sv, v3)
        osrc = x_sb.ap().rearrange("p (a j) -> p a j", a=DC)
        odst = dy.ap().rearrange("(a p) j -> p a j", p=128)
        y_out = B.emit("y", lambda M, o=odst, i=osrc: M.dma_start(out=o, in_=i),
                       inc=syo, amount=16)
        B.wait("y", syo, y_out)

        B.finalize()
    return nc


def _host_constants(cfg):
    E, Er, NH, NP = cfg.E, cfg.Er, cfg.NH, cfg.NP
    sel8 = np.zeros((E, Er * 128), dtype=BF)
    for e in range(Er):
        sel8[e, e * 128:(e + 1) * 128] = 1.0
    i128 = np.eye(128, dtype=np.float32).astype(BF)
    e16 = np.zeros((NH, NP * 128), dtype=BF)
    for p in range(NP):
        e16[2 * p, p * 128: p * 128 + 64] = 1.0
        e16[2 * p + 1, p * 128 + 64: (p + 1) * 128] = 1.0
    return sel8, i128, e16


def stage_inputs(cfg, x, Wq, bq, Wk, bk, Wv, bv, Wo, bo, rw, upw, gatew, downw,
                 sup, sgate, sdown, n_cores=8):
    f32 = np.float32
    x = np.asarray(x, f32)
    Bb = x.shape[0]
    per_batch = n_cores // Bb
    sel8, i128, e16 = _host_constants(cfg)
    common = {
        "wq": np.ascontiguousarray(np.asarray(Wq, f32).T),
        "wk": np.ascontiguousarray(np.asarray(Wk, f32).T),
        "wv": np.ascontiguousarray(np.asarray(Wv, f32).T),
        "wo": np.ascontiguousarray(np.asarray(Wo, f32).T),
        "bq": np.asarray(bq, f32).reshape(1, -1),
        "bk": np.asarray(bk, f32).reshape(1, -1),
        "bv": np.asarray(bv, f32).reshape(1, -1),
        "bo": np.asarray(bo, f32).reshape(1, -1),
        "rw": np.ascontiguousarray(np.asarray(rw, f32)),
        "upw": np.ascontiguousarray(np.asarray(upw, f32)),
        "gatew": np.ascontiguousarray(np.asarray(gatew, f32)),
        "downw": np.ascontiguousarray(np.asarray(downw, f32)),
        "sup": np.ascontiguousarray(np.asarray(sup, f32)),
        "sgate": np.ascontiguousarray(np.asarray(sgate, f32)),
        "sdown": np.ascontiguousarray(np.asarray(sdown, f32)),
        "sel8": sel8, "i128": i128, "e16": e16,
    }
    in_maps = []
    for c in range(n_cores):
        b = c // per_batch
        t0 = (c % per_batch) * cfg.T
        m = dict(common)
        m["x"] = np.ascontiguousarray(x[b, t0:t0 + cfg.T, :].T)
        in_maps.append(m)
    return in_maps


def gather_out(cfg, results, Bb, S, D, n_cores=8):
    per_batch = n_cores // Bb
    out = np.empty((Bb, S, D), dtype=np.float32)
    for c in range(n_cores):
        b = c // per_batch
        t0 = (c % per_batch) * cfg.T
        out[b, t0:t0 + cfg.T, :] = results[c]["y"].T
    return out


def kernel(L0_attn_x, L0_attn_Wq, L0_attn_bq, L0_attn_Wk, L0_attn_bk,
           L0_attn_Wv, L0_attn_bv, L0_attn_Wo, L0_attn_bo,
           L0_ffn_router_w, L0_ffn_up_w, L0_ffn_gate_w, L0_ffn_down_w,
           L0_ffn_shared_up_w, L0_ffn_shared_gate_w, L0_ffn_shared_down_w):
    x = np.asarray(L0_attn_x, dtype=np.float32)
    Bb, S, D = x.shape
    upw = np.asarray(L0_ffn_up_w)
    cfg = Cfg(T=(Bb * S) // 8, D=D, HF=upw.shape[2], Er=upw.shape[0],
              E=np.asarray(L0_ffn_router_w).shape[1], group=8 // Bb)
    in_maps = stage_inputs(cfg, x, L0_attn_Wq, L0_attn_bq, L0_attn_Wk,
                           L0_attn_bk, L0_attn_Wv, L0_attn_bv, L0_attn_Wo,
                           L0_attn_bo, L0_ffn_router_w, L0_ffn_up_w,
                           L0_ffn_gate_w, L0_ffn_down_w, L0_ffn_shared_up_w,
                           L0_ffn_shared_gate_w, L0_ffn_shared_down_w)
    nc = build_nc(cfg)
    res = run_bass_kernel_spmd(nc, in_maps, core_ids=list(range(8)))
    return gather_out(cfg, res.results, Bb, S, D)


# version 04
# speedup vs baseline: 2.0268x; 2.0162x over previous
from contextlib import ExitStack

import numpy as np
import ml_dtypes

import concourse.bass as bass
import concourse.mybir as mybir
from concourse.bass_utils import run_bass_kernel_spmd

F32 = mybir.dt.float32
BF16 = mybir.dt.bfloat16
I32 = mybir.dt.int32
AF = mybir.ActivationFunctionType
ALU = mybir.AluOpType
BF = ml_dtypes.bfloat16


class Cfg:
    def __init__(self, T=2048, D=1024, HF=4096, Er=7, E=8, group=4,
                 ln_eps=1e-5, attn_eps=1e-6):
        assert T % 512 == 0 and D % 128 == 0 and HF % 512 == 0
        self.T, self.D, self.HF, self.Er, self.E = T, D, HF, Er, E
        self.DC = D // 128
        self.TC = T // 128
        self.FT = T // 512
        self.HB = HF // 512
        self.NH = D // 64
        self.NP = D // 128
        self.group = group
        self.ln_eps, self.attn_eps = ln_eps, attn_eps
        self.C = T // 4
        self.SR = ((Er * self.C + 2 + 127) // 128) * 128


class Sem:
    def __init__(self, nc, name):
        self.h = nc.alloc_semaphore(name)
        self.n = 0

    def bump(self, amount=1):
        self.n += amount
        return self.n


class Builder:
    ENGS = ("t", "v", "s", "g", "y")

    def __init__(self, nc):
        self.nc = nc
        self.q = {e: [] for e in self.ENGS}
        self.sem = {k: Sem(nc, k) for k in
                    ("st", "sv", "ss", "scc", "sgd", "syd", "su", "sg2", "sd2",
                     "sw", "swb", "sx", "sconst", "sconsg", "se16", "sci", "syo",
                     "szf", "six0", "six1", "sxg", "srw", "sbg0", "sbg1")}
        self.cons = {}

    def wait(self, eng, sem, val):
        if val is None or val <= 0:
            return
        h = sem.h
        self.q[eng].append(lambda M: M.wait_ge(h, val))

    def wait_slot(self, eng, key):
        c = self.cons.get(key)
        if c is not None:
            self.wait(eng, c[0], c[1])

    def consumed(self, key, sem, val):
        self.cons[key] = (sem, val)

    def emit(self, eng, fn, inc=None, amount=1):
        if inc is None:
            self.q[eng].append(fn)
            return None
        val = inc.bump(amount)
        h = inc.h

        def L(M, fn=fn, h=h, amount=amount):
            fn(M).then_inc(h, amount)
        self.q[eng].append(L)
        return val

    def finalize(self):
        with self.nc.Block() as block:
            @block.tensor
            def _(E):
                for f in self.q["t"]:
                    f(E)

            @block.vector
            def _(E):
                for f in self.q["v"]:
                    f(E)

            @block.scalar
            def _(E):
                for f in self.q["s"]:
                    f(E)

            @block.gpsimd
            def _(E):
                for f in self.q["g"]:
                    f(E)

            @block.sync
            def _(E):
                for f in self.q["y"]:
                    f(E)


def build_nc(cfg: Cfg):
    nc = bass.Bass("TRN2", target_bir_lowering=False)
    T, D, HF, Er, E = cfg.T, cfg.D, cfg.HF, cfg.Er, cfg.E
    DC, TC, FT, HB, NH, NP = cfg.DC, cfg.TC, cfg.FT, cfg.HB, cfg.NH, cfg.NP
    WH = min(1024, T)
    NHALF = T // WH
    NQ = WH // 512
    OSW = min(512, D)
    NOS = D // OSW

    dx = nc.dram_tensor("x", [D, T], F32, kind="ExternalInput")
    dy = nc.dram_tensor("y", [D, T], F32, kind="ExternalOutput")
    dwq = nc.dram_tensor("wq", [D, D], F32, kind="ExternalInput")
    dwk = nc.dram_tensor("wk", [D, D], F32, kind="ExternalInput")
    dwv = nc.dram_tensor("wv", [D, D], F32, kind="ExternalInput")
    dwo = nc.dram_tensor("wo", [D, D], F32, kind="ExternalInput")
    dbq = nc.dram_tensor("bq", [1, D], F32, kind="ExternalInput")
    dbk = nc.dram_tensor("bk", [1, D], F32, kind="ExternalInput")
    dbv = nc.dram_tensor("bv", [1, D], F32, kind="ExternalInput")
    dbo = nc.dram_tensor("bo", [1, D], F32, kind="ExternalInput")
    drw = nc.dram_tensor("rw", [D, E], F32, kind="ExternalInput")
    dup = nc.dram_tensor("upw", [Er, D, HF], F32, kind="ExternalInput")
    dgate = nc.dram_tensor("gatew", [Er, D, HF], F32, kind="ExternalInput")
    ddown = nc.dram_tensor("downw", [Er, HF, D], F32, kind="ExternalInput")
    dsup = nc.dram_tensor("sup", [D, HF], F32, kind="ExternalInput")
    dsgate = nc.dram_tensor("sgate", [D, HF], F32, kind="ExternalInput")
    dsdown = nc.dram_tensor("sdown", [HF, D], F32, kind="ExternalInput")
    dltri = nc.dram_tensor("ltri", [128, 128], BF16, kind="ExternalInput")
    dltriT = nc.dram_tensor("ltriT", [TC, TC], F32, kind="ExternalInput")
    diTf = nc.dram_tensor("iTf", [TC, TC], F32, kind="ExternalInput")
    dslots = nc.dram_tensor("slots", [cfg.SR, D], BF16, kind="ExternalInput")
    di128 = nc.dram_tensor("i128", [128, 128], BF16, kind="ExternalInput")
    de16 = nc.dram_tensor("e16", [NH, NP * 128], BF16, kind="ExternalInput")
    dccin = nc.dram_tensor("cc_in", [128, NP * 64 + NP], F32)
    dccout = nc.dram_tensor("cc_out", [128, NP * 64 + NP], F32)

    with ExitStack() as top:
        ec = top.enter_context
        x_sb = ec(nc.sbuf_tensor("x_sb", [128, DC * T], F32))
        xln_sb = ec(nc.sbuf_tensor("xln_sb", [128, DC * T], BF16))
        rows = ec(nc.sbuf_tensor("rows", [128, T], F32))
        rstd_row = ec(nc.sbuf_tensor("rstd_row", [1, T], BF16))
        onesf = ec(nc.sbuf_tensor("onesf", [1, 128], F32))
        onesbf = ec(nc.sbuf_tensor("onesbf", [1, 512], BF16))
        onescol = ec(nc.sbuf_tensor("onescol", [128, 1], BF16))
        statw = ec(nc.sbuf_tensor("statw", [128, 1], BF16))
        ones128f = ec(nc.sbuf_tensor("ones128f", [128, 1], F32))
        rw_sb = ec(nc.sbuf_tensor("rw_sb", [128, DC * E], F32))
        srow = ec(nc.sbuf_tensor("srow", [1, E], F32))
        sb8 = ec(nc.sbuf_tensor("sb8", [128, E], F32))
        cols = ec(nc.sbuf_tensor("cols", [128, 4 * TC], F32))
        PS_A = ec(nc.psum_tensor("PS_A", [128, 512], F32))
        PS_B = ec(nc.psum_tensor("PS_B", [128, 512], F32))
        PS_C = ec(nc.psum_tensor("PS_C", [128, 512], F32))
        PS_D = ec(nc.psum_tensor("PS_D", [128, 512], F32))
        PS_W = ec(nc.psum_tensor("PS_W", [128, 2048], F32))
        PS = {"A": PS_A, "B": PS_B, "C": PS_C, "D": PS_D}

        B = Builder(nc)
        st, sv, ss = B.sem["st"], B.sem["sv"], B.sem["ss"]
        sgd, syd, scc = B.sem["sgd"], B.sem["syd"], B.sem["scc"]
        su, sg2, sd2, sw = B.sem["su"], B.sem["sg2"], B.sem["sd2"], B.sem["sw"]
        sx, sconst, se16 = B.sem["sx"], B.sem["sconst"], B.sem["se16"]
        sconsg = B.sem["sconsg"]
        sci, syo = B.sem["sci"], B.sem["syo"]

        def xs(dc, a, b):
            return x_sb[:, dc * T + a: dc * T + b]

        def xlns(dc, a, b):
            return xln_sb[:, dc * T + a: dc * T + b]

        def vq(fn):
            B.wait("v", sv, sv.n)
            return B.emit("v", fn, inc=sv)

        B.emit("v", lambda M: M.memset(onesf[:, :], 1.0))
        B.emit("v", lambda M: M.memset(onesbf[:, :], 1.0))
        B.emit("v", lambda M: M.memset(onescol[:, :], 1.0))
        B.emit("v", lambda M: M.memset(statw[:, :], 1.0 / D))
        v_consts = B.emit("v", lambda M: M.memset(ones128f[:, :], 1.0), inc=sv)

        src = dx.ap().rearrange("(a p) j -> p a j", p=128)
        dst = x_sb.ap().rearrange("p (a j) -> p a j", a=DC)
        y_xload = B.emit("y", lambda M, o=dst, i=src: M.dma_start(out=o, in_=i),
                         inc=sx, amount=16)
        rsrc = drw.ap().rearrange("(a p) j -> p a j", p=128)
        rdst = rw_sb.ap().rearrange("p (a j) -> p a j", a=DC)
        B.emit("g", lambda M, o=rdst, i=rsrc: M.dma_start(out=o, in_=i),
                      inc=sconsg, amount=16)
        CONSG_TOTAL = sconsg.n

        B.wait("t", sconsg, CONSG_TOTAL)
        B.wait("t", sv, v_consts)
        t_cs = None
        for dc in range(DC):
            t_cs = B.emit("t", lambda M, dc=dc: M.matmul(
                PS_B[0:1, 0:E], ones128f[:, 0:1], rw_sb[:, dc * E:(dc + 1) * E],
                start=(dc == 0), stop=(dc == DC - 1)),
                inc=(st if dc == DC - 1 else None))
        B.wait("v", st, t_cs)
        v_srow = B.emit("v", lambda M: M.tensor_copy(srow[:, :], PS_B[0:1, 0:E]), inc=sv)
        B.wait("t", sv, v_srow)
        t_sb8 = B.emit("t", lambda M: M.matmul(
            PS_B[:, 32:32 + E], onesf[0:1, 0:128], srow[0:1, :],
            start=True, stop=True), inc=st)
        B.wait("v", st, t_sb8)
        v_sb8 = B.emit("v", lambda M: M.tensor_copy(sb8[:, :], PS_B[:, 32:32 + E]), inc=sv)
        B.consumed("B", sv, v_sb8)

        def emit_ln(scr_sb, scr_off, sq_sb, sq_off):
            t_dc = 0
            for dc in range(DC):
                B.emit("s", lambda M, dc=dc: M.copy(xlns(dc, 0, T), xs(dc, 0, T)))
                B.wait("s", st, t_dc)
                v = B.emit("s", lambda M, dc=dc: M.activation(
                    sq_sb[:, sq_off: sq_off + T], xs(dc, 0, T), AF.Square), inc=ss)
                B.wait("t", ss, v)
                if dc == 0:
                    B.wait_slot("t", "Wstat")
                for ft in range(FT):
                    B.emit("t", lambda M, dc=dc, ft=ft: M.matmul(
                        PS_W[0:1, ft * 512:(ft + 1) * 512], statw[:, 0:1],
                        xlns(dc, ft * 512, (ft + 1) * 512),
                        start=(dc == 0), stop=(dc == DC - 1)))
                    t_dc = B.emit("t", lambda M, dc=dc, ft=ft: M.matmul(
                        PS_W[32:33, ft * 512:(ft + 1) * 512], statw[:, 0:1],
                        sq_sb[:, sq_off + ft * 512: sq_off + (ft + 1) * 512],
                        start=(dc == 0), stop=(dc == DC - 1)),
                        inc=(st if ft == FT - 1 else None))
            t_stats = t_dc
            B.wait("v", st, t_stats)
            vq(lambda M: M.tensor_copy(rows[0:1, :], PS_W[0:1, 0:T]))
            vq(lambda M: M.tensor_mul(rows[96:97, :], rows[0:1, :],
                                      rows[0:1, :]))
            v_var = vq(lambda M: M.scalar_tensor_tensor(
                rows[64:65, :], PS_W[32:33, 0:T], float(cfg.ln_eps),
                rows[96:97, :], ALU.add, ALU.subtract))
            B.consumed("Wstat", sv, v_var)
            B.wait("s", sv, v_var)
            s_sd = B.emit("s", lambda M: M.activation(
                rows[96:97, :], rows[64:65, :], AF.Sqrt), inc=ss)
            B.wait("v", ss, s_sd)
            def _rstd_lp(M):
                with nc.allow_low_precision(reason="bf16 rstd row"):
                    return M.reciprocal(rstd_row[0:1, :], rows[96:97, :])
            v_rstd = vq(_rstd_lp)
            B.wait("t", sv, v_rstd)
            names = ["A", "B", "C", "D"]
            for i in range(2 * FT):
                ft, which = i // 2, i % 2
                key = names[i % 4]
                ps = PS[key]
                B.wait_slot("t", key)
                if which == 0:
                    tv = B.emit("t", lambda M, ps=ps, ft=ft: M.matmul(
                        ps[:, 0:512], onesf[0:1, :],
                        rows[0:1, ft * 512:(ft + 1) * 512],
                        start=True, stop=True), inc=st)
                else:
                    tv = B.emit("t", lambda M, ps=ps, ft=ft: M.matmul(
                        ps[:, 0:512], onesbf[0:1, 0:128],
                        rstd_row[0:1, ft * 512:(ft + 1) * 512],
                        start=True, stop=True), inc=st)
                B.wait("v", st, tv)
                off = scr_off + T + which * T + ft * 512
                vv = B.emit("v", lambda M, ps=ps, off=off: M.tensor_copy(
                    scr_sb[:, off: off + 512], ps[:, 0:512]), inc=sv)
                B.consumed(key, sv, vv)
            v_ln = None
            for dc in range(DC):
                vq(lambda M, dc=dc: M.tensor_sub(
                    scr_sb[:, scr_off: scr_off + T], xs(dc, 0, T),
                    scr_sb[:, scr_off + T: scr_off + 2 * T]))
                v_ln = vq(lambda M, dc=dc: M.tensor_mul(
                    xlns(dc, 0, T), scr_sb[:, scr_off: scr_off + T],
                    scr_sb[:, scr_off + 2 * T: scr_off + 3 * T]))
            return v_ln, v_rstd

        def emit_router(v_rstd, logits_sb, mtok_sb):
            B.wait("t", sv, v_rstd)
            B.wait_slot("t", "B")
            for tc in range(TC):
                B.emit("t", lambda M, tc=tc: M.matmul(
                    PS_B[:, tc:tc + 1], rows[0:1, tc * 128:(tc + 1) * 128],
                    onesf[0:1, 0:1], start=True, stop=True))
                t_cols = B.emit("t", lambda M, tc=tc: M.matmul(
                    PS_B[:, TC + tc: TC + tc + 1],
                    rstd_row[0:1, tc * 128:(tc + 1) * 128],
                    onesbf[0:1, 0:1], start=True, stop=True),
                    inc=(st if tc == TC - 1 else None))
            B.wait("v", st, t_cols)
            v_cols = vq(lambda M: M.tensor_copy(
                cols[:, 0:2 * TC], PS_B[:, 0:2 * TC]))
            B.consumed("B", sv, v_cols)
            B.wait_slot("t", "Wstat")
            for tc in range(TC):
                for dc in range(DC):
                    t_raw = B.emit("t", lambda M, tc=tc, dc=dc: M.matmul(
                        PS_W[:, tc * E:(tc + 1) * E],
                        xs(dc, tc * 128, (tc + 1) * 128),
                        rw_sb[:, dc * E:(dc + 1) * E],
                        start=(dc == 0), stop=(dc == DC - 1)),
                        inc=(st if (tc == TC - 1 and dc == DC - 1) else None))
            B.wait("v", st, t_raw)
            B.wait("v", sv, v_cols)
            for tc in range(TC):
                vq(lambda M, tc=tc: M.tensor_scalar(
                    logits_sb[:, tc * E:(tc + 1) * E], sb8[:, :],
                    cols[:, tc:tc + 1], None, ALU.mult))
                vq(lambda M, tc=tc: M.tensor_sub(
                    logits_sb[:, tc * E:(tc + 1) * E], PS_W[:, tc * E:(tc + 1) * E],
                    logits_sb[:, tc * E:(tc + 1) * E]))
                vq(lambda M, tc=tc: M.tensor_scalar(
                    logits_sb[:, tc * E:(tc + 1) * E],
                    logits_sb[:, tc * E:(tc + 1) * E],
                    cols[:, TC + tc: TC + tc + 1], None, ALU.mult))
                vq(lambda M, tc=tc: M.tensor_reduce(
                    cols[:, 2 * TC + tc: 2 * TC + tc + 1],
                    logits_sb[:, tc * E:(tc + 1) * E], mybir.AxisListType.X, ALU.max))
            v_logit = vq(lambda M: M.tensor_scalar_mul(
                cols[:, 2 * TC: 3 * TC], cols[:, 2 * TC: 3 * TC], -1.0))
            B.wait("s", sv, v_logit)
            for tc in range(TC):
                s_exp = B.emit("s", lambda M, tc=tc: M.activation(
                    PS_W[:, tc * E:(tc + 1) * E], logits_sb[:, tc * E:(tc + 1) * E],
                    AF.Exp, bias=cols[:, 2 * TC + tc: 2 * TC + tc + 1], scale=1.0),
                    inc=(ss if tc == TC - 1 else None))
            B.wait("v", ss, s_exp)
            for tc in range(TC):
                vq(lambda M, tc=tc: M.tensor_reduce(
                    cols[:, 3 * TC + tc: 3 * TC + tc + 1],
                    PS_W[:, tc * E:(tc + 1) * E], mybir.AxisListType.X, ALU.add))
            v_se = vq(lambda M: M.reciprocal(
                cols[:, 3 * TC: 4 * TC], cols[:, 3 * TC: 4 * TC]))
            v_mtok = None
            for tc in range(TC):
                v_mtok = vq(lambda M, tc=tc: M.tensor_scalar(
                    mtok_sb[:, tc * E:(tc + 1) * E],
                    logits_sb[:, tc * E:(tc + 1) * E],
                    cols[:, 2 * TC + tc: 2 * TC + tc + 1], 0.0,
                    ALU.add, ALU.is_ge))
            B.consumed("Wstat", sv, v_mtok)
            return v_mtok

        def emit_ffn(v_gate, tag):
            C, SR = cfg.C, cfg.SR
            CB = C // 128
            ND = D // 512
            TRASH, ZROW = float(Er * C), float(Er * C + 1)
            ZB = (Er * C // 128) * 128
            with ExitStack() as ffn:
                fc = ffn.enter_context
                hid_sb = fc(nc.sbuf_tensor("hid" + tag, [128, 4 * T], BF16))
                U_sb = fc(nc.sbuf_tensor("U" + tag, [128, DC * 512], BF16))
                G_sb = fc(nc.sbuf_tensor("G" + tag, [128, DC * 512], BF16))
                D_sb = fc(nc.sbuf_tensor("Dw" + tag, [128, 4 * D], BF16))
                silu_sb = fc(nc.sbuf_tensor("silu" + tag, [128, 1024], BF16))
                logits_sb = fc(nc.sbuf_tensor("logits" + tag, [128, TC * E], F32))
                mtok_sb = fc(nc.sbuf_tensor("mtok" + tag, [128, TC * E], BF16))
                xg_tok = fc(nc.sbuf_tensor("xgt" + tag, [128, CB * D], BF16))
                xg_fm = fc(nc.sbuf_tensor("xgf" + tag, [128, DC * C], BF16))
                slotout = fc(nc.sbuf_tensor("sout" + tag, [128, CB * D], F32))
                xtt = fc(nc.sbuf_tensor("xtt" + tag, [128, 2 * D], BF16))
                rtt = fc(nc.sbuf_tensor("rtt" + tag, [128, 2 * D], BF16))
                pfx_sb = fc(nc.sbuf_tensor("pfx" + tag, [128, 2 * TC], F32))
                psc = fc(nc.sbuf_tensor("psc" + tag, [128, 4], F32))
                mpack = fc(nc.sbuf_tensor("mpack" + tag, [128, TC], BF16))
                selm = fc(nc.sbuf_tensor("selm" + tag, [128, TC], F32))
                tmp_sb = fc(nc.sbuf_tensor("tmpx" + tag, [128, TC], F32))
                orow_f = fc(nc.sbuf_tensor("orf" + tag, [128, TC], F32))
                orow_i = fc(nc.sbuf_tensor("ori" + tag, [128, TC], I32))
                brow_i = fc(nc.sbuf_tensor("bri" + tag, [128, TC], I32))
                ltri_sb = fc(nc.sbuf_tensor("ltri" + tag, [128, 128], BF16))
                ltriT_sb = fc(nc.sbuf_tensor("ltriT" + tag, [TC, TC], F32))
                iTf_sb = fc(nc.sbuf_tensor("iTf" + tag, [TC, TC], F32))
                i128_sb = fc(nc.sbuf_tensor("i128" + tag, [128, 128], BF16))

                B.wait("y", sv, v_gate)
                B.emit("y", lambda M: M.dma_start(
                    out=ltri_sb[:, :], in_=dltri[:, :]), inc=sconst, amount=16)
                B.emit("y", lambda M: M.dma_start(
                    out=ltriT_sb[:, :], in_=dltriT[:, :]), inc=sconst, amount=16)
                B.emit("y", lambda M: M.dma_start(
                    out=iTf_sb[:, :], in_=diTf[:, :]), inc=sconst, amount=16)
                B.emit("y", lambda M: M.dma_start(
                    out=i128_sb[:, :], in_=di128[:, :]), inc=sconst, amount=16)
                c_tot = sconst.n

                def wsrc(e):
                    if e < Er:
                        return (dup.ap()[e], dgate.ap()[e], ddown.ap()[e])
                    return (dsup.ap(), dsgate.ap(), dsdown.ap())

                def load_ug(e, hb):
                    up, gate, _ = wsrc(e)
                    s_u = up[:, hb * 512:(hb + 1) * 512].rearrange(
                        "(a p) j -> p a j", p=128)
                    d_u = U_sb.ap().rearrange("p (a j) -> p a j", a=DC)
                    vu = B.emit("g", lambda M, o=d_u, i=s_u: M.dma_start(out=o, in_=i),
                                inc=su, amount=16)
                    s_g = gate[:, hb * 512:(hb + 1) * 512].rearrange(
                        "(a p) j -> p a j", p=128)
                    d_g = G_sb.ap().rearrange("p (a j) -> p a j", a=DC)
                    vg = B.emit("g", lambda M, o=d_g, i=s_g: M.dma_start(out=o, in_=i),
                                inc=sg2, amount=16)
                    return (vu, vg)

                def load_d(e, hb):
                    _, _, down = wsrc(e)
                    s_d = down[hb * 512:(hb + 1) * 512, :].rearrange(
                        "(a p) j -> p a j", p=128)
                    d_d = D_sb.ap().rearrange("p (a j) -> p a j", a=4)
                    return B.emit("g", lambda M, o=d_d, i=s_d: M.dma_start(out=o, in_=i),
                                  inc=sd2, amount=16)

                order = [Er] + list(range(Er))
                B.wait("g", sv, v_gate)
                ug_ready = load_ug(order[0], 0)
                d_ready = load_d(order[0], 0)

                v_ln, v_rstd = emit_ln(hid_sb, 0, hid_sb, 3 * T)
                v_mtok = emit_router(v_rstd, logits_sb, mtok_sb)

                bwd_prev0, bwd_prev1 = B.sem["sbg0"].n, B.sem["sbg1"].n
                B.wait("g", B.sem["sbg0"], bwd_prev0)
                B.wait("g", B.sem["sbg1"], bwd_prev1)

                mtok3 = mtok_sb.ap().rearrange("p (t e) -> p t e", e=E)
                vq(lambda M: M.memset(orow_f[:, :], 0.0))
                vq(lambda M: M.memset(selm[:, :], 0.0))
                B.wait("t", sconst, c_tot)
                B.wait("t", sv, v_mtok)
                for e in range(Er):
                    vm = vq(lambda M, e=e: M.tensor_copy(
                        mpack[:, :], mtok3[:, :, e]))
                    B.wait("t", sv, vm)
                    B.wait_slot("t", "B")
                    B.emit("t", lambda M: M.matmul(
                        PS_B[:, 0:TC], ltri_sb[:, :], mpack[:, :],
                        start=True, stop=True))
                    t1 = B.emit("t", lambda M: M.matmul(
                        PS_B[0:TC, 64:65], mpack[:, :], onescol[:, 0:1],
                        start=True, stop=True), inc=st)
                    B.wait("v", st, t1)
                    vq(lambda M: M.tensor_copy(pfx_sb[:, 0:TC], PS_B[:, 0:TC]))
                    v2 = vq(lambda M: M.tensor_copy(psc[0:TC, 0:1],
                                                    PS_B[0:TC, 64:65]))
                    B.wait("t", sv, v2)
                    t2 = B.emit("t", lambda M: M.matmul(
                        PS_B[0:TC, 96:97], ltriT_sb[:, :], psc[0:TC, 0:1],
                        start=True, stop=True), inc=st)
                    B.wait("v", st, t2)
                    v3 = vq(lambda M: M.tensor_copy(psc[0:TC, 1:2],
                                                    PS_B[0:TC, 96:97]))
                    B.wait("t", sv, v3)
                    t3 = B.emit("t", lambda M: M.matmul(
                        PS_B[0:1, 128:128 + TC], psc[0:TC, 1:2], iTf_sb[:, :],
                        start=True, stop=True), inc=st)
                    B.wait("v", st, t3)
                    v4 = vq(lambda M: M.tensor_copy(
                        pfx_sb[0:1, TC:2 * TC], PS_B[0:1, 128:128 + TC]))
                    B.wait("t", sv, v4)
                    t4 = B.emit("t", lambda M: M.matmul(
                        PS_B[:, 160:160 + TC], onesf[0:1, :],
                        pfx_sb[0:1, TC:2 * TC], start=True, stop=True), inc=st)
                    B.wait("v", st, t4)
                    v5 = vq(lambda M: M.tensor_copy(
                        pfx_sb[:, TC:2 * TC], PS_B[:, 160:160 + TC]))
                    B.consumed("B", sv, v5)
                    vq(lambda M, e=e: M.scalar_tensor_tensor(
                        tmp_sb[:, :], pfx_sb[:, 0:TC], float(e * C),
                        pfx_sb[:, TC:2 * TC], ALU.add, ALU.add))
                    vq(lambda M: M.tensor_mul(tmp_sb[:, :], tmp_sb[:, :],
                                              mpack[:, :]))
                    vq(lambda M: M.tensor_add(orow_f[:, :], orow_f[:, :],
                                              tmp_sb[:, :]))
                    vq(lambda M: M.tensor_add(selm[:, :], selm[:, :],
                                              mpack[:, :]))
                vq(lambda M: M.tensor_scalar(selm[:, :], selm[:, :], -1.0, 1.0,
                                             ALU.mult, ALU.add))
                vq(lambda M: M.scalar_tensor_tensor(
                    tmp_sb[:, :], selm[:, :], TRASH, orow_f[:, :],
                    ALU.mult, ALU.add))
                v_oi = vq(lambda M: M.tensor_copy(orow_i[:, :], tmp_sb[:, :]))
                vq(lambda M: M.scalar_tensor_tensor(
                    tmp_sb[:, :], selm[:, :], ZROW, orow_f[:, :],
                    ALU.mult, ALU.add))
                v_bi = vq(lambda M: M.tensor_copy(brow_i[:, :], tmp_sb[:, :]))

                six0 = B.sem["six0"]
                scat_vals = []
                for tc in range(TC):
                    par = tc % 2
                    B.wait_slot("t", "W%d" % par)
                    t_tr = None
                    for dc in range(DC):
                        t_tr = B.emit("t", lambda M, par=par, dc=dc, tc=tc:
                                      M.matmul(
                            PS_W[:, par * 1024 + dc * 128: par * 1024 + (dc + 1) * 128],
                            xlns(dc, tc * 128, (tc + 1) * 128), i128_sb[:, :],
                            start=True, stop=True),
                            inc=(st if dc == DC - 1 else None))
                    B.wait("v", st, t_tr)
                    if tc >= 2:
                        B.wait("v", six0, scat_vals[tc - 2])
                    v_xt = vq(lambda M, par=par: M.tensor_copy(
                        xtt[:, par * D:(par + 1) * D],
                        PS_W[:, par * 1024: par * 1024 + D]))
                    B.consumed("W%d" % par, sv, v_xt)
                    B.wait("g", sv, max(v_xt, v_oi))
                    if scat_vals:
                        B.wait("g", six0, scat_vals[-1])
                    scat_vals.append(B.emit("g", lambda M, par=par, tc=tc:
                                            M.indirect_dma_start(
                        dslots[:, :],
                        bass.IndirectOffsetOnAxis(orow_i[:, tc:tc + 1], 0),
                        xtt[:, par * D:(par + 1) * D], None),
                        inc=six0, amount=16))
                sc_tot = six0.n

                t_xg_done = 0
                srw_prev = B.sem["srw"].n
                sxg_prev = B.sem["sxg"].n

                for ei, e in enumerate(order):
                    if e == Er:
                        B.wait("t", sv, v_ln)
                        for hb in range(HB):
                            B.wait("t", su, ug_ready[0])
                            B.wait("t", sg2, ug_ready[1])
                            tiles = [(h, f) for h in range(4) for f in range(FT)]
                            for i, (hc, ft) in enumerate(tiles):
                                kU = ["A", "C"][i % 2]
                                kG = ["B", "D"][i % 2]
                                psU, psG = PS[kU], PS[kG]
                                sl = i % 2
                                B.wait_slot("t", kU)
                                B.wait_slot("t", kG)
                                for dc in range(DC):
                                    B.emit("t", lambda M, psU=psU, dc=dc, hc=hc, ft=ft:
                                           M.matmul(
                                        psU[:, 0:512],
                                        U_sb[:, dc * 512 + hc * 128: dc * 512 + (hc + 1) * 128],
                                        xlns(dc, ft * 512, (ft + 1) * 512),
                                        start=(dc == 0), stop=(dc == DC - 1)))
                                for dc in range(DC):
                                    tA = B.emit("t", lambda M, psG=psG, dc=dc, hc=hc, ft=ft:
                                                M.matmul(
                                        psG[:, 0:512],
                                        G_sb[:, dc * 512 + hc * 128: dc * 512 + (hc + 1) * 128],
                                        xlns(dc, ft * 512, (ft + 1) * 512),
                                        start=(dc == 0), stop=(dc == DC - 1)),
                                        inc=(st if dc == DC - 1 else None))
                                B.wait("s", st, tA)
                                sS = B.emit("s", lambda M, psG=psG, sl=sl: M.activation(
                                    silu_sb[:, sl * 512:(sl + 1) * 512], psG[:, 0:512],
                                    AF.Silu), inc=ss)
                                B.wait("v", ss, sS)
                                vv = B.emit("v", lambda M, psU=psU, sl=sl, hc=hc, ft=ft:
                                            M.tensor_mul(
                                    hid_sb[:, hc * T + ft * 512: hc * T + (ft + 1) * 512],
                                    psU[:, 0:512],
                                    silu_sb[:, sl * 512:(sl + 1) * 512]), inc=sv)
                                B.consumed(kU, sv, vv)
                                B.consumed(kG, sv, vv)
                                v_hid = vv
                            t_lastA = tA
                            nxt = (e, hb + 1) if hb + 1 < HB else (order[ei + 1], 0)
                            B.wait("g", st, t_lastA)
                            ug_next = load_ug(*nxt)
                            B.wait("t", sd2, d_ready)
                            B.wait("t", sv, v_hid)
                            for dco in range(DC):
                                for half in range(NHALF):
                                    B.wait_slot("t", "W%d" % half)
                                    for hc in range(4):
                                        for q in range(NQ):
                                            c0 = half * WH + q * 512
                                            tB = B.emit("t",
                                                        lambda M, dco=dco, hc=hc, c0=c0:
                                                        M.matmul(
                                                PS_W[:, c0:c0 + 512],
                                                D_sb[:, hc * D + dco * 128: hc * D + (dco + 1) * 128],
                                                hid_sb[:, hc * T + c0: hc * T + c0 + 512],
                                                start=(hc == 0), stop=(hc == 3)),
                                                inc=(st if (hc == 3 and q == NQ - 1) else None))
                                    B.wait("v", st, tB)
                                    vv = B.emit("v", lambda M, dco=dco, half=half:
                                                M.scalar_tensor_tensor(
                                        xs(dco, half * WH, (half + 1) * WH),
                                        PS_W[:, half * WH:(half + 1) * WH], 0.5,
                                        xs(dco, half * WH, (half + 1) * WH),
                                        ALU.mult, ALU.add), inc=sv)
                                    B.consumed("W%d" % half, sv, vv)
                            B.wait("g", st, st.n)
                            d_ready = load_d(*nxt)
                            ug_ready = ug_next
                        continue

                    B.wait("g", six0, sc_tot)
                    B.wait("g", st, t_xg_done)
                    B.wait("g", B.sem["sxg"], sxg_prev)
                    sxg_prev = B.emit("g", lambda M, e=e: M.dma_start(
                        out=xg_tok.ap().rearrange("p (a j) -> p a j", a=CB),
                        in_=dslots[e * C:(e + 1) * C, :].rearrange(
                            "(a p) j -> p a j", p=128)),
                        inc=B.sem["sxg"], amount=16)
                    B.wait("t", B.sem["sxg"], sxg_prev)
                    for a in range(CB):
                        par = a % 2
                        B.wait_slot("t", "W%d" % par)
                        for dc in range(DC):
                            t_xg_done = B.emit("t", lambda M, a=a, dc=dc, par=par:
                                               M.matmul(
                                PS_W[:, par * 1024 + dc * 128: par * 1024 + (dc + 1) * 128],
                                xg_tok[:, a * D + dc * 128: a * D + (dc + 1) * 128],
                                i128_sb[:, :], start=True, stop=True),
                                inc=(st if dc == DC - 1 else None))
                        B.wait("v", st, t_xg_done)
                        dst3 = xg_fm.ap().rearrange(
                            "p (b j) -> p b j", j=C)[:, :, a * 128:(a + 1) * 128]
                        src3 = PS_W.ap().rearrange(
                            "p (b j) -> p b j", j=128)[:, par * 8: par * 8 + DC, :]
                        v_xgf = vq(lambda M, dst3=dst3, src3=src3: M.tensor_copy(
                            dst3, src3))
                        B.consumed("W%d" % par, sv, v_xgf)
                    for hb in range(HB):
                        B.wait("t", su, ug_ready[0])
                        B.wait("t", sg2, ug_ready[1])
                        if hb == 0:
                            B.wait("t", sv, v_xgf)
                        for hc in range(4):
                            kU = ["A", "C"][hc % 2]
                            kG = ["B", "D"][hc % 2]
                            psU, psG = PS[kU], PS[kG]
                            sl = hc % 2
                            B.wait_slot("t", kU)
                            B.wait_slot("t", kG)
                            for dc in range(DC):
                                B.emit("t", lambda M, psU=psU, dc=dc, hc=hc: M.matmul(
                                    psU[:, 0:C],
                                    U_sb[:, dc * 512 + hc * 128: dc * 512 + (hc + 1) * 128],
                                    xg_fm[:, dc * C:(dc + 1) * C],
                                    start=(dc == 0), stop=(dc == DC - 1)))
                            for dc in range(DC):
                                tA = B.emit("t", lambda M, psG=psG, dc=dc, hc=hc:
                                            M.matmul(
                                    psG[:, 0:C],
                                    G_sb[:, dc * 512 + hc * 128: dc * 512 + (hc + 1) * 128],
                                    xg_fm[:, dc * C:(dc + 1) * C],
                                    start=(dc == 0), stop=(dc == DC - 1)),
                                    inc=(st if dc == DC - 1 else None))
                            B.wait("s", st, tA)
                            sS = B.emit("s", lambda M, psG=psG, sl=sl: M.activation(
                                silu_sb[:, sl * 512: sl * 512 + C], psG[:, 0:C],
                                AF.Silu), inc=ss)
                            B.wait("v", ss, sS)
                            vv = B.emit("v", lambda M, psU=psU, sl=sl, hc=hc:
                                        M.tensor_mul(
                                hid_sb[:, hc * C:(hc + 1) * C], psU[:, 0:C],
                                silu_sb[:, sl * 512: sl * 512 + C]), inc=sv)
                            B.consumed(kU, sv, vv)
                            B.consumed(kG, sv, vv)
                            v_hid = vv
                        t_lastA = tA
                        nxt = (e, hb + 1) if hb + 1 < HB else (
                            (order[ei + 1], 0) if ei + 1 < len(order) else None)
                        if nxt is not None:
                            B.wait("g", st, t_lastA)
                            ug_next = load_ug(*nxt)
                        B.wait("t", sd2, d_ready)
                        B.wait("t", sv, v_hid)
                        if hb == 0:
                            B.wait("v", B.sem["srw"], srw_prev)
                        regions = [(s_, d_) for s_ in range(CB)
                                   for d_ in range(ND)]
                        for r, (sc_, dh) in enumerate(regions):
                            key = "ABCD"[r % 4]
                            ps = PS[key]
                            B.wait_slot("t", key)
                            for hc in range(4):
                                tB = B.emit("t",
                                            lambda M, ps=ps, sc_=sc_, dh=dh, hc=hc:
                                            M.matmul(
                                    ps[:, 0:512],
                                    hid_sb[:, hc * C + sc_ * 128: hc * C + sc_ * 128 + 128],
                                    D_sb[:, hc * D + dh * 512: hc * D + dh * 512 + 512],
                                    start=(hc == 0), stop=(hc == 3)),
                                    inc=(st if hc == 3 else None))
                            B.wait("v", st, tB)
                            off = sc_ * D + dh * 512
                            if hb == 0:
                                vv = B.emit("v", lambda M, ps=ps, off=off:
                                            M.tensor_copy(
                                    slotout[:, off:off + 512], ps[:, 0:512]),
                                    inc=sv)
                            else:
                                vv = B.emit("v", lambda M, ps=ps, off=off:
                                            M.tensor_add(
                                    slotout[:, off:off + 512],
                                    slotout[:, off:off + 512], ps[:, 0:512]),
                                    inc=sv)
                            B.consumed(key, sv, vv)
                            v_sout = vv
                        if nxt is not None:
                            B.wait("g", st, st.n)
                            d_ready = load_d(*nxt)
                            ug_ready = ug_next
                    B.wait("g", sv, v_sout)
                    srw_prev = B.emit("g", lambda M, e=e: M.dma_start(
                        out=dslots[e * C:(e + 1) * C, :].rearrange(
                            "(a p) j -> p a j", p=128),
                        in_=slotout.ap().rearrange("p (a j) -> p a j", a=CB)),
                        inc=B.sem["srw"], amount=16)

                sbg = [B.sem["sbg0"], B.sem["sbg1"]]
                sbg_prev = [sbg[0].n, sbg[1].n]
                t_bw = 0
                v_out = 0
                for tc in range(TC):
                    par = tc % 2
                    B.wait("g", B.sem["srw"], B.sem["srw"].n)
                    B.wait("g", sv, v_bi)
                    B.wait("g", st, t_bw)
                    B.wait("g", sbg[par], sbg_prev[par])
                    sbg_prev[par] = B.emit("g", lambda M, par=par, tc=tc:
                                           M.indirect_dma_start(
                        rtt[:, par * D:(par + 1) * D], None,
                        dslots[:, :],
                        bass.IndirectOffsetOnAxis(brow_i[:, tc:tc + 1], 0)),
                        inc=sbg[par], amount=16)
                    B.wait("v", sbg[par], sbg_prev[par])
                    v_sc = vq(lambda M, par=par, tc=tc: M.tensor_scalar(
                        rtt[:, par * D:(par + 1) * D],
                        rtt[:, par * D:(par + 1) * D],
                        cols[:, 3 * TC + tc: 3 * TC + tc + 1], None, ALU.mult))
                    B.wait("t", sv, v_sc)
                    B.wait_slot("t", "W%d" % par)
                    for dc in range(DC):
                        t_bw = B.emit("t", lambda M, par=par, dc=dc: M.matmul(
                            PS_W[:, par * 1024 + dc * 128: par * 1024 + (dc + 1) * 128],
                            rtt[:, par * D + dc * 128: par * D + (dc + 1) * 128],
                            i128_sb[:, :], start=True, stop=True),
                            inc=(st if dc == DC - 1 else None))
                    B.wait("v", st, t_bw)
                    xdst = x_sb.ap().rearrange("p (b j) -> p b j", j=T)[
                        :, :, tc * 128:(tc + 1) * 128]
                    psrc = PS_W.ap().rearrange("p (b j) -> p b j", j=128)[
                        :, par * 8: par * 8 + DC, :]
                    v_out = vq(lambda M, xdst=xdst, psrc=psrc: M.tensor_add(
                        xdst, xdst, psrc))
                    B.consumed("W%d" % par, sv, v_out)
                B.consumed("Wstat", sv, v_out)
                return v_out

        def emit_attn(v_gate):
            with ExitStack() as att:
                ac = att.enter_context
                W0 = ac(nc.sbuf_tensor("W0", [128, DC * D], BF16))
                W1 = ac(nc.sbuf_tensor("W1", [128, DC * D], BF16))
                K_tile = ac(nc.sbuf_tensor("K_tile", [128, 2 * D], BF16))
                V_tile = ac(nc.sbuf_tensor("V_tile", [128, 2 * D], BF16))
                ascr = ac(nc.sbuf_tensor("ascr", [128, DC * T], BF16))
                recip_sb = ac(nc.sbuf_tensor("recip_sb", [NH, T], BF16))
                cc_sb = ac(nc.sbuf_tensor("cc_sb", [128, NP * 64 + NP], F32))
                kvbf_sb = ac(nc.sbuf_tensor("kvbf_sb", [128, NP * 64], BF16))
                zbd_sb = ac(nc.sbuf_tensor("zbd_sb", [128, DC * NH], BF16))
                e16_sb = ac(nc.sbuf_tensor("e16_sb", [NH, NP * 128], BF16))
                bias_at = ac(nc.sbuf_tensor("bias_at", [1, 2 * D], BF16))
                sc1 = ac(nc.sbuf_tensor("sc1a", [128, 512], BF16))
                sc2 = ac(nc.sbuf_tensor("sc2a", [128, 512], BF16))

                def load_w(dw, W_t, sem):
                    s_w = dw.ap().rearrange("(a p) j -> p a j", p=128)
                    d_w = W_t.ap().rearrange("p (a j) -> p a j", a=DC)
                    return B.emit("g", lambda M, o=d_w, i=s_w: M.dma_start(
                        out=o, in_=i), inc=sem, amount=16)

                B.wait("y", sv, v_gate)
                y_e16 = B.emit("y", lambda M: M.dma_start(
                    out=e16_sb[:, :], in_=de16[:, :]), inc=se16, amount=16)
                B.wait("g", sv, v_gate)
                w_k = load_w(dwk, W0, sw)
                w_v = load_w(dwv, W1, B.sem["swb"])
                B.emit("g", lambda M: M.dma_start(
                    out=bias_at[0:1, 0:D], in_=dbk[:, :]), inc=sconsg, amount=16)
                B.emit("g", lambda M: M.dma_start(
                    out=bias_at[0:1, D:2 * D], in_=dbv[:, :]), inc=sconsg, amount=16)
                bias_kv = sconsg.n

                v_ln2, _ = emit_ln(ascr, 0, ascr, 3 * T)
                v_ccz = vq(lambda M: M.memset(cc_sb[:, :], 0.0))

                B.wait("t", sw, w_k)
                B.wait("t", B.sem["swb"], w_v)
                B.wait("t", sconsg, bias_kv)
                B.wait("t", sv, v_ln2)
                B.wait_slot("t", "A")
                B.wait_slot("t", "B")
                B.wait_slot("t", "C")
                B.wait_slot("t", "Wstat")
                t_kvtc = [0] * TC
                for tc in range(TC):
                    half = tc % 2
                    B.wait_slot("t", "WpK")
                    tvk = 0
                    for os in range(NOS):
                        c0 = os * OSW
                        for dc in range(DC):
                            B.emit("t", lambda M, dc=dc, tc=tc, c0=c0: M.matmul(
                                PS_W[:, c0:c0 + OSW],
                                xlns(dc, tc * 128, (tc + 1) * 128),
                                W0[:, dc * D + c0: dc * D + c0 + OSW],
                                start=(dc == 0), stop=False))
                        tvk = B.emit("t", lambda M, c0=c0: M.matmul(
                            PS_W[:, c0:c0 + OSW], onesbf[0:1, 0:128],
                            bias_at[0:1, c0:c0 + OSW],
                            start=False, stop=True),
                            inc=(st if os == NOS - 1 else None))
                    vphi = 0
                    for os in range(NOS):
                        c0 = os * OSW
                        dk = K_tile[:, half * D + c0: half * D + c0 + OSW]
                        B.wait("v", st, tvk)
                        if tc >= 2:
                            B.wait("v", st, t_kvtc[tc - 2])
                        v1 = vq(lambda M, c0=c0: M.tensor_scalar_min(
                            sc1[:, 0:OSW], PS_W[:, c0:c0 + OSW], 0.0))
                        B.wait("s", sv, v1)
                        s1 = B.emit("s", lambda M: M.activation(
                            sc2[:, 0:OSW], sc1[:, 0:OSW], AF.Exp), inc=ss)
                        B.wait("v", ss, s1)
                        vphi = vq(lambda M, c0=c0, dk=dk: M.scalar_tensor_tensor(
                            dk, PS_W[:, c0:c0 + OSW], 0.0,
                            sc2[:, 0:OSW], ALU.max, ALU.add))
                    B.consumed("WpK", sv, vphi)
                    B.wait_slot("t", "WpV")
                    tvv = 0
                    for os in range(NOS):
                        c0 = D + os * OSW
                        for dc in range(DC):
                            B.emit("t", lambda M, dc=dc, tc=tc, c0=c0: M.matmul(
                                PS_W[:, c0:c0 + OSW],
                                xlns(dc, tc * 128, (tc + 1) * 128),
                                W1[:, dc * D + c0 - D: dc * D + c0 - D + OSW],
                                start=(dc == 0), stop=False))
                        tvv = B.emit("t", lambda M, c0=c0: M.matmul(
                            PS_W[:, c0:c0 + OSW], onesbf[0:1, 0:128],
                            bias_at[0:1, c0:c0 + OSW],
                            start=False, stop=True),
                            inc=(st if os == NOS - 1 else None))
                    vcv = 0
                    for os in range(NOS):
                        c0 = D + os * OSW
                        dv = V_tile[:, half * D + os * OSW: half * D + (os + 1) * OSW]
                        B.wait("v", st, tvv)
                        if tc >= 2:
                            B.wait("v", st, t_kvtc[tc - 2])
                        vcv = vq(lambda M, c0=c0, dv=dv: M.tensor_copy(
                            dv, PS_W[:, c0:c0 + OSW]))
                    B.consumed("WpV", sv, vcv)
                    B.wait("t", sv, vcv)
                    if tc > 0:
                        B.wait("t", sv, v_kvadd)
                    for p in range(NP):
                        b0 = half * D + p * 128
                        B.emit("t", lambda M, p=p, b0=b0: M.matmul(
                            PS_A[0:64, p * 64:(p + 1) * 64],
                            K_tile[:, b0: b0 + 64], V_tile[:, b0: b0 + 64],
                            start=True, stop=True))
                        B.emit("t", lambda M, p=p, b0=b0: M.matmul(
                            PS_B[64:128, p * 64:(p + 1) * 64],
                            K_tile[:, b0 + 64: b0 + 128],
                            V_tile[:, b0 + 64: b0 + 128],
                            start=True, stop=True))
                        t_kvtc[tc] = B.emit("t", lambda M, p=p, b0=b0: M.matmul(
                            PS_C[:, p:p + 1],
                            K_tile[:, b0: b0 + 128], onescol[:, 0:1],
                            start=True, stop=True),
                            inc=(st if p == NP - 1 else None))
                    B.wait("v", st, t_kvtc[tc])
                    if tc == 0:
                        B.wait("v", sv, v_ccz)
                    vq(lambda M: M.tensor_add(
                        cc_sb[0:64, 0:NP * 64], cc_sb[0:64, 0:NP * 64],
                        PS_A[0:64, 0:NP * 64]))
                    vq(lambda M: M.tensor_add(
                        cc_sb[64:128, 0:NP * 64], cc_sb[64:128, 0:NP * 64],
                        PS_B[64:128, 0:NP * 64]))
                    v_kvadd = vq(lambda M: M.tensor_add(
                        cc_sb[:, NP * 64: NP * 64 + NP],
                        cc_sb[:, NP * 64: NP * 64 + NP], PS_C[:, 0:NP]))
                t_kv = t_kvtc[TC - 1]
                v_cc = v_kvadd

                B.wait("g", st, t_kv)
                w_q = load_w(dwq, W0, sw)
                w_o = load_w(dwo, W1, B.sem["swb"])
                B.emit("g", lambda M: M.dma_start(
                    out=bias_at[0:1, 0:D], in_=dbq[:, :]), inc=sconsg, amount=16)
                B.emit("g", lambda M: M.dma_start(
                    out=bias_at[0:1, D:2 * D], in_=dbo[:, :]), inc=sconsg, amount=16)
                bias_qo = sconsg.n

                B.consumed("A", sv, v_cc)
                B.consumed("B", sv, v_cc)
                B.consumed("C", sv, v_cc)
                B.wait("y", sv, v_cc)
                y_ccin = B.emit("y", lambda M: M.dma_start(
                    out=dccin[:, :], in_=cc_sb[:, :]), inc=sci, amount=16)
                B.wait("g", sci, y_ccin)
                g_cc = B.emit("g", lambda M: M.collective_compute(
                    "AllReduce", ALU.add,
                    replica_groups=[list(range(0, cfg.group)),
                                    list(range(cfg.group, 2 * cfg.group))],
                    ins=[dccin.ap().opt()], outs=[dccout.ap().opt()]), inc=scc)
                B.wait("g", scc, g_cc)
                g_ccr = B.emit("g", lambda M: M.dma_start(
                    out=cc_sb[:, :], in_=dccout[:, :]), inc=sgd, amount=16)

                phiQ = ascr
                B.wait("t", sw, w_q)
                B.wait("t", sconsg, bias_qo)
                t_q = 0
                v_q = 0
                for i, (oc, ft) in enumerate(
                        [(o, f) for o in range(DC) for f in range(FT)]):
                    key = ["C", "D"][i % 2]
                    ps = PS[key]
                    B.wait_slot("t", key)
                    for dc in range(DC):
                        B.emit("t", lambda M, ps=ps, dc=dc, oc=oc, ft=ft: M.matmul(
                            ps[:, 0:512],
                            W0[:, dc * D + oc * 128: dc * D + (oc + 1) * 128],
                            xlns(dc, ft * 512, (ft + 1) * 512),
                            start=(dc == 0), stop=False))
                    t_q = B.emit("t", lambda M, ps=ps, oc=oc: M.matmul(
                        ps[:, 0:512], bias_at[0:1, oc * 128:(oc + 1) * 128],
                        onesbf[0:1, :], start=False, stop=True), inc=st)
                    B.wait("v", st, t_q)
                    v1 = vq(lambda M, ps=ps: M.tensor_scalar_min(
                        sc1[:, :], ps[:, 0:512], 0.0))
                    B.wait("s", sv, v1)
                    s1 = B.emit("s", lambda M: M.activation(
                        sc2[:, :], sc1[:, :], AF.Exp), inc=ss)
                    B.wait("v", ss, s1)
                    v_q = vq(lambda M, ps=ps, oc=oc, ft=ft: M.scalar_tensor_tensor(
                        phiQ[:, oc * T + ft * 512: oc * T + (ft + 1) * 512],
                        ps[:, 0:512], 0.0, sc2[:, :], ALU.max, ALU.add))
                    B.consumed(key, sv, v_q)

                B.wait("v", sgd, g_ccr)
                vq(lambda M: M.tensor_copy(
                    kvbf_sb[:, :], cc_sb[:, 0:NP * 64]))
                vq(lambda M: M.memset(zbd_sb[:, :], 0.0))
                v_kvz = 0
                for dc in range(DC):
                    vq(lambda M, dc=dc: M.tensor_copy(
                        zbd_sb[0:64, dc * NH + 2 * dc: dc * NH + 2 * dc + 1],
                        cc_sb[0:64, NP * 64 + dc: NP * 64 + dc + 1]))
                    v_kvz = vq(lambda M, dc=dc: M.tensor_copy(
                        zbd_sb[64:128, dc * NH + 2 * dc + 1: dc * NH + 2 * dc + 2],
                        cc_sb[64:128, NP * 64 + dc: NP * 64 + dc + 1]))

                B.wait("t", sv, max(v_kvz, v_q))
                B.wait_slot("t", "Wstat")
                t_nrm = 0
                for qq in range(FT):
                    for dc in range(DC):
                        t_nrm = B.emit("t", lambda M, qq=qq, dc=dc: M.matmul(
                            PS_W[0:NH, qq * 512:(qq + 1) * 512],
                            zbd_sb[:, dc * NH:(dc + 1) * NH],
                            phiQ[:, dc * T + qq * 512: dc * T + (qq + 1) * 512],
                            start=(dc == 0), stop=(dc == DC - 1)),
                            inc=(st if (qq == FT - 1 and dc == DC - 1) else None))
                B.wait("v", st, t_nrm)
                vq(lambda M: M.tensor_scalar(
                    PS_W[32:32 + NH, 0:T], PS_W[0:NH, 0:T],
                    float(cfg.attn_eps), None, ALU.add))

                def _recip_lp(M):
                    with nc.allow_low_precision(reason="bf16 attn norm recip"):
                        return M.reciprocal(recip_sb[:, :], PS_W[32:32 + NH, 0:T])
                v_rec = vq(_recip_lp)
                B.consumed("Wstat", sv, v_rec)

                B.wait("t", sv, v_rec)
                B.wait("t", se16, y_e16)
                okeys = [("A", "B"), ("C", "D")]
                v_fm = 0
                for i, (p, ft) in enumerate(
                        [(p, f) for p in range(NP) for f in range(FT)]):
                    kO, kR = okeys[i % 2]
                    psO, psR = PS[kO], PS[kR]
                    B.wait_slot("t", kO)
                    B.wait_slot("t", kR)
                    B.emit("t", lambda M, psO=psO, p=p, ft=ft: M.matmul(
                        psO[0:64, 0:512], kvbf_sb[0:64, p * 64:(p + 1) * 64],
                        phiQ[0:64, p * T + ft * 512: p * T + (ft + 1) * 512],
                        start=True, stop=True))
                    B.emit("t", lambda M, psO=psO, p=p, ft=ft: M.matmul(
                        psO[64:128, 0:512], kvbf_sb[64:128, p * 64:(p + 1) * 64],
                        phiQ[64:128, p * T + ft * 512: p * T + (ft + 1) * 512],
                        start=True, stop=True))
                    tv = B.emit("t", lambda M, psR=psR, p=p, ft=ft: M.matmul(
                        psR[:, 0:512], e16_sb[:, p * 128:(p + 1) * 128],
                        recip_sb[:, ft * 512:(ft + 1) * 512],
                        start=True, stop=True), inc=st)
                    B.wait("v", st, tv)
                    vq(lambda M, psR=psR: M.tensor_copy(
                        sc1[:, :], psR[:, 0:512]))
                    v_fm = vq(lambda M, psO=psO, p=p, ft=ft: M.tensor_mul(
                        xlns(p, ft * 512, (ft + 1) * 512), psO[:, 0:512],
                        sc1[:, :]))
                    B.consumed(kO, sv, v_fm)
                    B.consumed(kR, sv, v_fm)

                B.wait("t", B.sem["swb"], w_o)
                B.wait("t", sv, v_fm)
                v_out = 0
                for i, (oc, ft) in enumerate(
                        [(o, f) for o in range(DC) for f in range(FT)]):
                    key = ["C", "D"][i % 2]
                    ps = PS[key]
                    B.wait_slot("t", key)
                    for dc in range(DC):
                        B.emit("t", lambda M, ps=ps, dc=dc, oc=oc, ft=ft: M.matmul(
                            ps[:, 0:512],
                            W1[:, dc * D + oc * 128: dc * D + (oc + 1) * 128],
                            xlns(dc, ft * 512, (ft + 1) * 512),
                            start=(dc == 0), stop=False))
                    tv = B.emit("t", lambda M, ps=ps, oc=oc: M.matmul(
                        ps[:, 0:512], bias_at[0:1, D + oc * 128: D + (oc + 1) * 128],
                        onesbf[0:1, :], start=False, stop=True), inc=st)
                    B.wait("v", st, tv)
                    v_out = B.emit("v", lambda M, ps=ps, oc=oc, ft=ft: M.tensor_add(
                        xs(oc, ft * 512, (ft + 1) * 512),
                        xs(oc, ft * 512, (ft + 1) * 512), ps[:, 0:512]), inc=sv)
                    B.consumed(key, sv, v_out)
                return v_out

        B.wait("s", sx, y_xload)
        B.wait("t", sx, y_xload)
        v1 = emit_ffn(v_sb8, "_f1")
        B.wait("s", sv, v1)
        v2 = emit_attn(v1)
        B.wait("s", sv, v2)
        v3 = emit_ffn(v2, "_f2")

        B.wait("y", sv, v3)
        osrc = x_sb.ap().rearrange("p (a j) -> p a j", a=DC)
        odst = dy.ap().rearrange("(a p) j -> p a j", p=128)
        y_out = B.emit("y", lambda M, o=odst, i=osrc: M.dma_start(out=o, in_=i),
                       inc=syo, amount=16)
        B.wait("y", syo, y_out)

        B.finalize()
    return nc


def _host_constants(cfg):
    NH, NP, TC = cfg.NH, cfg.NP, cfg.TC
    i128 = np.eye(128, dtype=np.float32).astype(BF)
    e16 = np.zeros((NH, NP * 128), dtype=BF)
    for p in range(NP):
        e16[2 * p, p * 128: p * 128 + 64] = 1.0
        e16[2 * p + 1, p * 128 + 64: (p + 1) * 128] = 1.0
    ltri = np.triu(np.ones((128, 128), np.float32), 1).astype(BF)
    ltriT = np.triu(np.ones((TC, TC), np.float32), 1)
    iTf = np.eye(TC, dtype=np.float32)
    return i128, e16, ltri, ltriT, iTf


def stage_inputs(cfg, x, Wq, bq, Wk, bk, Wv, bv, Wo, bo, rw, upw, gatew, downw,
                 sup, sgate, sdown, n_cores=8):
    f32 = np.float32
    x = np.asarray(x, f32)
    Bb = x.shape[0]
    per_batch = n_cores // Bb
    i128, e16, ltri, ltriT, iTf = _host_constants(cfg)
    common = {
        "wq": np.ascontiguousarray(np.asarray(Wq, f32).T),
        "wk": np.ascontiguousarray(np.asarray(Wk, f32).T),
        "wv": np.ascontiguousarray(np.asarray(Wv, f32).T),
        "wo": np.ascontiguousarray(np.asarray(Wo, f32).T),
        "bq": np.asarray(bq, f32).reshape(1, -1),
        "bk": np.asarray(bk, f32).reshape(1, -1),
        "bv": np.asarray(bv, f32).reshape(1, -1),
        "bo": np.asarray(bo, f32).reshape(1, -1),
        "rw": np.ascontiguousarray(np.asarray(rw, f32)),
        "upw": np.ascontiguousarray(np.asarray(upw, f32)),
        "gatew": np.ascontiguousarray(np.asarray(gatew, f32)),
        "downw": np.ascontiguousarray(np.asarray(downw, f32)),
        "sup": np.ascontiguousarray(np.asarray(sup, f32)),
        "sgate": np.ascontiguousarray(np.asarray(sgate, f32)),
        "sdown": np.ascontiguousarray(np.asarray(sdown, f32)),
        "i128": i128, "e16": e16, "ltri": ltri, "ltriT": ltriT, "iTf": iTf,
        "slots": np.zeros((cfg.SR, cfg.D), dtype=BF),
    }
    in_maps = []
    for c in range(n_cores):
        b = c // per_batch
        t0 = (c % per_batch) * cfg.T
        m = dict(common)
        m["x"] = np.ascontiguousarray(x[b, t0:t0 + cfg.T, :].T)
        in_maps.append(m)
    return in_maps


def gather_out(cfg, results, Bb, S, D, n_cores=8):
    per_batch = n_cores // Bb
    out = np.empty((Bb, S, D), dtype=np.float32)
    for c in range(n_cores):
        b = c // per_batch
        t0 = (c % per_batch) * cfg.T
        out[b, t0:t0 + cfg.T, :] = results[c]["y"].T
    return out


def kernel(L0_attn_x, L0_attn_Wq, L0_attn_bq, L0_attn_Wk, L0_attn_bk,
           L0_attn_Wv, L0_attn_bv, L0_attn_Wo, L0_attn_bo,
           L0_ffn_router_w, L0_ffn_up_w, L0_ffn_gate_w, L0_ffn_down_w,
           L0_ffn_shared_up_w, L0_ffn_shared_gate_w, L0_ffn_shared_down_w):
    x = np.asarray(L0_attn_x, dtype=np.float32)
    Bb, S, D = x.shape
    upw = np.asarray(L0_ffn_up_w)
    cfg = Cfg(T=(Bb * S) // 8, D=D, HF=upw.shape[2], Er=upw.shape[0],
              E=np.asarray(L0_ffn_router_w).shape[1], group=8 // Bb)
    in_maps = stage_inputs(cfg, x, L0_attn_Wq, L0_attn_bq, L0_attn_Wk,
                           L0_attn_bk, L0_attn_Wv, L0_attn_bv, L0_attn_Wo,
                           L0_attn_bo, L0_ffn_router_w, L0_ffn_up_w,
                           L0_ffn_gate_w, L0_ffn_down_w, L0_ffn_shared_up_w,
                           L0_ffn_shared_gate_w, L0_ffn_shared_down_w)
    nc = build_nc(cfg)
    res = run_bass_kernel_spmd(nc, in_maps, core_ids=list(range(8)))
    return gather_out(cfg, res.results, Bb, S, D)


# version 07
# speedup vs baseline: 2.2640x; 1.1170x over previous
from contextlib import ExitStack

import numpy as np
import ml_dtypes

import concourse.bass as bass
import concourse.mybir as mybir
from concourse.bass_utils import run_bass_kernel_spmd

F32 = mybir.dt.float32
BF16 = mybir.dt.bfloat16
I32 = mybir.dt.int32
AF = mybir.ActivationFunctionType
ALU = mybir.AluOpType
BF = ml_dtypes.bfloat16


class Cfg:
    def __init__(self, T=2048, D=1024, HF=4096, Er=7, E=8, group=4,
                 ln_eps=1e-5, attn_eps=1e-6):
        assert T % 512 == 0 and D % 128 == 0 and HF % 512 == 0
        self.T, self.D, self.HF, self.Er, self.E = T, D, HF, Er, E
        self.DC = D // 128
        self.TC = T // 128
        self.FT = T // 512
        self.HB = HF // 512
        self.NH = D // 64
        self.NP = D // 128
        self.group = group
        self.ln_eps, self.attn_eps = ln_eps, attn_eps
        self.C = T // 4
        self.SR = ((Er * self.C + 2 + 127) // 128) * 128


class Sem:
    def __init__(self, nc, name):
        self.h = nc.alloc_semaphore(name)
        self.n = 0

    def bump(self, amount=1):
        self.n += amount
        return self.n


class Builder:
    ENGS = ("t", "v", "s", "g", "y")

    def __init__(self, nc):
        self.nc = nc
        self.q = {e: [] for e in self.ENGS}
        self.sem = {k: Sem(nc, k) for k in
                    ("st", "sv", "ss", "scc", "sgd", "syd", "su", "sg2", "sd2",
                     "sw", "swb", "sx", "sconst", "sconsg", "se16", "sci", "syo",
                     "szf", "six0", "six1", "sxg", "srw", "sbg0", "sbg1")}
        self.cons = {}

    def wait(self, eng, sem, val):
        if val is None or val <= 0:
            return
        h = sem.h
        self.q[eng].append(lambda M: M.wait_ge(h, val))

    def wait_slot(self, eng, key):
        c = self.cons.get(key)
        if c is not None:
            self.wait(eng, c[0], c[1])

    def consumed(self, key, sem, val):
        self.cons[key] = (sem, val)

    def emit(self, eng, fn, inc=None, amount=1):
        if inc is None:
            self.q[eng].append(fn)
            return None
        val = inc.bump(amount)
        h = inc.h

        def L(M, fn=fn, h=h, amount=amount):
            fn(M).then_inc(h, amount)
        self.q[eng].append(L)
        return val

    def finalize(self):
        with self.nc.Block() as block:
            @block.tensor
            def _(E):
                for f in self.q["t"]:
                    f(E)

            @block.vector
            def _(E):
                for f in self.q["v"]:
                    f(E)

            @block.scalar
            def _(E):
                for f in self.q["s"]:
                    f(E)

            @block.gpsimd
            def _(E):
                for f in self.q["g"]:
                    f(E)

            @block.sync
            def _(E):
                for f in self.q["y"]:
                    f(E)


def build_nc(cfg: Cfg):
    nc = bass.Bass("TRN2", target_bir_lowering=False)
    T, D, HF, Er, E = cfg.T, cfg.D, cfg.HF, cfg.Er, cfg.E
    DC, TC, FT, HB, NH, NP = cfg.DC, cfg.TC, cfg.FT, cfg.HB, cfg.NH, cfg.NP
    WH = min(1024, T)
    NHALF = T // WH
    NQ = WH // 512
    OSW = min(512, D)
    NOS = D // OSW

    dx = nc.dram_tensor("x", [D, T], F32, kind="ExternalInput")
    dy = nc.dram_tensor("y", [D, T], F32, kind="ExternalOutput")
    dwq = nc.dram_tensor("wq", [D, D], F32, kind="ExternalInput")
    dwk = nc.dram_tensor("wk", [D, D], F32, kind="ExternalInput")
    dwv = nc.dram_tensor("wv", [D, D], F32, kind="ExternalInput")
    dwo = nc.dram_tensor("wo", [D, D], F32, kind="ExternalInput")
    dbq = nc.dram_tensor("bq", [1, D], F32, kind="ExternalInput")
    dbk = nc.dram_tensor("bk", [1, D], F32, kind="ExternalInput")
    dbv = nc.dram_tensor("bv", [1, D], F32, kind="ExternalInput")
    dbo = nc.dram_tensor("bo", [1, D], F32, kind="ExternalInput")
    drw = nc.dram_tensor("rw", [D, E], F32, kind="ExternalInput")
    dup = nc.dram_tensor("upw", [Er, D, HF], F32, kind="ExternalInput")
    dgate = nc.dram_tensor("gatew", [Er, D, HF], F32, kind="ExternalInput")
    ddown = nc.dram_tensor("downw", [Er, HF, D], F32, kind="ExternalInput")
    dsup = nc.dram_tensor("sup", [D, HF], F32, kind="ExternalInput")
    dsgate = nc.dram_tensor("sgate", [D, HF], F32, kind="ExternalInput")
    dsdown = nc.dram_tensor("sdown", [HF, D], F32, kind="ExternalInput")
    dltri = nc.dram_tensor("ltri", [128, 128], BF16, kind="ExternalInput")
    dltriT = nc.dram_tensor("ltriT", [TC, TC], F32, kind="ExternalInput")
    diTf = nc.dram_tensor("iTf", [TC, TC], F32, kind="ExternalInput")
    dslots = nc.dram_tensor("slots", [cfg.SR, D], BF16, kind="ExternalInput")
    di128 = nc.dram_tensor("i128", [128, 128], BF16, kind="ExternalInput")
    de16 = nc.dram_tensor("e16", [NH, NP * 128], BF16, kind="ExternalInput")
    dccin = nc.dram_tensor("cc_in", [128, NP * 64 + NP], F32)
    dccout = nc.dram_tensor("cc_out", [128, NP * 64 + NP], F32)

    with ExitStack() as top:
        ec = top.enter_context
        x_sb = ec(nc.sbuf_tensor("x_sb", [128, DC * T], F32))
        xln_sb = ec(nc.sbuf_tensor("xln_sb", [128, DC * T], BF16))
        rows = ec(nc.sbuf_tensor("rows", [128, T], F32))
        rstd_row = ec(nc.sbuf_tensor("rstd_row", [1, T], BF16))
        onesf = ec(nc.sbuf_tensor("onesf", [1, 128], F32))
        onesbf = ec(nc.sbuf_tensor("onesbf", [1, 512], BF16))
        onescol = ec(nc.sbuf_tensor("onescol", [128, 1], BF16))
        statw = ec(nc.sbuf_tensor("statw", [128, 1], BF16))
        ones128f = ec(nc.sbuf_tensor("ones128f", [128, 1], F32))
        rw_sb = ec(nc.sbuf_tensor("rw_sb", [128, DC * E], F32))
        srow = ec(nc.sbuf_tensor("srow", [1, E], F32))
        sb8 = ec(nc.sbuf_tensor("sb8", [128, E], F32))
        cols = ec(nc.sbuf_tensor("cols", [128, 4 * TC], F32))
        PS_A = ec(nc.psum_tensor("PS_A", [128, 512], F32))
        PS_B = ec(nc.psum_tensor("PS_B", [128, 512], F32))
        PS_C = ec(nc.psum_tensor("PS_C", [128, 512], F32))
        PS_D = ec(nc.psum_tensor("PS_D", [128, 512], F32))
        PS_W = ec(nc.psum_tensor("PS_W", [128, 2048], F32))
        PS = {"A": PS_A, "B": PS_B, "C": PS_C, "D": PS_D}

        B = Builder(nc)
        st, sv, ss = B.sem["st"], B.sem["sv"], B.sem["ss"]
        sgd, syd, scc = B.sem["sgd"], B.sem["syd"], B.sem["scc"]
        su, sg2, sd2, sw = B.sem["su"], B.sem["sg2"], B.sem["sd2"], B.sem["sw"]
        sx, sconst, se16 = B.sem["sx"], B.sem["sconst"], B.sem["se16"]
        sconsg = B.sem["sconsg"]
        sci, syo = B.sem["sci"], B.sem["syo"]

        def xs(dc, a, b):
            return x_sb[:, dc * T + a: dc * T + b]

        def xlns(dc, a, b):
            return xln_sb[:, dc * T + a: dc * T + b]

        def vq(fn):
            B.wait("v", sv, sv.n)
            return B.emit("v", fn, inc=sv)

        B.emit("v", lambda M: M.memset(onesf[:, :], 1.0))
        B.emit("v", lambda M: M.memset(onesbf[:, :], 1.0))
        B.emit("v", lambda M: M.memset(onescol[:, :], 1.0))
        B.emit("v", lambda M: M.memset(statw[:, :], 1.0 / D))
        v_consts = B.emit("v", lambda M: M.memset(ones128f[:, :], 1.0), inc=sv)

        src = dx.ap().rearrange("(a p) j -> p a j", p=128)
        dst = x_sb.ap().rearrange("p (a j) -> p a j", a=DC)
        y_xload = B.emit("y", lambda M, o=dst, i=src: M.dma_start(out=o, in_=i),
                         inc=sx, amount=16)
        rsrc = drw.ap().rearrange("(a p) j -> p a j", p=128)
        rdst = rw_sb.ap().rearrange("p (a j) -> p a j", a=DC)
        B.emit("g", lambda M, o=rdst, i=rsrc: M.dma_start(out=o, in_=i),
                      inc=sconsg, amount=16)
        CONSG_TOTAL = sconsg.n

        B.wait("t", sconsg, CONSG_TOTAL)
        B.wait("t", sv, v_consts)
        t_cs = None
        for dc in range(DC):
            t_cs = B.emit("t", lambda M, dc=dc: M.matmul(
                PS_B[0:1, 0:E], ones128f[:, 0:1], rw_sb[:, dc * E:(dc + 1) * E],
                start=(dc == 0), stop=(dc == DC - 1)),
                inc=(st if dc == DC - 1 else None))
        B.wait("v", st, t_cs)
        v_srow = B.emit("v", lambda M: M.tensor_copy(srow[:, :], PS_B[0:1, 0:E]), inc=sv)
        B.wait("t", sv, v_srow)
        t_sb8 = B.emit("t", lambda M: M.matmul(
            PS_B[:, 32:32 + E], onesf[0:1, 0:128], srow[0:1, :],
            start=True, stop=True), inc=st)
        B.wait("v", st, t_sb8)
        v_sb8 = B.emit("v", lambda M: M.tensor_copy(sb8[:, :], PS_B[:, 32:32 + E]), inc=sv)
        B.consumed("B", sv, v_sb8)

        def emit_ln(scr_sb, scr_off, sq_sb, sq_off):
            t_dc = 0
            for dc in range(DC):
                B.emit("s", lambda M, dc=dc: M.copy(xlns(dc, 0, T), xs(dc, 0, T)))
                B.wait("s", st, t_dc)
                v = B.emit("s", lambda M, dc=dc: M.activation(
                    sq_sb[:, sq_off: sq_off + T], xs(dc, 0, T), AF.Square), inc=ss)
                B.wait("t", ss, v)
                if dc == 0:
                    B.wait_slot("t", "Wstat")
                for ft in range(FT):
                    B.emit("t", lambda M, dc=dc, ft=ft: M.matmul(
                        PS_W[0:1, ft * 512:(ft + 1) * 512], statw[:, 0:1],
                        xlns(dc, ft * 512, (ft + 1) * 512),
                        start=(dc == 0), stop=(dc == DC - 1)))
                    t_dc = B.emit("t", lambda M, dc=dc, ft=ft: M.matmul(
                        PS_W[32:33, ft * 512:(ft + 1) * 512], statw[:, 0:1],
                        sq_sb[:, sq_off + ft * 512: sq_off + (ft + 1) * 512],
                        start=(dc == 0), stop=(dc == DC - 1)),
                        inc=(st if ft == FT - 1 else None))
            t_stats = t_dc
            B.wait("v", st, t_stats)
            vq(lambda M: M.tensor_copy(rows[0:1, :], PS_W[0:1, 0:T]))
            vq(lambda M: M.tensor_mul(rows[96:97, :], rows[0:1, :],
                                      rows[0:1, :]))
            v_var = vq(lambda M: M.scalar_tensor_tensor(
                rows[64:65, :], PS_W[32:33, 0:T], float(cfg.ln_eps),
                rows[96:97, :], ALU.add, ALU.subtract))
            B.consumed("Wstat", sv, v_var)
            B.wait("s", sv, v_var)
            s_sd = B.emit("s", lambda M: M.activation(
                rows[96:97, :], rows[64:65, :], AF.Sqrt), inc=ss)
            B.wait("v", ss, s_sd)
            def _rstd_lp(M):
                with nc.allow_low_precision(reason="bf16 rstd row"):
                    return M.reciprocal(rstd_row[0:1, :], rows[96:97, :])
            v_rstd = vq(_rstd_lp)
            B.wait("t", sv, v_rstd)
            names = ["A", "B", "C", "D"]
            for i in range(2 * FT):
                ft, which = i // 2, i % 2
                key = names[i % 4]
                ps = PS[key]
                B.wait_slot("t", key)
                if which == 0:
                    tv = B.emit("t", lambda M, ps=ps, ft=ft: M.matmul(
                        ps[:, 0:512], onesf[0:1, :],
                        rows[0:1, ft * 512:(ft + 1) * 512],
                        start=True, stop=True), inc=st)
                else:
                    tv = B.emit("t", lambda M, ps=ps, ft=ft: M.matmul(
                        ps[:, 0:512], onesbf[0:1, 0:128],
                        rstd_row[0:1, ft * 512:(ft + 1) * 512],
                        start=True, stop=True), inc=st)
                B.wait("v", st, tv)
                off = scr_off + T + which * T + ft * 512
                vv = B.emit("v", lambda M, ps=ps, off=off: M.tensor_copy(
                    scr_sb[:, off: off + 512], ps[:, 0:512]), inc=sv)
                B.consumed(key, sv, vv)
            v_ln = None
            for dc in range(DC):
                vq(lambda M, dc=dc: M.tensor_sub(
                    scr_sb[:, scr_off: scr_off + T], xs(dc, 0, T),
                    scr_sb[:, scr_off + T: scr_off + 2 * T]))
                v_ln = vq(lambda M, dc=dc: M.tensor_mul(
                    xlns(dc, 0, T), scr_sb[:, scr_off: scr_off + T],
                    scr_sb[:, scr_off + 2 * T: scr_off + 3 * T]))
            return v_ln, v_rstd

        def emit_router(v_rstd, logits_sb, mtok_sb):
            B.wait("t", sv, v_rstd)
            B.wait_slot("t", "B")
            for tc in range(TC):
                B.emit("t", lambda M, tc=tc: M.matmul(
                    PS_B[:, tc:tc + 1], rows[0:1, tc * 128:(tc + 1) * 128],
                    onesf[0:1, 0:1], start=True, stop=True))
                t_cols = B.emit("t", lambda M, tc=tc: M.matmul(
                    PS_B[:, TC + tc: TC + tc + 1],
                    rstd_row[0:1, tc * 128:(tc + 1) * 128],
                    onesbf[0:1, 0:1], start=True, stop=True),
                    inc=(st if tc == TC - 1 else None))
            B.wait("v", st, t_cols)
            v_cols = vq(lambda M: M.tensor_copy(
                cols[:, 0:2 * TC], PS_B[:, 0:2 * TC]))
            B.consumed("B", sv, v_cols)
            B.wait_slot("t", "Wstat")
            for tc in range(TC):
                for dc in range(DC):
                    t_raw = B.emit("t", lambda M, tc=tc, dc=dc: M.matmul(
                        PS_W[:, tc * E:(tc + 1) * E],
                        xs(dc, tc * 128, (tc + 1) * 128),
                        rw_sb[:, dc * E:(dc + 1) * E],
                        start=(dc == 0), stop=(dc == DC - 1)),
                        inc=(st if (tc == TC - 1 and dc == DC - 1) else None))
            B.wait("v", st, t_raw)
            B.wait("v", sv, v_cols)
            for tc in range(TC):
                vq(lambda M, tc=tc: M.tensor_scalar(
                    logits_sb[:, tc * E:(tc + 1) * E], sb8[:, :],
                    cols[:, tc:tc + 1], None, ALU.mult))
                vq(lambda M, tc=tc: M.tensor_sub(
                    logits_sb[:, tc * E:(tc + 1) * E], PS_W[:, tc * E:(tc + 1) * E],
                    logits_sb[:, tc * E:(tc + 1) * E]))
                vq(lambda M, tc=tc: M.tensor_scalar(
                    logits_sb[:, tc * E:(tc + 1) * E],
                    logits_sb[:, tc * E:(tc + 1) * E],
                    cols[:, TC + tc: TC + tc + 1], None, ALU.mult))
                vq(lambda M, tc=tc: M.tensor_reduce(
                    cols[:, 2 * TC + tc: 2 * TC + tc + 1],
                    logits_sb[:, tc * E:(tc + 1) * E], mybir.AxisListType.X, ALU.max))
            v_logit = vq(lambda M: M.tensor_scalar_mul(
                cols[:, 2 * TC: 3 * TC], cols[:, 2 * TC: 3 * TC], -1.0))
            B.wait("s", sv, v_logit)
            for tc in range(TC):
                s_exp = B.emit("s", lambda M, tc=tc: M.activation(
                    PS_W[:, tc * E:(tc + 1) * E], logits_sb[:, tc * E:(tc + 1) * E],
                    AF.Exp, bias=cols[:, 2 * TC + tc: 2 * TC + tc + 1], scale=1.0),
                    inc=(ss if tc == TC - 1 else None))
            B.wait("v", ss, s_exp)
            for tc in range(TC):
                vq(lambda M, tc=tc: M.tensor_reduce(
                    cols[:, 3 * TC + tc: 3 * TC + tc + 1],
                    PS_W[:, tc * E:(tc + 1) * E], mybir.AxisListType.X, ALU.add))
            v_se = vq(lambda M: M.reciprocal(
                cols[:, 3 * TC: 4 * TC], cols[:, 3 * TC: 4 * TC]))
            v_mtok = None
            for tc in range(TC):
                v_mtok = vq(lambda M, tc=tc: M.tensor_scalar(
                    mtok_sb[:, tc * E:(tc + 1) * E],
                    logits_sb[:, tc * E:(tc + 1) * E],
                    cols[:, 2 * TC + tc: 2 * TC + tc + 1], 0.0,
                    ALU.add, ALU.is_ge))
            B.consumed("Wstat", sv, v_mtok)
            return v_mtok

        def emit_ffn(v_gate, tag):
            C, SR = cfg.C, cfg.SR
            CB = C // 128
            ND = D // 512
            TRASH, ZROW = float(Er * C), float(Er * C + 1)
            ZB = (Er * C // 128) * 128
            with ExitStack() as ffn:
                fc = ffn.enter_context
                hid_sb = fc(nc.sbuf_tensor("hid" + tag, [128, 4 * T], BF16))
                U_sb = fc(nc.sbuf_tensor("U" + tag, [128, DC * 512], BF16))
                G_sb = fc(nc.sbuf_tensor("G" + tag, [128, DC * 512], BF16))
                U2_sb = fc(nc.sbuf_tensor("U2" + tag, [128, DC * 512], BF16))
                G2_sb = fc(nc.sbuf_tensor("G2" + tag, [128, DC * 512], BF16))
                D_sb = fc(nc.sbuf_tensor("Dw" + tag, [128, 4 * D], BF16))
                silu_sb = fc(nc.sbuf_tensor("silu" + tag, [128, 1024], BF16))
                logits_sb = fc(nc.sbuf_tensor("logits" + tag, [128, TC * E], F32))
                mtok_sb = fc(nc.sbuf_tensor("mtok" + tag, [128, TC * E], BF16))
                xg_tok = fc(nc.sbuf_tensor("xgt" + tag, [128, CB * D], BF16))
                xg_fm = fc(nc.sbuf_tensor("xgf" + tag, [128, DC * C], BF16))
                slotout = fc(nc.sbuf_tensor("sout" + tag, [128, CB * D], F32))
                xtt = fc(nc.sbuf_tensor("xtt" + tag, [128, D], BF16))
                rtt = fc(nc.sbuf_tensor("rtt" + tag, [128, D], BF16))
                pfx_sb = fc(nc.sbuf_tensor("pfx" + tag, [128, 2 * TC], F32))
                psc = fc(nc.sbuf_tensor("psc" + tag, [128, 4], F32))
                mpack = fc(nc.sbuf_tensor("mpack" + tag, [128, TC], BF16))
                selm = fc(nc.sbuf_tensor("selm" + tag, [128, TC], F32))
                tmp_sb = fc(nc.sbuf_tensor("tmpx" + tag, [128, TC], F32))
                orow_f = fc(nc.sbuf_tensor("orf" + tag, [128, TC], F32))
                orow_i = fc(nc.sbuf_tensor("ori" + tag, [128, TC], I32))
                brow_i = fc(nc.sbuf_tensor("bri" + tag, [128, TC], I32))
                ltri_sb = fc(nc.sbuf_tensor("ltri" + tag, [128, 128], BF16))
                ltriT_sb = fc(nc.sbuf_tensor("ltriT" + tag, [TC, TC], F32))
                iTf_sb = fc(nc.sbuf_tensor("iTf" + tag, [TC, TC], F32))
                i128_sb = fc(nc.sbuf_tensor("i128" + tag, [128, 128], BF16))

                B.wait("y", sv, v_gate)
                B.emit("y", lambda M: M.dma_start(
                    out=ltri_sb[:, :], in_=dltri[:, :]), inc=sconst, amount=16)
                B.emit("y", lambda M: M.dma_start(
                    out=ltriT_sb[:, :], in_=dltriT[:, :]), inc=sconst, amount=16)
                B.emit("y", lambda M: M.dma_start(
                    out=iTf_sb[:, :], in_=diTf[:, :]), inc=sconst, amount=16)
                B.emit("y", lambda M: M.dma_start(
                    out=i128_sb[:, :], in_=di128[:, :]), inc=sconst, amount=16)
                c_tot = sconst.n

                def wsrc(e):
                    if e < Er:
                        return (dup.ap()[e], dgate.ap()[e], ddown.ap()[e])
                    return (dsup.ap(), dsgate.ap(), dsdown.ap())

                UB, GB = [U_sb, U2_sb], [G_sb, G2_sb]
                order = [Er] + list(range(Er))
                r_units = [(e, hb) for e in range(Er) for hb in range(HB)]
                units = []
                nr = len(r_units)
                ri = 0
                for s_hb in range(HB):
                    units.append((Er, s_hb))
                    take = (nr * (s_hb + 1)) // HB - ri
                    units.extend(r_units[ri:ri + take])
                    ri += take

                def load_ug(k):
                    e, hb = units[k]
                    up, gate, _ = wsrc(e)
                    buf = k % 2
                    sem = [su, sg2][buf]
                    s_u = up[:, hb * 512:(hb + 1) * 512].rearrange(
                        "(a p) j -> p a j", p=128)
                    d_u = UB[buf].ap().rearrange("p (a j) -> p a j", a=DC)
                    B.emit("g", lambda M, o=d_u, i=s_u: M.dma_start(out=o, in_=i),
                           inc=sem, amount=16)
                    s_g = gate[:, hb * 512:(hb + 1) * 512].rearrange(
                        "(a p) j -> p a j", p=128)
                    d_g = GB[buf].ap().rearrange("p (a j) -> p a j", a=DC)
                    B.emit("g", lambda M, o=d_g, i=s_g: M.dma_start(out=o, in_=i),
                           inc=sem, amount=16)
                    return sem.n

                def load_d(k):
                    e, hb = units[k]
                    _, _, down = wsrc(e)
                    s_d = down[hb * 512:(hb + 1) * 512, :].rearrange(
                        "(a p) j -> p a j", p=128)
                    d_d = D_sb.ap().rearrange("p (a j) -> p a j", a=4)
                    return B.emit("g", lambda M, o=d_d, i=s_d: M.dma_start(out=o, in_=i),
                                  inc=sd2, amount=16)

                B.wait("g", sv, v_gate)
                ugv = {0: load_ug(0)}
                if len(units) > 1:
                    ugv[1] = load_ug(1)
                dv = load_d(0)

                v_ln, v_rstd = emit_ln(hid_sb, 0, hid_sb, 3 * T)
                v_mtok = emit_router(v_rstd, logits_sb, mtok_sb)

                B.wait("g", B.sem["sbg0"], B.sem["sbg0"].n)
                B.wait("g", B.sem["sbg1"], B.sem["sbg1"].n)

                mtok3 = mtok_sb.ap().rearrange("p (t e) -> p t e", e=E)
                vq(lambda M: M.memset(orow_f[:, :], 0.0))
                vq(lambda M: M.memset(selm[:, :], 0.0))
                B.wait("t", sconst, c_tot)
                B.wait("t", sv, v_mtok)
                for e in range(Er):
                    vm = vq(lambda M, e=e: M.tensor_copy(
                        mpack[:, :], mtok3[:, :, e]))
                    B.wait("t", sv, vm)
                    B.wait_slot("t", "B")
                    B.emit("t", lambda M: M.matmul(
                        PS_B[:, 0:TC], ltri_sb[:, :], mpack[:, :],
                        start=True, stop=True))
                    t1 = B.emit("t", lambda M: M.matmul(
                        PS_B[0:TC, 64:65], mpack[:, :], onescol[:, 0:1],
                        start=True, stop=True), inc=st)
                    B.wait("v", st, t1)
                    vq(lambda M: M.tensor_copy(pfx_sb[:, 0:TC], PS_B[:, 0:TC]))
                    v2 = vq(lambda M: M.tensor_copy(psc[0:TC, 0:1],
                                                    PS_B[0:TC, 64:65]))
                    B.wait("t", sv, v2)
                    t2 = B.emit("t", lambda M: M.matmul(
                        PS_B[0:TC, 96:97], ltriT_sb[:, :], psc[0:TC, 0:1],
                        start=True, stop=True), inc=st)
                    B.wait("v", st, t2)
                    v3 = vq(lambda M: M.tensor_copy(psc[0:TC, 1:2],
                                                    PS_B[0:TC, 96:97]))
                    B.wait("t", sv, v3)
                    t3 = B.emit("t", lambda M: M.matmul(
                        PS_B[0:1, 128:128 + TC], psc[0:TC, 1:2], iTf_sb[:, :],
                        start=True, stop=True), inc=st)
                    B.wait("v", st, t3)
                    v4 = vq(lambda M: M.tensor_copy(
                        pfx_sb[0:1, TC:2 * TC], PS_B[0:1, 128:128 + TC]))
                    B.wait("t", sv, v4)
                    t4 = B.emit("t", lambda M: M.matmul(
                        PS_B[:, 160:160 + TC], onesf[0:1, :],
                        pfx_sb[0:1, TC:2 * TC], start=True, stop=True), inc=st)
                    B.wait("v", st, t4)
                    v5 = vq(lambda M: M.tensor_copy(
                        pfx_sb[:, TC:2 * TC], PS_B[:, 160:160 + TC]))
                    B.consumed("B", sv, v5)
                    vq(lambda M, e=e: M.scalar_tensor_tensor(
                        tmp_sb[:, :], pfx_sb[:, 0:TC], float(e * C),
                        pfx_sb[:, TC:2 * TC], ALU.add, ALU.add))
                    vq(lambda M: M.tensor_mul(tmp_sb[:, :], tmp_sb[:, :],
                                              mpack[:, :]))
                    vq(lambda M: M.tensor_add(orow_f[:, :], orow_f[:, :],
                                              tmp_sb[:, :]))
                    vq(lambda M: M.tensor_add(selm[:, :], selm[:, :],
                                              mpack[:, :]))
                vq(lambda M: M.tensor_scalar(selm[:, :], selm[:, :], -1.0, 1.0,
                                             ALU.mult, ALU.add))
                vq(lambda M: M.scalar_tensor_tensor(
                    tmp_sb[:, :], selm[:, :], TRASH, orow_f[:, :],
                    ALU.mult, ALU.add))
                v_oi = vq(lambda M: M.tensor_copy(orow_i[:, :], tmp_sb[:, :]))
                vq(lambda M: M.scalar_tensor_tensor(
                    tmp_sb[:, :], selm[:, :], ZROW, orow_f[:, :],
                    ALU.mult, ALU.add))
                v_bi = vq(lambda M: M.tensor_copy(brow_i[:, :], tmp_sb[:, :]))

                six0 = B.sem["six0"]
                scat_vals = []
                for tc in range(TC):
                    par = tc % 2
                    B.wait_slot("t", "W%d" % par)
                    t_tr = None
                    for dc in range(DC):
                        t_tr = B.emit("t", lambda M, par=par, dc=dc, tc=tc:
                                      M.matmul(
                            PS_W[:, par * 1024 + dc * 128: par * 1024 + (dc + 1) * 128],
                            xlns(dc, tc * 128, (tc + 1) * 128), i128_sb[:, :],
                            start=True, stop=True),
                            inc=(st if dc == DC - 1 else None))
                    B.wait("v", st, t_tr)
                    if scat_vals:
                        B.wait("v", six0, scat_vals[-1])
                    v_xt = vq(lambda M, par=par: M.tensor_copy(
                        xtt[:, 0:D], PS_W[:, par * 1024: par * 1024 + D]))
                    B.consumed("W%d" % par, sv, v_xt)
                    B.wait("g", sv, max(v_xt, v_oi))
                    if scat_vals:
                        B.wait("g", six0, scat_vals[-1])
                    scat_vals.append(B.emit("g", lambda M, tc=tc:
                                            M.indirect_dma_start(
                        dslots[:, :],
                        bass.IndirectOffsetOnAxis(orow_i[:, tc:tc + 1], 0),
                        xtt[:, 0:D], None),
                        inc=six0, amount=16))
                sc_tot = six0.n

                t_xg_done = 0
                srw_prev = B.sem["srw"].n
                sxg_prev = B.sem["sxg"].n
                routed = [e for e in order if e < Er]

                def issue_xg(e):
                    nonlocal sxg_prev
                    B.wait("g", six0, sc_tot)
                    B.wait("g", st, t_xg_done)
                    B.wait("g", B.sem["sxg"], sxg_prev)
                    sxg_prev = B.emit("g", lambda M, e=e: M.dma_start(
                        out=xg_tok.ap().rearrange("p (a j) -> p a j", a=CB),
                        in_=dslots[e * C:(e + 1) * C, :].rearrange(
                            "(a p) j -> p a j", p=128)),
                        inc=B.sem["sxg"], amount=16)

                issue_xg(routed[0])
                xg_ready = sxg_prev

                for k, (e, hb) in enumerate(units):
                    Ub, Gb = UB[k % 2], GB[k % 2]
                    if e == Er:
                        if hb == 0:
                            B.wait("t", sv, v_ln)
                        B.wait("t", [su, sg2][k % 2], ugv[k])
                        tiles = [(h, f) for h in range(4) for f in range(FT)]
                        for i, (hc, ft) in enumerate(tiles):
                            kU = ["A", "C"][i % 2]
                            kG = ["B", "D"][i % 2]
                            psU, psG = PS[kU], PS[kG]
                            sl = i % 2
                            B.wait_slot("t", kU)
                            B.wait_slot("t", kG)
                            for dc in range(DC):
                                B.emit("t", lambda M, psU=psU, Ub=Ub, dc=dc, hc=hc, ft=ft:
                                       M.matmul(
                                    psU[:, 0:512],
                                    Ub[:, dc * 512 + hc * 128: dc * 512 + (hc + 1) * 128],
                                    xlns(dc, ft * 512, (ft + 1) * 512),
                                    start=(dc == 0), stop=(dc == DC - 1)))
                            for dc in range(DC):
                                tA = B.emit("t", lambda M, psG=psG, Gb=Gb, dc=dc, hc=hc, ft=ft:
                                            M.matmul(
                                    psG[:, 0:512],
                                    Gb[:, dc * 512 + hc * 128: dc * 512 + (hc + 1) * 128],
                                    xlns(dc, ft * 512, (ft + 1) * 512),
                                    start=(dc == 0), stop=(dc == DC - 1)),
                                    inc=(st if dc == DC - 1 else None))
                            B.wait("s", st, tA)
                            sS = B.emit("s", lambda M, psG=psG, sl=sl: M.activation(
                                silu_sb[:, sl * 512:(sl + 1) * 512], psG[:, 0:512],
                                AF.Silu), inc=ss)
                            B.wait("v", ss, sS)
                            vv = B.emit("v", lambda M, psU=psU, sl=sl, hc=hc, ft=ft:
                                        M.tensor_mul(
                                hid_sb[:, hc * T + ft * 512: hc * T + (ft + 1) * 512],
                                psU[:, 0:512],
                                silu_sb[:, sl * 512:(sl + 1) * 512]), inc=sv)
                            B.consumed(kU, sv, vv)
                            B.consumed(kG, sv, vv)
                            v_hid = vv
                        if k + 2 < len(units):
                            B.wait("g", st, tA)
                            ugv[k + 2] = load_ug(k + 2)
                        ugv.pop(k)
                        B.wait("t", sd2, dv)
                        B.wait("t", sv, v_hid)
                        for dco in range(DC):
                            for half in range(NHALF):
                                B.wait_slot("t", "W%d" % half)
                                for hc in range(4):
                                    for q in range(NQ):
                                        c0 = half * WH + q * 512
                                        tB = B.emit("t",
                                                    lambda M, dco=dco, hc=hc, c0=c0:
                                                    M.matmul(
                                            PS_W[:, c0:c0 + 512],
                                            D_sb[:, hc * D + dco * 128: hc * D + (dco + 1) * 128],
                                            hid_sb[:, hc * T + c0: hc * T + c0 + 512],
                                            start=(hc == 0), stop=(hc == 3)),
                                            inc=(st if (hc == 3 and q == NQ - 1) else None))
                                B.wait("v", st, tB)
                                vv = B.emit("v", lambda M, dco=dco, half=half:
                                            M.scalar_tensor_tensor(
                                    xs(dco, half * WH, (half + 1) * WH),
                                    PS_W[:, half * WH:(half + 1) * WH], 0.5,
                                    xs(dco, half * WH, (half + 1) * WH),
                                    ALU.mult, ALU.add), inc=sv)
                                B.consumed("W%d" % half, sv, vv)
                        if k + 1 < len(units):
                            B.wait("g", st, st.n)
                            dv = load_d(k + 1)
                        continue

                    if hb == 0:
                        B.wait("t", B.sem["sxg"], xg_ready)
                        for a in range(CB):
                            par = a % 2
                            B.wait_slot("t", "W%d" % par)
                            for dc in range(DC):
                                t_xg_done = B.emit("t", lambda M, a=a, dc=dc, par=par:
                                                   M.matmul(
                                    PS_W[:, par * 1024 + dc * 128: par * 1024 + (dc + 1) * 128],
                                    xg_tok[:, a * D + dc * 128: a * D + (dc + 1) * 128],
                                    i128_sb[:, :], start=True, stop=True),
                                    inc=(st if dc == DC - 1 else None))
                            B.wait("v", st, t_xg_done)
                            dst3 = xg_fm.ap().rearrange(
                                "p (b j) -> p b j", j=C)[:, :, a * 128:(a + 1) * 128]
                            src3 = PS_W.ap().rearrange(
                                "p (b j) -> p b j", j=128)[:, par * 8: par * 8 + DC, :]
                            v_xgf = vq(lambda M, dst3=dst3, src3=src3: M.tensor_copy(
                                dst3, src3))
                            B.consumed("W%d" % par, sv, v_xgf)
                        ri = routed.index(e)
                        if ri + 1 < len(routed):
                            issue_xg(routed[ri + 1])
                            xg_ready = sxg_prev
                    B.wait("t", [su, sg2][k % 2], ugv[k])
                    if hb == 0:
                        B.wait("t", sv, v_xgf)
                    for hc in range(4):
                        kU = ["A", "C"][hc % 2]
                        kG = ["B", "D"][hc % 2]
                        psU, psG = PS[kU], PS[kG]
                        sl = hc % 2
                        B.wait_slot("t", kU)
                        B.wait_slot("t", kG)
                        for dc in range(DC):
                            B.emit("t", lambda M, psU=psU, Ub=Ub, dc=dc, hc=hc: M.matmul(
                                psU[:, 0:C],
                                Ub[:, dc * 512 + hc * 128: dc * 512 + (hc + 1) * 128],
                                xg_fm[:, dc * C:(dc + 1) * C],
                                start=(dc == 0), stop=(dc == DC - 1)))
                        for dc in range(DC):
                            tA = B.emit("t", lambda M, psG=psG, Gb=Gb, dc=dc, hc=hc:
                                        M.matmul(
                                psG[:, 0:C],
                                Gb[:, dc * 512 + hc * 128: dc * 512 + (hc + 1) * 128],
                                xg_fm[:, dc * C:(dc + 1) * C],
                                start=(dc == 0), stop=(dc == DC - 1)),
                                inc=(st if dc == DC - 1 else None))
                        B.wait("s", st, tA)
                        sS = B.emit("s", lambda M, psG=psG, sl=sl: M.activation(
                            silu_sb[:, sl * 512: sl * 512 + C], psG[:, 0:C],
                            AF.Silu), inc=ss)
                        B.wait("v", ss, sS)
                        vv = B.emit("v", lambda M, psU=psU, sl=sl, hc=hc:
                                    M.tensor_mul(
                            hid_sb[:, hc * C:(hc + 1) * C], psU[:, 0:C],
                            silu_sb[:, sl * 512: sl * 512 + C]), inc=sv)
                        B.consumed(kU, sv, vv)
                        B.consumed(kG, sv, vv)
                        v_hid = vv
                    if k + 2 < len(units):
                        B.wait("g", st, tA)
                        ugv[k + 2] = load_ug(k + 2)
                    ugv.pop(k)
                    B.wait("t", sd2, dv)
                    B.wait("t", sv, v_hid)
                    if hb == 0:
                        B.wait("v", B.sem["srw"], srw_prev)
                    regions = [(s_, d_) for s_ in range(CB)
                               for d_ in range(ND)]
                    for r, (sc_, dh) in enumerate(regions):
                        key = "ABCD"[r % 4]
                        ps = PS[key]
                        B.wait_slot("t", key)
                        for hc in range(4):
                            tB = B.emit("t",
                                        lambda M, ps=ps, sc_=sc_, dh=dh, hc=hc:
                                        M.matmul(
                                ps[:, 0:512],
                                hid_sb[:, hc * C + sc_ * 128: hc * C + sc_ * 128 + 128],
                                D_sb[:, hc * D + dh * 512: hc * D + dh * 512 + 512],
                                start=(hc == 0), stop=(hc == 3)),
                                inc=(st if hc == 3 else None))
                        B.wait("v", st, tB)
                        off = sc_ * D + dh * 512
                        if hb == 0:
                            vv = B.emit("v", lambda M, ps=ps, off=off:
                                        M.tensor_copy(
                                slotout[:, off:off + 512], ps[:, 0:512]),
                                inc=sv)
                        else:
                            vv = B.emit("v", lambda M, ps=ps, off=off:
                                        M.tensor_add(
                                slotout[:, off:off + 512],
                                slotout[:, off:off + 512], ps[:, 0:512]),
                                inc=sv)
                        B.consumed(key, sv, vv)
                        v_sout = vv
                    if k + 1 < len(units):
                        B.wait("g", st, st.n)
                        dv = load_d(k + 1)
                    if hb == HB - 1:
                        B.wait("g", sv, v_sout)
                        srw_prev = B.emit("g", lambda M, e=e: M.dma_start(
                            out=dslots[e * C:(e + 1) * C, :].rearrange(
                                "(a p) j -> p a j", p=128),
                            in_=slotout.ap().rearrange("p (a j) -> p a j", a=CB)),
                            inc=B.sem["srw"], amount=16)

                sbg = [B.sem["sbg0"], B.sem["sbg1"]]
                sbg_prev = [sbg[0].n, sbg[1].n]
                bb = [rtt, xtt]
                t_bw_par = [0, 0]
                t_bw = 0
                v_out = 0
                for tc in range(TC):
                    par = tc % 2
                    B.wait("g", B.sem["srw"], B.sem["srw"].n)
                    B.wait("g", sv, v_bi)
                    B.wait("g", st, t_bw_par[par])
                    B.wait("g", sbg[par], sbg_prev[par])
                    sbg_prev[par] = B.emit("g", lambda M, tc=tc, par=par:
                                           M.indirect_dma_start(
                        bb[par][:, 0:D], None,
                        dslots[:, :],
                        bass.IndirectOffsetOnAxis(brow_i[:, tc:tc + 1], 0)),
                        inc=sbg[par], amount=16)
                    B.wait("v", sbg[par], sbg_prev[par])
                    v_sc = vq(lambda M, tc=tc, par=par: M.tensor_scalar(
                        bb[par][:, 0:D], bb[par][:, 0:D],
                        cols[:, 3 * TC + tc: 3 * TC + tc + 1], None, ALU.mult))
                    B.wait("t", sv, v_sc)
                    B.wait_slot("t", "W%d" % par)
                    for dc in range(DC):
                        t_bw = B.emit("t", lambda M, par=par, dc=dc: M.matmul(
                            PS_W[:, par * 1024 + dc * 128: par * 1024 + (dc + 1) * 128],
                            bb[par][:, dc * 128:(dc + 1) * 128],
                            i128_sb[:, :], start=True, stop=True),
                            inc=(st if dc == DC - 1 else None))
                    t_bw_par[par] = t_bw
                    B.wait("v", st, t_bw)
                    xdst = x_sb.ap().rearrange("p (b j) -> p b j", j=T)[
                        :, :, tc * 128:(tc + 1) * 128]
                    psrc = PS_W.ap().rearrange("p (b j) -> p b j", j=128)[
                        :, par * 8: par * 8 + DC, :]
                    v_out = vq(lambda M, xdst=xdst, psrc=psrc: M.tensor_add(
                        xdst, xdst, psrc))
                    B.consumed("W%d" % par, sv, v_out)
                B.consumed("Wstat", sv, v_out)
                return v_out

        def emit_attn(v_gate):
            with ExitStack() as att:
                ac = att.enter_context
                W0 = ac(nc.sbuf_tensor("W0", [128, DC * D], BF16))
                W1 = ac(nc.sbuf_tensor("W1", [128, DC * D], BF16))
                K_tile = ac(nc.sbuf_tensor("K_tile", [128, 2 * D], BF16))
                V_tile = ac(nc.sbuf_tensor("V_tile", [128, 2 * D], BF16))
                ascr = ac(nc.sbuf_tensor("ascr", [128, DC * T], BF16))
                recip_sb = ac(nc.sbuf_tensor("recip_sb", [NH, T], BF16))
                cc_sb = ac(nc.sbuf_tensor("cc_sb", [128, NP * 64 + NP], F32))
                kvbf_sb = ac(nc.sbuf_tensor("kvbf_sb", [128, NP * 64], BF16))
                zbd_sb = ac(nc.sbuf_tensor("zbd_sb", [128, DC * NH], BF16))
                e16_sb = ac(nc.sbuf_tensor("e16_sb", [NH, NP * 128], BF16))
                bias_at = ac(nc.sbuf_tensor("bias_at", [1, 2 * D], BF16))
                sc1 = ac(nc.sbuf_tensor("sc1a", [128, 512], BF16))
                sc2 = ac(nc.sbuf_tensor("sc2a", [128, 512], BF16))

                def load_w(dw, W_t, sem):
                    s_w = dw.ap().rearrange("(a p) j -> p a j", p=128)
                    d_w = W_t.ap().rearrange("p (a j) -> p a j", a=DC)
                    return B.emit("g", lambda M, o=d_w, i=s_w: M.dma_start(
                        out=o, in_=i), inc=sem, amount=16)

                B.wait("y", sv, v_gate)
                y_e16 = B.emit("y", lambda M: M.dma_start(
                    out=e16_sb[:, :], in_=de16[:, :]), inc=se16, amount=16)
                B.wait("g", sv, v_gate)
                w_k = load_w(dwk, W0, sw)
                w_v = load_w(dwv, W1, B.sem["swb"])
                B.emit("g", lambda M: M.dma_start(
                    out=bias_at[0:1, 0:D], in_=dbk[:, :]), inc=sconsg, amount=16)
                B.emit("g", lambda M: M.dma_start(
                    out=bias_at[0:1, D:2 * D], in_=dbv[:, :]), inc=sconsg, amount=16)
                bias_kv = sconsg.n

                v_ln2, _ = emit_ln(ascr, 0, ascr, 3 * T)
                v_ccz = vq(lambda M: M.memset(cc_sb[:, :], 0.0))

                B.wait("t", sw, w_k)
                B.wait("t", B.sem["swb"], w_v)
                B.wait("t", sconsg, bias_kv)
                B.wait("t", sv, v_ln2)
                B.wait_slot("t", "A")
                B.wait_slot("t", "B")
                B.wait_slot("t", "C")
                B.wait_slot("t", "Wstat")
                t_kvtc = [0] * TC
                for tc in range(TC):
                    half = tc % 2
                    B.wait_slot("t", "WpK")
                    tvk = 0
                    for os in range(NOS):
                        c0 = os * OSW
                        for dc in range(DC):
                            B.emit("t", lambda M, dc=dc, tc=tc, c0=c0: M.matmul(
                                PS_W[:, c0:c0 + OSW],
                                xlns(dc, tc * 128, (tc + 1) * 128),
                                W0[:, dc * D + c0: dc * D + c0 + OSW],
                                start=(dc == 0), stop=False))
                        tvk = B.emit("t", lambda M, c0=c0: M.matmul(
                            PS_W[:, c0:c0 + OSW], onesbf[0:1, 0:128],
                            bias_at[0:1, c0:c0 + OSW],
                            start=False, stop=True),
                            inc=(st if os == NOS - 1 else None))
                    vphi = 0
                    for os in range(NOS):
                        c0 = os * OSW
                        dk = K_tile[:, half * D + c0: half * D + c0 + OSW]
                        B.wait("v", st, tvk)
                        if tc >= 2:
                            B.wait("v", st, t_kvtc[tc - 2])
                        v1 = vq(lambda M, c0=c0: M.tensor_scalar_min(
                            sc1[:, 0:OSW], PS_W[:, c0:c0 + OSW], 0.0))
                        B.wait("s", sv, v1)
                        s1 = B.emit("s", lambda M: M.activation(
                            sc2[:, 0:OSW], sc1[:, 0:OSW], AF.Exp), inc=ss)
                        B.wait("v", ss, s1)
                        vphi = vq(lambda M, c0=c0, dk=dk: M.scalar_tensor_tensor(
                            dk, PS_W[:, c0:c0 + OSW], 0.0,
                            sc2[:, 0:OSW], ALU.max, ALU.add))
                    B.consumed("WpK", sv, vphi)
                    B.wait_slot("t", "WpV")
                    tvv = 0
                    for os in range(NOS):
                        c0 = D + os * OSW
                        for dc in range(DC):
                            B.emit("t", lambda M, dc=dc, tc=tc, c0=c0: M.matmul(
                                PS_W[:, c0:c0 + OSW],
                                xlns(dc, tc * 128, (tc + 1) * 128),
                                W1[:, dc * D + c0 - D: dc * D + c0 - D + OSW],
                                start=(dc == 0), stop=False))
                        tvv = B.emit("t", lambda M, c0=c0: M.matmul(
                            PS_W[:, c0:c0 + OSW], onesbf[0:1, 0:128],
                            bias_at[0:1, c0:c0 + OSW],
                            start=False, stop=True),
                            inc=(st if os == NOS - 1 else None))
                    vcv = 0
                    for os in range(NOS):
                        c0 = D + os * OSW
                        dv = V_tile[:, half * D + os * OSW: half * D + (os + 1) * OSW]
                        B.wait("v", st, tvv)
                        if tc >= 2:
                            B.wait("v", st, t_kvtc[tc - 2])
                        vcv = vq(lambda M, c0=c0, dv=dv: M.tensor_copy(
                            dv, PS_W[:, c0:c0 + OSW]))
                    B.consumed("WpV", sv, vcv)
                    B.wait("t", sv, vcv)
                    if tc > 0:
                        B.wait("t", sv, v_kvadd)
                    for p in range(NP):
                        b0 = half * D + p * 128
                        B.emit("t", lambda M, p=p, b0=b0: M.matmul(
                            PS_A[0:64, p * 64:(p + 1) * 64],
                            K_tile[:, b0: b0 + 64], V_tile[:, b0: b0 + 64],
                            start=True, stop=True))
                        B.emit("t", lambda M, p=p, b0=b0: M.matmul(
                            PS_B[64:128, p * 64:(p + 1) * 64],
                            K_tile[:, b0 + 64: b0 + 128],
                            V_tile[:, b0 + 64: b0 + 128],
                            start=True, stop=True))
                        t_kvtc[tc] = B.emit("t", lambda M, p=p, b0=b0: M.matmul(
                            PS_C[:, p:p + 1],
                            K_tile[:, b0: b0 + 128], onescol[:, 0:1],
                            start=True, stop=True),
                            inc=(st if p == NP - 1 else None))
                    B.wait("v", st, t_kvtc[tc])
                    if tc == 0:
                        B.wait("v", sv, v_ccz)
                    vq(lambda M: M.tensor_add(
                        cc_sb[0:64, 0:NP * 64], cc_sb[0:64, 0:NP * 64],
                        PS_A[0:64, 0:NP * 64]))
                    vq(lambda M: M.tensor_add(
                        cc_sb[64:128, 0:NP * 64], cc_sb[64:128, 0:NP * 64],
                        PS_B[64:128, 0:NP * 64]))
                    v_kvadd = vq(lambda M: M.tensor_add(
                        cc_sb[:, NP * 64: NP * 64 + NP],
                        cc_sb[:, NP * 64: NP * 64 + NP], PS_C[:, 0:NP]))
                t_kv = t_kvtc[TC - 1]
                v_cc = v_kvadd

                B.wait("g", st, t_kv)
                w_q = load_w(dwq, W0, sw)
                w_o = load_w(dwo, W1, B.sem["swb"])
                B.emit("g", lambda M: M.dma_start(
                    out=bias_at[0:1, 0:D], in_=dbq[:, :]), inc=sconsg, amount=16)
                B.emit("g", lambda M: M.dma_start(
                    out=bias_at[0:1, D:2 * D], in_=dbo[:, :]), inc=sconsg, amount=16)
                bias_qo = sconsg.n

                B.consumed("A", sv, v_cc)
                B.consumed("B", sv, v_cc)
                B.consumed("C", sv, v_cc)
                B.wait("y", sv, v_cc)
                y_ccin = B.emit("y", lambda M: M.dma_start(
                    out=dccin[:, :], in_=cc_sb[:, :]), inc=sci, amount=16)
                B.wait("g", sci, y_ccin)
                g_cc = B.emit("g", lambda M: M.collective_compute(
                    "AllReduce", ALU.add,
                    replica_groups=[list(range(0, cfg.group)),
                                    list(range(cfg.group, 2 * cfg.group))],
                    ins=[dccin.ap().opt()], outs=[dccout.ap().opt()]), inc=scc)
                B.wait("g", scc, g_cc)
                g_ccr = B.emit("g", lambda M: M.dma_start(
                    out=cc_sb[:, :], in_=dccout[:, :]), inc=sgd, amount=16)

                phiQ = ascr
                B.wait("t", sw, w_q)
                B.wait("t", sconsg, bias_qo)
                t_q = 0
                v_q = 0
                for i, (oc, ft) in enumerate(
                        [(o, f) for o in range(DC) for f in range(FT)]):
                    key = ["C", "D"][i % 2]
                    ps = PS[key]
                    B.wait_slot("t", key)
                    for dc in range(DC):
                        B.emit("t", lambda M, ps=ps, dc=dc, oc=oc, ft=ft: M.matmul(
                            ps[:, 0:512],
                            W0[:, dc * D + oc * 128: dc * D + (oc + 1) * 128],
                            xlns(dc, ft * 512, (ft + 1) * 512),
                            start=(dc == 0), stop=False))
                    t_q = B.emit("t", lambda M, ps=ps, oc=oc: M.matmul(
                        ps[:, 0:512], bias_at[0:1, oc * 128:(oc + 1) * 128],
                        onesbf[0:1, :], start=False, stop=True), inc=st)
                    B.wait("v", st, t_q)
                    v1 = vq(lambda M, ps=ps: M.tensor_scalar_min(
                        sc1[:, :], ps[:, 0:512], 0.0))
                    B.wait("s", sv, v1)
                    s1 = B.emit("s", lambda M: M.activation(
                        sc2[:, :], sc1[:, :], AF.Exp), inc=ss)
                    B.wait("v", ss, s1)
                    v_q = vq(lambda M, ps=ps, oc=oc, ft=ft: M.scalar_tensor_tensor(
                        phiQ[:, oc * T + ft * 512: oc * T + (ft + 1) * 512],
                        ps[:, 0:512], 0.0, sc2[:, :], ALU.max, ALU.add))
                    B.consumed(key, sv, v_q)

                B.wait("v", sgd, g_ccr)
                vq(lambda M: M.tensor_copy(
                    kvbf_sb[:, :], cc_sb[:, 0:NP * 64]))
                vq(lambda M: M.memset(zbd_sb[:, :], 0.0))
                v_kvz = 0
                for dc in range(DC):
                    vq(lambda M, dc=dc: M.tensor_copy(
                        zbd_sb[0:64, dc * NH + 2 * dc: dc * NH + 2 * dc + 1],
                        cc_sb[0:64, NP * 64 + dc: NP * 64 + dc + 1]))
                    v_kvz = vq(lambda M, dc=dc: M.tensor_copy(
                        zbd_sb[64:128, dc * NH + 2 * dc + 1: dc * NH + 2 * dc + 2],
                        cc_sb[64:128, NP * 64 + dc: NP * 64 + dc + 1]))

                B.wait("t", sv, max(v_kvz, v_q))
                B.wait_slot("t", "Wstat")
                t_nrm = 0
                for qq in range(FT):
                    for dc in range(DC):
                        t_nrm = B.emit("t", lambda M, qq=qq, dc=dc: M.matmul(
                            PS_W[0:NH, qq * 512:(qq + 1) * 512],
                            zbd_sb[:, dc * NH:(dc + 1) * NH],
                            phiQ[:, dc * T + qq * 512: dc * T + (qq + 1) * 512],
                            start=(dc == 0), stop=(dc == DC - 1)),
                            inc=(st if (qq == FT - 1 and dc == DC - 1) else None))
                B.wait("v", st, t_nrm)
                vq(lambda M: M.tensor_scalar(
                    PS_W[32:32 + NH, 0:T], PS_W[0:NH, 0:T],
                    float(cfg.attn_eps), None, ALU.add))

                def _recip_lp(M):
                    with nc.allow_low_precision(reason="bf16 attn norm recip"):
                        return M.reciprocal(recip_sb[:, :], PS_W[32:32 + NH, 0:T])
                v_rec = vq(_recip_lp)
                B.consumed("Wstat", sv, v_rec)

                B.wait("t", sv, v_rec)
                B.wait("t", se16, y_e16)
                okeys = [("A", "B"), ("C", "D")]
                v_fm = 0
                for i, (p, ft) in enumerate(
                        [(p, f) for p in range(NP) for f in range(FT)]):
                    kO, kR = okeys[i % 2]
                    psO, psR = PS[kO], PS[kR]
                    B.wait_slot("t", kO)
                    B.wait_slot("t", kR)
                    B.emit("t", lambda M, psO=psO, p=p, ft=ft: M.matmul(
                        psO[0:64, 0:512], kvbf_sb[0:64, p * 64:(p + 1) * 64],
                        phiQ[0:64, p * T + ft * 512: p * T + (ft + 1) * 512],
                        start=True, stop=True))
                    B.emit("t", lambda M, psO=psO, p=p, ft=ft: M.matmul(
                        psO[64:128, 0:512], kvbf_sb[64:128, p * 64:(p + 1) * 64],
                        phiQ[64:128, p * T + ft * 512: p * T + (ft + 1) * 512],
                        start=True, stop=True))
                    tv = B.emit("t", lambda M, psR=psR, p=p, ft=ft: M.matmul(
                        psR[:, 0:512], e16_sb[:, p * 128:(p + 1) * 128],
                        recip_sb[:, ft * 512:(ft + 1) * 512],
                        start=True, stop=True), inc=st)
                    B.wait("v", st, tv)
                    vq(lambda M, psR=psR: M.tensor_copy(
                        sc1[:, :], psR[:, 0:512]))
                    v_fm = vq(lambda M, psO=psO, p=p, ft=ft: M.tensor_mul(
                        xlns(p, ft * 512, (ft + 1) * 512), psO[:, 0:512],
                        sc1[:, :]))
                    B.consumed(kO, sv, v_fm)
                    B.consumed(kR, sv, v_fm)

                B.wait("t", B.sem["swb"], w_o)
                B.wait("t", sv, v_fm)
                v_out = 0
                for i, (oc, ft) in enumerate(
                        [(o, f) for o in range(DC) for f in range(FT)]):
                    key = ["C", "D"][i % 2]
                    ps = PS[key]
                    B.wait_slot("t", key)
                    for dc in range(DC):
                        B.emit("t", lambda M, ps=ps, dc=dc, oc=oc, ft=ft: M.matmul(
                            ps[:, 0:512],
                            W1[:, dc * D + oc * 128: dc * D + (oc + 1) * 128],
                            xlns(dc, ft * 512, (ft + 1) * 512),
                            start=(dc == 0), stop=False))
                    tv = B.emit("t", lambda M, ps=ps, oc=oc: M.matmul(
                        ps[:, 0:512], bias_at[0:1, D + oc * 128: D + (oc + 1) * 128],
                        onesbf[0:1, :], start=False, stop=True), inc=st)
                    B.wait("v", st, tv)
                    v_out = B.emit("v", lambda M, ps=ps, oc=oc, ft=ft: M.tensor_add(
                        xs(oc, ft * 512, (ft + 1) * 512),
                        xs(oc, ft * 512, (ft + 1) * 512), ps[:, 0:512]), inc=sv)
                    B.consumed(key, sv, v_out)
                return v_out

        B.wait("s", sx, y_xload)
        B.wait("t", sx, y_xload)
        v1 = emit_ffn(v_sb8, "_f1")
        B.wait("s", sv, v1)
        v2 = emit_attn(v1)
        B.wait("s", sv, v2)
        v3 = emit_ffn(v2, "_f2")

        B.wait("y", sv, v3)
        osrc = x_sb.ap().rearrange("p (a j) -> p a j", a=DC)
        odst = dy.ap().rearrange("(a p) j -> p a j", p=128)
        y_out = B.emit("y", lambda M, o=odst, i=osrc: M.dma_start(out=o, in_=i),
                       inc=syo, amount=16)
        B.wait("y", syo, y_out)

        B.finalize()
    return nc


def _host_constants(cfg):
    NH, NP, TC = cfg.NH, cfg.NP, cfg.TC
    i128 = np.eye(128, dtype=np.float32).astype(BF)
    e16 = np.zeros((NH, NP * 128), dtype=BF)
    for p in range(NP):
        e16[2 * p, p * 128: p * 128 + 64] = 1.0
        e16[2 * p + 1, p * 128 + 64: (p + 1) * 128] = 1.0
    ltri = np.triu(np.ones((128, 128), np.float32), 1).astype(BF)
    ltriT = np.triu(np.ones((TC, TC), np.float32), 1)
    iTf = np.eye(TC, dtype=np.float32)
    return i128, e16, ltri, ltriT, iTf


def stage_inputs(cfg, x, Wq, bq, Wk, bk, Wv, bv, Wo, bo, rw, upw, gatew, downw,
                 sup, sgate, sdown, n_cores=8):
    f32 = np.float32
    x = np.asarray(x, f32)
    Bb = x.shape[0]
    per_batch = n_cores // Bb
    i128, e16, ltri, ltriT, iTf = _host_constants(cfg)
    common = {
        "wq": np.ascontiguousarray(np.asarray(Wq, f32).T),
        "wk": np.ascontiguousarray(np.asarray(Wk, f32).T),
        "wv": np.ascontiguousarray(np.asarray(Wv, f32).T),
        "wo": np.ascontiguousarray(np.asarray(Wo, f32).T),
        "bq": np.asarray(bq, f32).reshape(1, -1),
        "bk": np.asarray(bk, f32).reshape(1, -1),
        "bv": np.asarray(bv, f32).reshape(1, -1),
        "bo": np.asarray(bo, f32).reshape(1, -1),
        "rw": np.ascontiguousarray(np.asarray(rw, f32)),
        "upw": np.ascontiguousarray(np.asarray(upw, f32)),
        "gatew": np.ascontiguousarray(np.asarray(gatew, f32)),
        "downw": np.ascontiguousarray(np.asarray(downw, f32)),
        "sup": np.ascontiguousarray(np.asarray(sup, f32)),
        "sgate": np.ascontiguousarray(np.asarray(sgate, f32)),
        "sdown": np.ascontiguousarray(np.asarray(sdown, f32)),
        "i128": i128, "e16": e16, "ltri": ltri, "ltriT": ltriT, "iTf": iTf,
        "slots": np.zeros((cfg.SR, cfg.D), dtype=BF),
    }
    in_maps = []
    for c in range(n_cores):
        b = c // per_batch
        t0 = (c % per_batch) * cfg.T
        m = dict(common)
        m["x"] = np.ascontiguousarray(x[b, t0:t0 + cfg.T, :].T)
        in_maps.append(m)
    return in_maps


def gather_out(cfg, results, Bb, S, D, n_cores=8):
    per_batch = n_cores // Bb
    out = np.empty((Bb, S, D), dtype=np.float32)
    for c in range(n_cores):
        b = c // per_batch
        t0 = (c % per_batch) * cfg.T
        out[b, t0:t0 + cfg.T, :] = results[c]["y"].T
    return out


def kernel(L0_attn_x, L0_attn_Wq, L0_attn_bq, L0_attn_Wk, L0_attn_bk,
           L0_attn_Wv, L0_attn_bv, L0_attn_Wo, L0_attn_bo,
           L0_ffn_router_w, L0_ffn_up_w, L0_ffn_gate_w, L0_ffn_down_w,
           L0_ffn_shared_up_w, L0_ffn_shared_gate_w, L0_ffn_shared_down_w):
    x = np.asarray(L0_attn_x, dtype=np.float32)
    Bb, S, D = x.shape
    upw = np.asarray(L0_ffn_up_w)
    cfg = Cfg(T=(Bb * S) // 8, D=D, HF=upw.shape[2], Er=upw.shape[0],
              E=np.asarray(L0_ffn_router_w).shape[1], group=8 // Bb)
    in_maps = stage_inputs(cfg, x, L0_attn_Wq, L0_attn_bq, L0_attn_Wk,
                           L0_attn_bk, L0_attn_Wv, L0_attn_bv, L0_attn_Wo,
                           L0_attn_bo, L0_ffn_router_w, L0_ffn_up_w,
                           L0_ffn_gate_w, L0_ffn_down_w, L0_ffn_shared_up_w,
                           L0_ffn_shared_gate_w, L0_ffn_shared_down_w)
    nc = build_nc(cfg)
    res = run_bass_kernel_spmd(nc, in_maps, core_ids=list(range(8)))
    return gather_out(cfg, res.results, Bb, S, D)


# version 08
# speedup vs baseline: 2.2932x; 1.0129x over previous
from contextlib import ExitStack

import numpy as np
import ml_dtypes

import concourse.bass as bass
import concourse.mybir as mybir
from concourse.bass_utils import run_bass_kernel_spmd

F32 = mybir.dt.float32
BF16 = mybir.dt.bfloat16
I32 = mybir.dt.int32
AF = mybir.ActivationFunctionType
ALU = mybir.AluOpType
BF = ml_dtypes.bfloat16


class Cfg:
    def __init__(self, T=2048, D=1024, HF=4096, Er=7, E=8, group=4,
                 ln_eps=1e-5, attn_eps=1e-6):
        assert T % 512 == 0 and D % 128 == 0 and HF % 512 == 0
        self.T, self.D, self.HF, self.Er, self.E = T, D, HF, Er, E
        self.DC = D // 128
        self.TC = T // 128
        self.FT = T // 512
        self.HB = HF // 512
        self.NH = D // 64
        self.NP = D // 128
        self.group = group
        self.ln_eps, self.attn_eps = ln_eps, attn_eps
        self.C = T // 4
        self.SR = ((Er * self.C + 2 + 127) // 128) * 128


class Sem:
    def __init__(self, nc, name):
        self.h = nc.alloc_semaphore(name)
        self.n = 0

    def bump(self, amount=1):
        self.n += amount
        return self.n


class Builder:
    ENGS = ("t", "v", "s", "g", "y")

    def __init__(self, nc):
        self.nc = nc
        self.q = {e: [] for e in self.ENGS}
        self.sem = {k: Sem(nc, k) for k in
                    ("st", "sv", "ss", "scc", "sgd", "syd", "su", "sg2", "sd2",
                     "sw", "swb", "sx", "sconst", "sconsg", "se16", "sci", "syo",
                     "szf", "six0", "six1", "sxg", "srw", "sbg0", "sbg1")}
        self.cons = {}

    def wait(self, eng, sem, val):
        if val is None or val <= 0:
            return
        h = sem.h
        self.q[eng].append(lambda M: M.wait_ge(h, val))

    def wait_slot(self, eng, key):
        c = self.cons.get(key)
        if c is not None:
            self.wait(eng, c[0], c[1])

    def consumed(self, key, sem, val):
        self.cons[key] = (sem, val)

    def emit(self, eng, fn, inc=None, amount=1):
        if inc is None:
            self.q[eng].append(fn)
            return None
        val = inc.bump(amount)
        h = inc.h

        def L(M, fn=fn, h=h, amount=amount):
            fn(M).then_inc(h, amount)
        self.q[eng].append(L)
        return val

    def finalize(self):
        with self.nc.Block() as block:
            @block.tensor
            def _(E):
                for f in self.q["t"]:
                    f(E)

            @block.vector
            def _(E):
                for f in self.q["v"]:
                    f(E)

            @block.scalar
            def _(E):
                for f in self.q["s"]:
                    f(E)

            @block.gpsimd
            def _(E):
                for f in self.q["g"]:
                    f(E)

            @block.sync
            def _(E):
                for f in self.q["y"]:
                    f(E)


def build_nc(cfg: Cfg):
    nc = bass.Bass("TRN2", target_bir_lowering=False)
    T, D, HF, Er, E = cfg.T, cfg.D, cfg.HF, cfg.Er, cfg.E
    DC, TC, FT, HB, NH, NP = cfg.DC, cfg.TC, cfg.FT, cfg.HB, cfg.NH, cfg.NP
    WH = min(1024, T)
    NHALF = T // WH
    NQ = WH // 512
    OSW = min(512, D)
    NOS = D // OSW

    dx = nc.dram_tensor("x", [D, T], F32, kind="ExternalInput")
    dy = nc.dram_tensor("y", [D, T], F32, kind="ExternalOutput")
    dwq = nc.dram_tensor("wq", [D, D], F32, kind="ExternalInput")
    dwk = nc.dram_tensor("wk", [D, D], F32, kind="ExternalInput")
    dwv = nc.dram_tensor("wv", [D, D], F32, kind="ExternalInput")
    dwo = nc.dram_tensor("wo", [D, D], F32, kind="ExternalInput")
    dbq = nc.dram_tensor("bq", [1, D], F32, kind="ExternalInput")
    dbk = nc.dram_tensor("bk", [1, D], F32, kind="ExternalInput")
    dbv = nc.dram_tensor("bv", [1, D], F32, kind="ExternalInput")
    dbo = nc.dram_tensor("bo", [1, D], F32, kind="ExternalInput")
    drw = nc.dram_tensor("rw", [D, E], F32, kind="ExternalInput")
    dup = nc.dram_tensor("upw", [Er, D, HF], F32, kind="ExternalInput")
    dgate = nc.dram_tensor("gatew", [Er, D, HF], F32, kind="ExternalInput")
    ddown = nc.dram_tensor("downw", [Er, HF, D], F32, kind="ExternalInput")
    dsup = nc.dram_tensor("sup", [D, HF], F32, kind="ExternalInput")
    dsgate = nc.dram_tensor("sgate", [D, HF], F32, kind="ExternalInput")
    dsdown = nc.dram_tensor("sdown", [HF, D], F32, kind="ExternalInput")
    dltri = nc.dram_tensor("ltri", [128, 128], BF16, kind="ExternalInput")
    dltriT = nc.dram_tensor("ltriT", [TC, TC], F32, kind="ExternalInput")
    diTf = nc.dram_tensor("iTf", [TC, TC], F32, kind="ExternalInput")
    dslots = nc.dram_tensor("slots", [cfg.SR, D], BF16, kind="ExternalInput")
    di128 = nc.dram_tensor("i128", [128, 128], BF16, kind="ExternalInput")
    de16 = nc.dram_tensor("e16", [NH, NP * 128], BF16, kind="ExternalInput")
    dccin = nc.dram_tensor("cc_in", [128, NP * 64 + NP], F32)
    dccout = nc.dram_tensor("cc_out", [128, NP * 64 + NP], F32)

    with ExitStack() as top:
        ec = top.enter_context
        x_sb = ec(nc.sbuf_tensor("x_sb", [128, DC * T], F32))
        xln_sb = ec(nc.sbuf_tensor("xln_sb", [128, DC * T], BF16))
        rows = ec(nc.sbuf_tensor("rows", [128, T], F32))
        rstd_row = ec(nc.sbuf_tensor("rstd_row", [1, T], BF16))
        onesf = ec(nc.sbuf_tensor("onesf", [1, 128], F32))
        onesbf = ec(nc.sbuf_tensor("onesbf", [1, 512], BF16))
        onescol = ec(nc.sbuf_tensor("onescol", [128, 1], BF16))
        statw = ec(nc.sbuf_tensor("statw", [128, 1], BF16))
        ones128f = ec(nc.sbuf_tensor("ones128f", [128, 1], F32))
        rw_sb = ec(nc.sbuf_tensor("rw_sb", [128, DC * E], F32))
        srow = ec(nc.sbuf_tensor("srow", [1, E], F32))
        sb8 = ec(nc.sbuf_tensor("sb8", [128, E], F32))
        cols = ec(nc.sbuf_tensor("cols", [128, 4 * TC], F32))
        PS_A = ec(nc.psum_tensor("PS_A", [128, 512], F32))
        PS_B = ec(nc.psum_tensor("PS_B", [128, 512], F32))
        PS_C = ec(nc.psum_tensor("PS_C", [128, 512], F32))
        PS_D = ec(nc.psum_tensor("PS_D", [128, 512], F32))
        PS_W = ec(nc.psum_tensor("PS_W", [128, 2048], F32))
        PS = {"A": PS_A, "B": PS_B, "C": PS_C, "D": PS_D}

        B = Builder(nc)
        st, sv, ss = B.sem["st"], B.sem["sv"], B.sem["ss"]
        sgd, syd, scc = B.sem["sgd"], B.sem["syd"], B.sem["scc"]
        su, sg2, sd2, sw = B.sem["su"], B.sem["sg2"], B.sem["sd2"], B.sem["sw"]
        sx, sconst, se16 = B.sem["sx"], B.sem["sconst"], B.sem["se16"]
        sconsg = B.sem["sconsg"]
        sci, syo = B.sem["sci"], B.sem["syo"]

        def xs(dc, a, b):
            return x_sb[:, dc * T + a: dc * T + b]

        def xlns(dc, a, b):
            return xln_sb[:, dc * T + a: dc * T + b]

        def vq(fn):
            B.wait("v", sv, sv.n)
            return B.emit("v", fn, inc=sv)

        B.emit("v", lambda M: M.memset(onesf[:, :], 1.0))
        B.emit("v", lambda M: M.memset(onesbf[:, :], 1.0))
        B.emit("v", lambda M: M.memset(onescol[:, :], 1.0))
        B.emit("v", lambda M: M.memset(statw[:, :], 1.0 / D))
        v_consts = B.emit("v", lambda M: M.memset(ones128f[:, :], 1.0), inc=sv)

        src = dx.ap().rearrange("(a p) j -> p a j", p=128)
        dst = x_sb.ap().rearrange("p (a j) -> p a j", a=DC)
        y_xload = B.emit("y", lambda M, o=dst, i=src: M.dma_start(out=o, in_=i),
                         inc=sx, amount=16)
        rsrc = drw.ap().rearrange("(a p) j -> p a j", p=128)
        rdst = rw_sb.ap().rearrange("p (a j) -> p a j", a=DC)
        B.emit("g", lambda M, o=rdst, i=rsrc: M.dma_start(out=o, in_=i),
                      inc=sconsg, amount=16)
        CONSG_TOTAL = sconsg.n

        B.wait("t", sconsg, CONSG_TOTAL)
        B.wait("t", sv, v_consts)
        t_cs = None
        for dc in range(DC):
            t_cs = B.emit("t", lambda M, dc=dc: M.matmul(
                PS_B[0:1, 0:E], ones128f[:, 0:1], rw_sb[:, dc * E:(dc + 1) * E],
                start=(dc == 0), stop=(dc == DC - 1)),
                inc=(st if dc == DC - 1 else None))
        B.wait("v", st, t_cs)
        v_srow = B.emit("v", lambda M: M.tensor_copy(srow[:, :], PS_B[0:1, 0:E]), inc=sv)
        B.wait("t", sv, v_srow)
        t_sb8 = B.emit("t", lambda M: M.matmul(
            PS_B[:, 32:32 + E], onesf[0:1, 0:128], srow[0:1, :],
            start=True, stop=True), inc=st)
        B.wait("v", st, t_sb8)
        v_sb8 = B.emit("v", lambda M: M.tensor_copy(sb8[:, :], PS_B[:, 32:32 + E]), inc=sv)
        B.consumed("B", sv, v_sb8)

        def emit_ln(scr_sb, scr_off, sq_sb, sq_off):
            t_dc = 0
            for dc in range(DC):
                B.emit("s", lambda M, dc=dc: M.copy(xlns(dc, 0, T), xs(dc, 0, T)))
                B.wait("s", st, t_dc)
                v = B.emit("s", lambda M, dc=dc: M.activation(
                    sq_sb[:, sq_off: sq_off + T], xs(dc, 0, T), AF.Square), inc=ss)
                B.wait("t", ss, v)
                if dc == 0:
                    B.wait_slot("t", "Wstat")
                for ft in range(FT):
                    B.emit("t", lambda M, dc=dc, ft=ft: M.matmul(
                        PS_W[0:1, ft * 512:(ft + 1) * 512], statw[:, 0:1],
                        xlns(dc, ft * 512, (ft + 1) * 512),
                        start=(dc == 0), stop=(dc == DC - 1)))
                    t_dc = B.emit("t", lambda M, dc=dc, ft=ft: M.matmul(
                        PS_W[32:33, ft * 512:(ft + 1) * 512], statw[:, 0:1],
                        sq_sb[:, sq_off + ft * 512: sq_off + (ft + 1) * 512],
                        start=(dc == 0), stop=(dc == DC - 1)),
                        inc=(st if ft == FT - 1 else None))
            t_stats = t_dc
            B.wait("v", st, t_stats)
            vq(lambda M: M.tensor_copy(rows[0:1, :], PS_W[0:1, 0:T]))
            vq(lambda M: M.tensor_mul(rows[96:97, :], rows[0:1, :],
                                      rows[0:1, :]))
            v_var = vq(lambda M: M.scalar_tensor_tensor(
                rows[64:65, :], PS_W[32:33, 0:T], float(cfg.ln_eps),
                rows[96:97, :], ALU.add, ALU.subtract))
            B.consumed("Wstat", sv, v_var)
            B.wait("s", sv, v_var)
            s_sd = B.emit("s", lambda M: M.activation(
                rows[96:97, :], rows[64:65, :], AF.Sqrt), inc=ss)
            B.wait("v", ss, s_sd)
            def _rstd_lp(M):
                with nc.allow_low_precision(reason="bf16 rstd row"):
                    return M.reciprocal(rstd_row[0:1, :], rows[96:97, :])
            v_rstd = vq(_rstd_lp)
            B.wait("t", sv, v_rstd)
            names = ["A", "B", "C", "D"]
            for i in range(2 * FT):
                ft, which = i // 2, i % 2
                key = names[i % 4]
                ps = PS[key]
                B.wait_slot("t", key)
                if which == 0:
                    tv = B.emit("t", lambda M, ps=ps, ft=ft: M.matmul(
                        ps[:, 0:512], onesf[0:1, :],
                        rows[0:1, ft * 512:(ft + 1) * 512],
                        start=True, stop=True), inc=st)
                else:
                    tv = B.emit("t", lambda M, ps=ps, ft=ft: M.matmul(
                        ps[:, 0:512], onesbf[0:1, 0:128],
                        rstd_row[0:1, ft * 512:(ft + 1) * 512],
                        start=True, stop=True), inc=st)
                B.wait("v", st, tv)
                off = scr_off + T + which * T + ft * 512
                vv = B.emit("v", lambda M, ps=ps, off=off: M.tensor_copy(
                    scr_sb[:, off: off + 512], ps[:, 0:512]), inc=sv)
                B.consumed(key, sv, vv)
            v_ln = None
            for dc in range(DC):
                vq(lambda M, dc=dc: M.tensor_sub(
                    scr_sb[:, scr_off: scr_off + T], xs(dc, 0, T),
                    scr_sb[:, scr_off + T: scr_off + 2 * T]))
                v_ln = vq(lambda M, dc=dc: M.tensor_mul(
                    xlns(dc, 0, T), scr_sb[:, scr_off: scr_off + T],
                    scr_sb[:, scr_off + 2 * T: scr_off + 3 * T]))
            return v_ln, v_rstd

        def emit_router(v_rstd, logits_sb, mtok_sb):
            B.wait("t", sv, v_rstd)
            B.wait_slot("t", "B")
            for tc in range(TC):
                B.emit("t", lambda M, tc=tc: M.matmul(
                    PS_B[:, tc:tc + 1], rows[0:1, tc * 128:(tc + 1) * 128],
                    onesf[0:1, 0:1], start=True, stop=True))
                t_cols = B.emit("t", lambda M, tc=tc: M.matmul(
                    PS_B[:, TC + tc: TC + tc + 1],
                    rstd_row[0:1, tc * 128:(tc + 1) * 128],
                    onesbf[0:1, 0:1], start=True, stop=True),
                    inc=(st if tc == TC - 1 else None))
            B.wait("v", st, t_cols)
            v_cols = vq(lambda M: M.tensor_copy(
                cols[:, 0:2 * TC], PS_B[:, 0:2 * TC]))
            B.consumed("B", sv, v_cols)
            B.wait_slot("t", "Wstat")
            for tc in range(TC):
                for dc in range(DC):
                    t_raw = B.emit("t", lambda M, tc=tc, dc=dc: M.matmul(
                        PS_W[:, tc * E:(tc + 1) * E],
                        xs(dc, tc * 128, (tc + 1) * 128),
                        rw_sb[:, dc * E:(dc + 1) * E],
                        start=(dc == 0), stop=(dc == DC - 1)),
                        inc=(st if (tc == TC - 1 and dc == DC - 1) else None))
            B.wait("v", st, t_raw)
            B.wait("v", sv, v_cols)
            for tc in range(TC):
                vq(lambda M, tc=tc: M.tensor_scalar(
                    logits_sb[:, tc * E:(tc + 1) * E], sb8[:, :],
                    cols[:, tc:tc + 1], None, ALU.mult))
                vq(lambda M, tc=tc: M.tensor_sub(
                    logits_sb[:, tc * E:(tc + 1) * E], PS_W[:, tc * E:(tc + 1) * E],
                    logits_sb[:, tc * E:(tc + 1) * E]))
                vq(lambda M, tc=tc: M.tensor_scalar(
                    logits_sb[:, tc * E:(tc + 1) * E],
                    logits_sb[:, tc * E:(tc + 1) * E],
                    cols[:, TC + tc: TC + tc + 1], None, ALU.mult))
                vq(lambda M, tc=tc: M.tensor_reduce(
                    cols[:, 2 * TC + tc: 2 * TC + tc + 1],
                    logits_sb[:, tc * E:(tc + 1) * E], mybir.AxisListType.X, ALU.max))
            v_logit = vq(lambda M: M.tensor_scalar_mul(
                cols[:, 2 * TC: 3 * TC], cols[:, 2 * TC: 3 * TC], -1.0))
            B.wait("s", sv, v_logit)
            for tc in range(TC):
                s_exp = B.emit("s", lambda M, tc=tc: M.activation(
                    PS_W[:, tc * E:(tc + 1) * E], logits_sb[:, tc * E:(tc + 1) * E],
                    AF.Exp, bias=cols[:, 2 * TC + tc: 2 * TC + tc + 1], scale=1.0),
                    inc=(ss if tc == TC - 1 else None))
            B.wait("v", ss, s_exp)
            for tc in range(TC):
                vq(lambda M, tc=tc: M.tensor_reduce(
                    cols[:, 3 * TC + tc: 3 * TC + tc + 1],
                    PS_W[:, tc * E:(tc + 1) * E], mybir.AxisListType.X, ALU.add))
            v_se = vq(lambda M: M.reciprocal(
                cols[:, 3 * TC: 4 * TC], cols[:, 3 * TC: 4 * TC]))
            v_mtok = None
            for tc in range(TC):
                v_mtok = vq(lambda M, tc=tc: M.tensor_scalar(
                    mtok_sb[:, tc * E:(tc + 1) * E],
                    logits_sb[:, tc * E:(tc + 1) * E],
                    cols[:, 2 * TC + tc: 2 * TC + tc + 1], 0.0,
                    ALU.add, ALU.is_ge))
            B.consumed("Wstat", sv, v_mtok)
            return v_mtok

        def emit_ffn(v_gate, tag):
            C, SR = cfg.C, cfg.SR
            CB = C // 128
            ND = D // 512
            TRASH, ZROW = float(Er * C), float(Er * C + 1)
            ZB = (Er * C // 128) * 128
            with ExitStack() as ffn:
                fc = ffn.enter_context
                hid_sb = fc(nc.sbuf_tensor("hid" + tag, [128, 4 * T], BF16))
                U_sb = fc(nc.sbuf_tensor("U" + tag, [128, DC * 512], BF16))
                G_sb = fc(nc.sbuf_tensor("G" + tag, [128, DC * 512], BF16))
                U2_sb = fc(nc.sbuf_tensor("U2" + tag, [128, DC * 512], BF16))
                G2_sb = fc(nc.sbuf_tensor("G2" + tag, [128, DC * 512], BF16))
                D_sb = fc(nc.sbuf_tensor("Dw" + tag, [128, 4 * D], BF16))
                silu_sb = fc(nc.sbuf_tensor("silu" + tag, [128, 1024], BF16))
                logits_sb = fc(nc.sbuf_tensor("logits" + tag, [128, TC * E], F32))
                mtok_sb = fc(nc.sbuf_tensor("mtok" + tag, [128, TC * E], BF16))
                xg_tok = fc(nc.sbuf_tensor("xgt" + tag, [128, CB * D], BF16))
                xg_fm = fc(nc.sbuf_tensor("xgf" + tag, [128, DC * C], BF16))
                slotout = fc(nc.sbuf_tensor("sout" + tag, [128, CB * D], F32))
                xtt = fc(nc.sbuf_tensor("xtt" + tag, [128, D], BF16))
                rtt = fc(nc.sbuf_tensor("rtt" + tag, [128, D], BF16))
                pfx_sb = fc(nc.sbuf_tensor("pfx" + tag, [128, 2 * TC], F32))
                psc = fc(nc.sbuf_tensor("psc" + tag, [128, 4], F32))
                mpack = fc(nc.sbuf_tensor("mpack" + tag, [128, TC], BF16))
                selm = fc(nc.sbuf_tensor("selm" + tag, [128, TC], F32))
                tmp_sb = fc(nc.sbuf_tensor("tmpx" + tag, [128, TC], F32))
                orow_f = fc(nc.sbuf_tensor("orf" + tag, [128, TC], F32))
                orow_i = fc(nc.sbuf_tensor("ori" + tag, [128, TC], I32))
                brow_i = fc(nc.sbuf_tensor("bri" + tag, [128, TC], I32))
                ltri_sb = fc(nc.sbuf_tensor("ltri" + tag, [128, 128], BF16))
                ltriT_sb = fc(nc.sbuf_tensor("ltriT" + tag, [TC, TC], F32))
                iTf_sb = fc(nc.sbuf_tensor("iTf" + tag, [TC, TC], F32))
                i128_sb = fc(nc.sbuf_tensor("i128" + tag, [128, 128], BF16))

                B.wait("y", sv, v_gate)
                B.emit("y", lambda M: M.dma_start(
                    out=ltri_sb[:, :], in_=dltri[:, :]), inc=sconst, amount=16)
                B.emit("y", lambda M: M.dma_start(
                    out=ltriT_sb[:, :], in_=dltriT[:, :]), inc=sconst, amount=16)
                B.emit("y", lambda M: M.dma_start(
                    out=iTf_sb[:, :], in_=diTf[:, :]), inc=sconst, amount=16)
                B.emit("y", lambda M: M.dma_start(
                    out=i128_sb[:, :], in_=di128[:, :]), inc=sconst, amount=16)
                c_tot = sconst.n

                def wsrc(e):
                    if e < Er:
                        return (dup.ap()[e], dgate.ap()[e], ddown.ap()[e])
                    return (dsup.ap(), dsgate.ap(), dsdown.ap())

                UB, GB = [U_sb, U2_sb], [G_sb, G2_sb]
                order = [Er] + list(range(Er))
                r_units = [(e, hb) for e in range(Er) for hb in range(HB)]
                nr = len(r_units)
                units = [(Er, 0)]
                ri = 0
                for s_hb in range(1, HB):
                    units.append((Er, s_hb))
                    take = (nr * s_hb) // max(HB - 1, 1) - ri
                    units.extend(r_units[ri:ri + take])
                    ri += take
                units.extend(r_units[ri:])

                def load_ug(k):
                    e, hb = units[k]
                    up, gate, _ = wsrc(e)
                    buf = k % 2
                    sem = [su, sg2][buf]
                    s_u = up[:, hb * 512:(hb + 1) * 512].rearrange(
                        "(a p) j -> p a j", p=128)
                    d_u = UB[buf].ap().rearrange("p (a j) -> p a j", a=DC)
                    B.emit("g", lambda M, o=d_u, i=s_u: M.dma_start(out=o, in_=i),
                           inc=sem, amount=16)
                    s_g = gate[:, hb * 512:(hb + 1) * 512].rearrange(
                        "(a p) j -> p a j", p=128)
                    d_g = GB[buf].ap().rearrange("p (a j) -> p a j", a=DC)
                    B.emit("g", lambda M, o=d_g, i=s_g: M.dma_start(out=o, in_=i),
                           inc=sem, amount=16)
                    return sem.n

                def load_d(k):
                    e, hb = units[k]
                    _, _, down = wsrc(e)
                    s_d = down[hb * 512:(hb + 1) * 512, :].rearrange(
                        "(a p) j -> p a j", p=128)
                    d_d = D_sb.ap().rearrange("p (a j) -> p a j", a=4)
                    return B.emit("g", lambda M, o=d_d, i=s_d: M.dma_start(out=o, in_=i),
                                  inc=sd2, amount=16)

                B.wait("g", sv, v_gate)
                ugv = {0: load_ug(0)}
                if len(units) > 1:
                    ugv[1] = load_ug(1)
                dv = load_d(0)

                v_ln, v_rstd = emit_ln(hid_sb, 0, hid_sb, 3 * T)
                v_mtok = emit_router(v_rstd, logits_sb, mtok_sb)

                B.wait("g", B.sem["sbg0"], B.sem["sbg0"].n)
                B.wait("g", B.sem["sbg1"], B.sem["sbg1"].n)

                R = {"v_bi": None, "sc_tot": None,
                     "xg_ready": None}

                def emit_routing():
                    mtok3 = mtok_sb.ap().rearrange("p (t e) -> p t e", e=E)
                    vq(lambda M: M.memset(orow_f[:, :], 0.0))
                    vq(lambda M: M.memset(selm[:, :], 0.0))
                    B.wait("t", sconst, c_tot)
                    B.wait("t", sv, v_mtok)
                    for e in range(Er):
                        vm = vq(lambda M, e=e: M.tensor_copy(
                            mpack[:, :], mtok3[:, :, e]))
                        B.wait("t", sv, vm)
                        B.wait_slot("t", "B")
                        B.emit("t", lambda M: M.matmul(
                            PS_B[:, 0:TC], ltri_sb[:, :], mpack[:, :],
                            start=True, stop=True))
                        t1 = B.emit("t", lambda M: M.matmul(
                            PS_B[0:TC, 64:65], mpack[:, :], onescol[:, 0:1],
                            start=True, stop=True), inc=st)
                        B.wait("v", st, t1)
                        vq(lambda M: M.tensor_copy(pfx_sb[:, 0:TC], PS_B[:, 0:TC]))
                        v2 = vq(lambda M: M.tensor_copy(psc[0:TC, 0:1],
                                                        PS_B[0:TC, 64:65]))
                        B.wait("t", sv, v2)
                        t2 = B.emit("t", lambda M: M.matmul(
                            PS_B[0:TC, 96:97], ltriT_sb[:, :], psc[0:TC, 0:1],
                            start=True, stop=True), inc=st)
                        B.wait("v", st, t2)
                        v3 = vq(lambda M: M.tensor_copy(psc[0:TC, 1:2],
                                                        PS_B[0:TC, 96:97]))
                        B.wait("t", sv, v3)
                        t3 = B.emit("t", lambda M: M.matmul(
                            PS_B[0:1, 128:128 + TC], psc[0:TC, 1:2], iTf_sb[:, :],
                            start=True, stop=True), inc=st)
                        B.wait("v", st, t3)
                        v4 = vq(lambda M: M.tensor_copy(
                            pfx_sb[0:1, TC:2 * TC], PS_B[0:1, 128:128 + TC]))
                        B.wait("t", sv, v4)
                        t4 = B.emit("t", lambda M: M.matmul(
                            PS_B[:, 160:160 + TC], onesf[0:1, :],
                            pfx_sb[0:1, TC:2 * TC], start=True, stop=True), inc=st)
                        B.wait("v", st, t4)
                        v5 = vq(lambda M: M.tensor_copy(
                            pfx_sb[:, TC:2 * TC], PS_B[:, 160:160 + TC]))
                        B.consumed("B", sv, v5)
                        vq(lambda M, e=e: M.scalar_tensor_tensor(
                            tmp_sb[:, :], pfx_sb[:, 0:TC], float(e * C),
                            pfx_sb[:, TC:2 * TC], ALU.add, ALU.add))
                        vq(lambda M: M.tensor_mul(tmp_sb[:, :], tmp_sb[:, :],
                                                  mpack[:, :]))
                        vq(lambda M: M.tensor_add(orow_f[:, :], orow_f[:, :],
                                                  tmp_sb[:, :]))
                        vq(lambda M: M.tensor_add(selm[:, :], selm[:, :],
                                                  mpack[:, :]))
                    vq(lambda M: M.tensor_scalar(selm[:, :], selm[:, :], -1.0, 1.0,
                                                 ALU.mult, ALU.add))
                    vq(lambda M: M.scalar_tensor_tensor(
                        tmp_sb[:, :], selm[:, :], TRASH, orow_f[:, :],
                        ALU.mult, ALU.add))
                    v_oi = vq(lambda M: M.tensor_copy(orow_i[:, :], tmp_sb[:, :]))
                    vq(lambda M: M.scalar_tensor_tensor(
                        tmp_sb[:, :], selm[:, :], ZROW, orow_f[:, :],
                        ALU.mult, ALU.add))
                    v_bi = vq(lambda M: M.tensor_copy(brow_i[:, :], tmp_sb[:, :]))

                    six0 = B.sem["six0"]
                    scat_vals = []
                    for tc in range(TC):
                        par = tc % 2
                        B.wait_slot("t", "W%d" % par)
                        t_tr = None
                        for dc in range(DC):
                            t_tr = B.emit("t", lambda M, par=par, dc=dc, tc=tc:
                                          M.matmul(
                                PS_W[:, par * 1024 + dc * 128: par * 1024 + (dc + 1) * 128],
                                xlns(dc, tc * 128, (tc + 1) * 128), i128_sb[:, :],
                                start=True, stop=True),
                                inc=(st if dc == DC - 1 else None))
                        sb_src = [xtt, rtt][tc % 2]
                        B.wait("v", st, t_tr)
                        if tc >= 2:
                            B.wait("v", six0, scat_vals[tc - 2])
                        v_xt = vq(lambda M, par=par, sb_src=sb_src: M.tensor_copy(
                            sb_src[:, 0:D], PS_W[:, par * 1024: par * 1024 + D]))
                        B.consumed("W%d" % par, sv, v_xt)
                        B.wait("g", sv, max(v_xt, v_oi))
                        if scat_vals:
                            B.wait("g", six0, scat_vals[-1])
                        scat_vals.append(B.emit("g", lambda M, tc=tc, sb_src=sb_src:
                                                M.indirect_dma_start(
                            dslots[:, :],
                            bass.IndirectOffsetOnAxis(orow_i[:, tc:tc + 1], 0),
                            sb_src[:, 0:D], None),
                            inc=six0, amount=16))
                    sc_tot = six0.n
                    R["v_bi"] = v_bi
                    R["sc_tot"] = six0.n
                    issue_xg(routed[0])
                    R["xg_ready"] = sxg_prev


                t_xg_done = 0
                srw_prev = B.sem["srw"].n
                sxg_prev = B.sem["sxg"].n
                routed = [e for e in order if e < Er]

                def issue_xg(e):
                    nonlocal sxg_prev
                    B.wait("g", B.sem["six0"], R["sc_tot"])
                    B.wait("g", st, t_xg_done)
                    B.wait("g", B.sem["sxg"], sxg_prev)
                    sxg_prev = B.emit("g", lambda M, e=e: M.dma_start(
                        out=xg_tok.ap().rearrange("p (a j) -> p a j", a=CB),
                        in_=dslots[e * C:(e + 1) * C, :].rearrange(
                            "(a p) j -> p a j", p=128)),
                        inc=B.sem["sxg"], amount=16)

                for k, (e, hb) in enumerate(units):
                    Ub, Gb = UB[k % 2], GB[k % 2]
                    if e == Er:
                        if hb == 0:
                            B.wait("t", sv, v_ln)
                        B.wait("t", [su, sg2][k % 2], ugv[k])
                        tiles = [(h, f) for h in range(4) for f in range(FT)]
                        for i, (hc, ft) in enumerate(tiles):
                            kU = ["A", "C"][i % 2]
                            kG = ["B", "D"][i % 2]
                            psU, psG = PS[kU], PS[kG]
                            sl = i % 2
                            B.wait_slot("t", kU)
                            B.wait_slot("t", kG)
                            for dc in range(DC):
                                B.emit("t", lambda M, psU=psU, Ub=Ub, dc=dc, hc=hc, ft=ft:
                                       M.matmul(
                                    psU[:, 0:512],
                                    Ub[:, dc * 512 + hc * 128: dc * 512 + (hc + 1) * 128],
                                    xlns(dc, ft * 512, (ft + 1) * 512),
                                    start=(dc == 0), stop=(dc == DC - 1)))
                            for dc in range(DC):
                                tA = B.emit("t", lambda M, psG=psG, Gb=Gb, dc=dc, hc=hc, ft=ft:
                                            M.matmul(
                                    psG[:, 0:512],
                                    Gb[:, dc * 512 + hc * 128: dc * 512 + (hc + 1) * 128],
                                    xlns(dc, ft * 512, (ft + 1) * 512),
                                    start=(dc == 0), stop=(dc == DC - 1)),
                                    inc=(st if dc == DC - 1 else None))
                            B.wait("s", st, tA)
                            sS = B.emit("s", lambda M, psG=psG, sl=sl: M.activation(
                                silu_sb[:, sl * 512:(sl + 1) * 512], psG[:, 0:512],
                                AF.Silu), inc=ss)
                            B.wait("v", ss, sS)
                            vv = B.emit("v", lambda M, psU=psU, sl=sl, hc=hc, ft=ft:
                                        M.tensor_mul(
                                hid_sb[:, hc * T + ft * 512: hc * T + (ft + 1) * 512],
                                psU[:, 0:512],
                                silu_sb[:, sl * 512:(sl + 1) * 512]), inc=sv)
                            B.consumed(kU, sv, vv)
                            B.consumed(kG, sv, vv)
                            v_hid = vv
                        if k + 2 < len(units):
                            B.wait("g", st, tA)
                            ugv[k + 2] = load_ug(k + 2)
                        ugv.pop(k)
                        B.wait("t", sd2, dv)
                        B.wait("t", sv, v_hid)
                        for dco in range(DC):
                            for half in range(NHALF):
                                B.wait_slot("t", "W%d" % half)
                                for hc in range(4):
                                    for q in range(NQ):
                                        c0 = half * WH + q * 512
                                        tB = B.emit("t",
                                                    lambda M, dco=dco, hc=hc, c0=c0:
                                                    M.matmul(
                                            PS_W[:, c0:c0 + 512],
                                            D_sb[:, hc * D + dco * 128: hc * D + (dco + 1) * 128],
                                            hid_sb[:, hc * T + c0: hc * T + c0 + 512],
                                            start=(hc == 0), stop=(hc == 3)),
                                            inc=(st if (hc == 3 and q == NQ - 1) else None))
                                B.wait("v", st, tB)
                                vv = B.emit("v", lambda M, dco=dco, half=half:
                                            M.scalar_tensor_tensor(
                                    xs(dco, half * WH, (half + 1) * WH),
                                    PS_W[:, half * WH:(half + 1) * WH], 0.5,
                                    xs(dco, half * WH, (half + 1) * WH),
                                    ALU.mult, ALU.add), inc=sv)
                                B.consumed("W%d" % half, sv, vv)
                        if k + 1 < len(units):
                            B.wait("g", st, st.n)
                            dv = load_d(k + 1)
                        if k == 0:
                            emit_routing()
                        continue

                    if hb == 0:
                        B.wait("t", B.sem["sxg"], R["xg_ready"])
                        for a in range(CB):
                            par = a % 2
                            B.wait_slot("t", "W%d" % par)
                            for dc in range(DC):
                                t_xg_done = B.emit("t", lambda M, a=a, dc=dc, par=par:
                                                   M.matmul(
                                    PS_W[:, par * 1024 + dc * 128: par * 1024 + (dc + 1) * 128],
                                    xg_tok[:, a * D + dc * 128: a * D + (dc + 1) * 128],
                                    i128_sb[:, :], start=True, stop=True),
                                    inc=(st if dc == DC - 1 else None))
                            B.wait("v", st, t_xg_done)
                            dst3 = xg_fm.ap().rearrange(
                                "p (b j) -> p b j", j=C)[:, :, a * 128:(a + 1) * 128]
                            src3 = PS_W.ap().rearrange(
                                "p (b j) -> p b j", j=128)[:, par * 8: par * 8 + DC, :]
                            v_xgf = vq(lambda M, dst3=dst3, src3=src3: M.tensor_copy(
                                dst3, src3))
                            B.consumed("W%d" % par, sv, v_xgf)
                        ri = routed.index(e)
                        if ri + 1 < len(routed):
                            issue_xg(routed[ri + 1])
                            R["xg_ready"] = sxg_prev
                    B.wait("t", [su, sg2][k % 2], ugv[k])
                    if hb == 0:
                        B.wait("t", sv, v_xgf)
                    for hc in range(4):
                        kU = ["A", "C"][hc % 2]
                        kG = ["B", "D"][hc % 2]
                        psU, psG = PS[kU], PS[kG]
                        sl = hc % 2
                        B.wait_slot("t", kU)
                        B.wait_slot("t", kG)
                        for dc in range(DC):
                            B.emit("t", lambda M, psU=psU, Ub=Ub, dc=dc, hc=hc: M.matmul(
                                psU[:, 0:C],
                                Ub[:, dc * 512 + hc * 128: dc * 512 + (hc + 1) * 128],
                                xg_fm[:, dc * C:(dc + 1) * C],
                                start=(dc == 0), stop=(dc == DC - 1)))
                        for dc in range(DC):
                            tA = B.emit("t", lambda M, psG=psG, Gb=Gb, dc=dc, hc=hc:
                                        M.matmul(
                                psG[:, 0:C],
                                Gb[:, dc * 512 + hc * 128: dc * 512 + (hc + 1) * 128],
                                xg_fm[:, dc * C:(dc + 1) * C],
                                start=(dc == 0), stop=(dc == DC - 1)),
                                inc=(st if dc == DC - 1 else None))
                        B.wait("s", st, tA)
                        sS = B.emit("s", lambda M, psG=psG, sl=sl: M.activation(
                            silu_sb[:, sl * 512: sl * 512 + C], psG[:, 0:C],
                            AF.Silu), inc=ss)
                        B.wait("v", ss, sS)
                        vv = B.emit("v", lambda M, psU=psU, sl=sl, hc=hc:
                                    M.tensor_mul(
                            hid_sb[:, hc * C:(hc + 1) * C], psU[:, 0:C],
                            silu_sb[:, sl * 512: sl * 512 + C]), inc=sv)
                        B.consumed(kU, sv, vv)
                        B.consumed(kG, sv, vv)
                        v_hid = vv
                    if k + 2 < len(units):
                        B.wait("g", st, tA)
                        ugv[k + 2] = load_ug(k + 2)
                    ugv.pop(k)
                    B.wait("t", sd2, dv)
                    B.wait("t", sv, v_hid)
                    if hb == 0:
                        B.wait("v", B.sem["srw"], srw_prev)
                    regions = [(s_, d_) for s_ in range(CB)
                               for d_ in range(ND)]
                    for r, (sc_, dh) in enumerate(regions):
                        key = "ABCD"[r % 4]
                        ps = PS[key]
                        B.wait_slot("t", key)
                        for hc in range(4):
                            tB = B.emit("t",
                                        lambda M, ps=ps, sc_=sc_, dh=dh, hc=hc:
                                        M.matmul(
                                ps[:, 0:512],
                                hid_sb[:, hc * C + sc_ * 128: hc * C + sc_ * 128 + 128],
                                D_sb[:, hc * D + dh * 512: hc * D + dh * 512 + 512],
                                start=(hc == 0), stop=(hc == 3)),
                                inc=(st if hc == 3 else None))
                        B.wait("v", st, tB)
                        off = sc_ * D + dh * 512
                        if hb == 0:
                            vv = B.emit("v", lambda M, ps=ps, off=off:
                                        M.tensor_copy(
                                slotout[:, off:off + 512], ps[:, 0:512]),
                                inc=sv)
                        else:
                            vv = B.emit("v", lambda M, ps=ps, off=off:
                                        M.tensor_add(
                                slotout[:, off:off + 512],
                                slotout[:, off:off + 512], ps[:, 0:512]),
                                inc=sv)
                        B.consumed(key, sv, vv)
                        v_sout = vv
                    if k + 1 < len(units):
                        B.wait("g", st, st.n)
                        dv = load_d(k + 1)
                    if hb == HB - 1:
                        B.wait("g", sv, v_sout)
                        srw_prev = B.emit("g", lambda M, e=e: M.dma_start(
                            out=dslots[e * C:(e + 1) * C, :].rearrange(
                                "(a p) j -> p a j", p=128),
                            in_=slotout.ap().rearrange("p (a j) -> p a j", a=CB)),
                            inc=B.sem["srw"], amount=16)

                sbg = [B.sem["sbg0"], B.sem["sbg1"]]
                sbg_prev = [sbg[0].n, sbg[1].n]
                bb = [rtt, xtt]
                t_bw_par = [0, 0]
                t_bw = 0
                v_out = 0
                for tc in range(TC):
                    par = tc % 2
                    B.wait("g", B.sem["srw"], B.sem["srw"].n)
                    B.wait("g", sv, R["v_bi"])
                    B.wait("g", st, t_bw_par[par])
                    B.wait("g", sbg[par], sbg_prev[par])
                    sbg_prev[par] = B.emit("g", lambda M, tc=tc, par=par:
                                           M.indirect_dma_start(
                        bb[par][:, 0:D], None,
                        dslots[:, :],
                        bass.IndirectOffsetOnAxis(brow_i[:, tc:tc + 1], 0)),
                        inc=sbg[par], amount=16)
                    B.wait("v", sbg[par], sbg_prev[par])
                    v_sc = vq(lambda M, tc=tc, par=par: M.tensor_scalar(
                        bb[par][:, 0:D], bb[par][:, 0:D],
                        cols[:, 3 * TC + tc: 3 * TC + tc + 1], None, ALU.mult))
                    B.wait("t", sv, v_sc)
                    B.wait_slot("t", "W%d" % par)
                    for dc in range(DC):
                        t_bw = B.emit("t", lambda M, par=par, dc=dc: M.matmul(
                            PS_W[:, par * 1024 + dc * 128: par * 1024 + (dc + 1) * 128],
                            bb[par][:, dc * 128:(dc + 1) * 128],
                            i128_sb[:, :], start=True, stop=True),
                            inc=(st if dc == DC - 1 else None))
                    t_bw_par[par] = t_bw
                    B.wait("v", st, t_bw)
                    xdst = x_sb.ap().rearrange("p (b j) -> p b j", j=T)[
                        :, :, tc * 128:(tc + 1) * 128]
                    psrc = PS_W.ap().rearrange("p (b j) -> p b j", j=128)[
                        :, par * 8: par * 8 + DC, :]
                    v_out = vq(lambda M, xdst=xdst, psrc=psrc: M.tensor_add(
                        xdst, xdst, psrc))
                    B.consumed("W%d" % par, sv, v_out)
                B.consumed("Wstat", sv, v_out)
                return v_out

        def emit_attn(v_gate):
            with ExitStack() as att:
                ac = att.enter_context
                W0 = ac(nc.sbuf_tensor("W0", [128, DC * D], BF16))
                W1 = ac(nc.sbuf_tensor("W1", [128, DC * D], BF16))
                K_tile = ac(nc.sbuf_tensor("K_tile", [128, 2 * D], BF16))
                V_tile = ac(nc.sbuf_tensor("V_tile", [128, 2 * D], BF16))
                ascr = ac(nc.sbuf_tensor("ascr", [128, DC * T], BF16))
                recip_sb = ac(nc.sbuf_tensor("recip_sb", [NH, T], BF16))
                cc_sb = ac(nc.sbuf_tensor("cc_sb", [128, NP * 64 + NP], F32))
                kvbf_sb = ac(nc.sbuf_tensor("kvbf_sb", [128, NP * 64], BF16))
                zbd_sb = ac(nc.sbuf_tensor("zbd_sb", [128, DC * NH], BF16))
                e16_sb = ac(nc.sbuf_tensor("e16_sb", [NH, NP * 128], BF16))
                bias_at = ac(nc.sbuf_tensor("bias_at", [1, 2 * D], BF16))
                sc1 = ac(nc.sbuf_tensor("sc1a", [128, 512], BF16))
                sc2 = ac(nc.sbuf_tensor("sc2a", [128, 512], BF16))

                def load_w(dw, W_t, sem):
                    s_w = dw.ap().rearrange("(a p) j -> p a j", p=128)
                    d_w = W_t.ap().rearrange("p (a j) -> p a j", a=DC)
                    return B.emit("g", lambda M, o=d_w, i=s_w: M.dma_start(
                        out=o, in_=i), inc=sem, amount=16)

                B.wait("y", sv, v_gate)
                y_e16 = B.emit("y", lambda M: M.dma_start(
                    out=e16_sb[:, :], in_=de16[:, :]), inc=se16, amount=16)
                B.wait("g", sv, v_gate)
                w_k = load_w(dwk, W0, sw)
                w_v = load_w(dwv, W1, B.sem["swb"])
                B.emit("g", lambda M: M.dma_start(
                    out=bias_at[0:1, 0:D], in_=dbk[:, :]), inc=sconsg, amount=16)
                B.emit("g", lambda M: M.dma_start(
                    out=bias_at[0:1, D:2 * D], in_=dbv[:, :]), inc=sconsg, amount=16)
                bias_kv = sconsg.n

                v_ln2, _ = emit_ln(ascr, 0, ascr, 3 * T)
                v_ccz = vq(lambda M: M.memset(cc_sb[:, :], 0.0))

                B.wait("t", sw, w_k)
                B.wait("t", B.sem["swb"], w_v)
                B.wait("t", sconsg, bias_kv)
                B.wait("t", sv, v_ln2)
                B.wait_slot("t", "A")
                B.wait_slot("t", "B")
                B.wait_slot("t", "C")
                B.wait_slot("t", "Wstat")
                t_kvtc = [0] * TC
                for tc in range(TC):
                    half = tc % 2
                    B.wait_slot("t", "WpK")
                    tvk = 0
                    for os in range(NOS):
                        c0 = os * OSW
                        for dc in range(DC):
                            B.emit("t", lambda M, dc=dc, tc=tc, c0=c0: M.matmul(
                                PS_W[:, c0:c0 + OSW],
                                xlns(dc, tc * 128, (tc + 1) * 128),
                                W0[:, dc * D + c0: dc * D + c0 + OSW],
                                start=(dc == 0), stop=False))
                        tvk = B.emit("t", lambda M, c0=c0: M.matmul(
                            PS_W[:, c0:c0 + OSW], onesbf[0:1, 0:128],
                            bias_at[0:1, c0:c0 + OSW],
                            start=False, stop=True),
                            inc=(st if os == NOS - 1 else None))
                    vphi = 0
                    for os in range(NOS):
                        c0 = os * OSW
                        dk = K_tile[:, half * D + c0: half * D + c0 + OSW]
                        B.wait("v", st, tvk)
                        if tc >= 2:
                            B.wait("v", st, t_kvtc[tc - 2])
                        v1 = vq(lambda M, c0=c0: M.tensor_scalar_min(
                            sc1[:, 0:OSW], PS_W[:, c0:c0 + OSW], 0.0))
                        B.wait("s", sv, v1)
                        s1 = B.emit("s", lambda M: M.activation(
                            sc2[:, 0:OSW], sc1[:, 0:OSW], AF.Exp), inc=ss)
                        B.wait("v", ss, s1)
                        vphi = vq(lambda M, c0=c0, dk=dk: M.scalar_tensor_tensor(
                            dk, PS_W[:, c0:c0 + OSW], 0.0,
                            sc2[:, 0:OSW], ALU.max, ALU.add))
                    B.consumed("WpK", sv, vphi)
                    B.wait_slot("t", "WpV")
                    tvv = 0
                    for os in range(NOS):
                        c0 = D + os * OSW
                        for dc in range(DC):
                            B.emit("t", lambda M, dc=dc, tc=tc, c0=c0: M.matmul(
                                PS_W[:, c0:c0 + OSW],
                                xlns(dc, tc * 128, (tc + 1) * 128),
                                W1[:, dc * D + c0 - D: dc * D + c0 - D + OSW],
                                start=(dc == 0), stop=False))
                        tvv = B.emit("t", lambda M, c0=c0: M.matmul(
                            PS_W[:, c0:c0 + OSW], onesbf[0:1, 0:128],
                            bias_at[0:1, c0:c0 + OSW],
                            start=False, stop=True),
                            inc=(st if os == NOS - 1 else None))
                    vcv = 0
                    for os in range(NOS):
                        c0 = D + os * OSW
                        dv = V_tile[:, half * D + os * OSW: half * D + (os + 1) * OSW]
                        B.wait("v", st, tvv)
                        if tc >= 2:
                            B.wait("v", st, t_kvtc[tc - 2])
                        vcv = vq(lambda M, c0=c0, dv=dv: M.tensor_copy(
                            dv, PS_W[:, c0:c0 + OSW]))
                    B.consumed("WpV", sv, vcv)
                    B.wait("t", sv, vcv)
                    if tc > 0:
                        B.wait("t", sv, v_kvadd)
                    for p in range(NP):
                        b0 = half * D + p * 128
                        B.emit("t", lambda M, p=p, b0=b0: M.matmul(
                            PS_A[0:64, p * 64:(p + 1) * 64],
                            K_tile[:, b0: b0 + 64], V_tile[:, b0: b0 + 64],
                            start=True, stop=True))
                        B.emit("t", lambda M, p=p, b0=b0: M.matmul(
                            PS_B[64:128, p * 64:(p + 1) * 64],
                            K_tile[:, b0 + 64: b0 + 128],
                            V_tile[:, b0 + 64: b0 + 128],
                            start=True, stop=True))
                        t_kvtc[tc] = B.emit("t", lambda M, p=p, b0=b0: M.matmul(
                            PS_C[:, p:p + 1],
                            K_tile[:, b0: b0 + 128], onescol[:, 0:1],
                            start=True, stop=True),
                            inc=(st if p == NP - 1 else None))
                    B.wait("v", st, t_kvtc[tc])
                    if tc == 0:
                        B.wait("v", sv, v_ccz)
                    vq(lambda M: M.tensor_add(
                        cc_sb[0:64, 0:NP * 64], cc_sb[0:64, 0:NP * 64],
                        PS_A[0:64, 0:NP * 64]))
                    vq(lambda M: M.tensor_add(
                        cc_sb[64:128, 0:NP * 64], cc_sb[64:128, 0:NP * 64],
                        PS_B[64:128, 0:NP * 64]))
                    v_kvadd = vq(lambda M: M.tensor_add(
                        cc_sb[:, NP * 64: NP * 64 + NP],
                        cc_sb[:, NP * 64: NP * 64 + NP], PS_C[:, 0:NP]))
                t_kv = t_kvtc[TC - 1]
                v_cc = v_kvadd

                B.wait("g", st, t_kv)
                w_q = load_w(dwq, W0, sw)
                w_o = load_w(dwo, W1, B.sem["swb"])
                B.emit("g", lambda M: M.dma_start(
                    out=bias_at[0:1, 0:D], in_=dbq[:, :]), inc=sconsg, amount=16)
                B.emit("g", lambda M: M.dma_start(
                    out=bias_at[0:1, D:2 * D], in_=dbo[:, :]), inc=sconsg, amount=16)
                bias_qo = sconsg.n

                B.consumed("A", sv, v_cc)
                B.consumed("B", sv, v_cc)
                B.consumed("C", sv, v_cc)
                B.wait("y", sv, v_cc)
                y_ccin = B.emit("y", lambda M: M.dma_start(
                    out=dccin[:, :], in_=cc_sb[:, :]), inc=sci, amount=16)
                B.wait("g", sci, y_ccin)
                g_cc = B.emit("g", lambda M: M.collective_compute(
                    "AllReduce", ALU.add,
                    replica_groups=[list(range(0, cfg.group)),
                                    list(range(cfg.group, 2 * cfg.group))],
                    ins=[dccin.ap().opt()], outs=[dccout.ap().opt()]), inc=scc)
                B.wait("g", scc, g_cc)
                g_ccr = B.emit("g", lambda M: M.dma_start(
                    out=cc_sb[:, :], in_=dccout[:, :]), inc=sgd, amount=16)

                phiQ = ascr
                B.wait("t", sw, w_q)
                B.wait("t", sconsg, bias_qo)
                t_q = 0
                v_q = 0
                for i, (oc, ft) in enumerate(
                        [(o, f) for o in range(DC) for f in range(FT)]):
                    key = ["C", "D"][i % 2]
                    ps = PS[key]
                    B.wait_slot("t", key)
                    for dc in range(DC):
                        B.emit("t", lambda M, ps=ps, dc=dc, oc=oc, ft=ft: M.matmul(
                            ps[:, 0:512],
                            W0[:, dc * D + oc * 128: dc * D + (oc + 1) * 128],
                            xlns(dc, ft * 512, (ft + 1) * 512),
                            start=(dc == 0), stop=False))
                    t_q = B.emit("t", lambda M, ps=ps, oc=oc: M.matmul(
                        ps[:, 0:512], bias_at[0:1, oc * 128:(oc + 1) * 128],
                        onesbf[0:1, :], start=False, stop=True), inc=st)
                    B.wait("v", st, t_q)
                    v1 = vq(lambda M, ps=ps: M.tensor_scalar_min(
                        sc1[:, :], ps[:, 0:512], 0.0))
                    B.wait("s", sv, v1)
                    s1 = B.emit("s", lambda M: M.activation(
                        sc2[:, :], sc1[:, :], AF.Exp), inc=ss)
                    B.wait("v", ss, s1)
                    v_q = vq(lambda M, ps=ps, oc=oc, ft=ft: M.scalar_tensor_tensor(
                        phiQ[:, oc * T + ft * 512: oc * T + (ft + 1) * 512],
                        ps[:, 0:512], 0.0, sc2[:, :], ALU.max, ALU.add))
                    B.consumed(key, sv, v_q)

                B.wait("v", sgd, g_ccr)
                vq(lambda M: M.tensor_copy(
                    kvbf_sb[:, :], cc_sb[:, 0:NP * 64]))
                vq(lambda M: M.memset(zbd_sb[:, :], 0.0))
                v_kvz = 0
                for dc in range(DC):
                    vq(lambda M, dc=dc: M.tensor_copy(
                        zbd_sb[0:64, dc * NH + 2 * dc: dc * NH + 2 * dc + 1],
                        cc_sb[0:64, NP * 64 + dc: NP * 64 + dc + 1]))
                    v_kvz = vq(lambda M, dc=dc: M.tensor_copy(
                        zbd_sb[64:128, dc * NH + 2 * dc + 1: dc * NH + 2 * dc + 2],
                        cc_sb[64:128, NP * 64 + dc: NP * 64 + dc + 1]))

                B.wait("t", sv, max(v_kvz, v_q))
                B.wait_slot("t", "Wstat")
                t_nrm = 0
                for qq in range(FT):
                    for dc in range(DC):
                        t_nrm = B.emit("t", lambda M, qq=qq, dc=dc: M.matmul(
                            PS_W[0:NH, qq * 512:(qq + 1) * 512],
                            zbd_sb[:, dc * NH:(dc + 1) * NH],
                            phiQ[:, dc * T + qq * 512: dc * T + (qq + 1) * 512],
                            start=(dc == 0), stop=(dc == DC - 1)),
                            inc=(st if (qq == FT - 1 and dc == DC - 1) else None))
                B.wait("v", st, t_nrm)
                vq(lambda M: M.tensor_scalar(
                    PS_W[32:32 + NH, 0:T], PS_W[0:NH, 0:T],
                    float(cfg.attn_eps), None, ALU.add))

                def _recip_lp(M):
                    with nc.allow_low_precision(reason="bf16 attn norm recip"):
                        return M.reciprocal(recip_sb[:, :], PS_W[32:32 + NH, 0:T])
                v_rec = vq(_recip_lp)
                B.consumed("Wstat", sv, v_rec)

                B.wait("t", sv, v_rec)
                B.wait("t", se16, y_e16)
                okeys = [("A", "B"), ("C", "D")]
                v_fm = 0
                for i, (p, ft) in enumerate(
                        [(p, f) for p in range(NP) for f in range(FT)]):
                    kO, kR = okeys[i % 2]
                    psO, psR = PS[kO], PS[kR]
                    B.wait_slot("t", kO)
                    B.wait_slot("t", kR)
                    B.emit("t", lambda M, psO=psO, p=p, ft=ft: M.matmul(
                        psO[0:64, 0:512], kvbf_sb[0:64, p * 64:(p + 1) * 64],
                        phiQ[0:64, p * T + ft * 512: p * T + (ft + 1) * 512],
                        start=True, stop=True))
                    B.emit("t", lambda M, psO=psO, p=p, ft=ft: M.matmul(
                        psO[64:128, 0:512], kvbf_sb[64:128, p * 64:(p + 1) * 64],
                        phiQ[64:128, p * T + ft * 512: p * T + (ft + 1) * 512],
                        start=True, stop=True))
                    tv = B.emit("t", lambda M, psR=psR, p=p, ft=ft: M.matmul(
                        psR[:, 0:512], e16_sb[:, p * 128:(p + 1) * 128],
                        recip_sb[:, ft * 512:(ft + 1) * 512],
                        start=True, stop=True), inc=st)
                    B.wait("v", st, tv)
                    vq(lambda M, psR=psR: M.tensor_copy(
                        sc1[:, :], psR[:, 0:512]))
                    v_fm = vq(lambda M, psO=psO, p=p, ft=ft: M.tensor_mul(
                        xlns(p, ft * 512, (ft + 1) * 512), psO[:, 0:512],
                        sc1[:, :]))
                    B.consumed(kO, sv, v_fm)
                    B.consumed(kR, sv, v_fm)

                B.wait("t", B.sem["swb"], w_o)
                B.wait("t", sv, v_fm)
                v_out = 0
                for i, (oc, ft) in enumerate(
                        [(o, f) for o in range(DC) for f in range(FT)]):
                    key = ["C", "D"][i % 2]
                    ps = PS[key]
                    B.wait_slot("t", key)
                    for dc in range(DC):
                        B.emit("t", lambda M, ps=ps, dc=dc, oc=oc, ft=ft: M.matmul(
                            ps[:, 0:512],
                            W1[:, dc * D + oc * 128: dc * D + (oc + 1) * 128],
                            xlns(dc, ft * 512, (ft + 1) * 512),
                            start=(dc == 0), stop=False))
                    tv = B.emit("t", lambda M, ps=ps, oc=oc: M.matmul(
                        ps[:, 0:512], bias_at[0:1, D + oc * 128: D + (oc + 1) * 128],
                        onesbf[0:1, :], start=False, stop=True), inc=st)
                    B.wait("v", st, tv)
                    v_out = B.emit("v", lambda M, ps=ps, oc=oc, ft=ft: M.tensor_add(
                        xs(oc, ft * 512, (ft + 1) * 512),
                        xs(oc, ft * 512, (ft + 1) * 512), ps[:, 0:512]), inc=sv)
                    B.consumed(key, sv, v_out)
                return v_out

        B.wait("s", sx, y_xload)
        B.wait("t", sx, y_xload)
        v1 = emit_ffn(v_sb8, "_f1")
        B.wait("s", sv, v1)
        v2 = emit_attn(v1)
        B.wait("s", sv, v2)
        v3 = emit_ffn(v2, "_f2")

        B.wait("y", sv, v3)
        osrc = x_sb.ap().rearrange("p (a j) -> p a j", a=DC)
        odst = dy.ap().rearrange("(a p) j -> p a j", p=128)
        y_out = B.emit("y", lambda M, o=odst, i=osrc: M.dma_start(out=o, in_=i),
                       inc=syo, amount=16)
        B.wait("y", syo, y_out)

        B.finalize()
    return nc


def _host_constants(cfg):
    NH, NP, TC = cfg.NH, cfg.NP, cfg.TC
    i128 = np.eye(128, dtype=np.float32).astype(BF)
    e16 = np.zeros((NH, NP * 128), dtype=BF)
    for p in range(NP):
        e16[2 * p, p * 128: p * 128 + 64] = 1.0
        e16[2 * p + 1, p * 128 + 64: (p + 1) * 128] = 1.0
    ltri = np.triu(np.ones((128, 128), np.float32), 1).astype(BF)
    ltriT = np.triu(np.ones((TC, TC), np.float32), 1)
    iTf = np.eye(TC, dtype=np.float32)
    return i128, e16, ltri, ltriT, iTf


def stage_inputs(cfg, x, Wq, bq, Wk, bk, Wv, bv, Wo, bo, rw, upw, gatew, downw,
                 sup, sgate, sdown, n_cores=8):
    f32 = np.float32
    x = np.asarray(x, f32)
    Bb = x.shape[0]
    per_batch = n_cores // Bb
    i128, e16, ltri, ltriT, iTf = _host_constants(cfg)
    common = {
        "wq": np.ascontiguousarray(np.asarray(Wq, f32).T),
        "wk": np.ascontiguousarray(np.asarray(Wk, f32).T),
        "wv": np.ascontiguousarray(np.asarray(Wv, f32).T),
        "wo": np.ascontiguousarray(np.asarray(Wo, f32).T),
        "bq": np.asarray(bq, f32).reshape(1, -1),
        "bk": np.asarray(bk, f32).reshape(1, -1),
        "bv": np.asarray(bv, f32).reshape(1, -1),
        "bo": np.asarray(bo, f32).reshape(1, -1),
        "rw": np.ascontiguousarray(np.asarray(rw, f32)),
        "upw": np.ascontiguousarray(np.asarray(upw, f32)),
        "gatew": np.ascontiguousarray(np.asarray(gatew, f32)),
        "downw": np.ascontiguousarray(np.asarray(downw, f32)),
        "sup": np.ascontiguousarray(np.asarray(sup, f32)),
        "sgate": np.ascontiguousarray(np.asarray(sgate, f32)),
        "sdown": np.ascontiguousarray(np.asarray(sdown, f32)),
        "i128": i128, "e16": e16, "ltri": ltri, "ltriT": ltriT, "iTf": iTf,
        "slots": np.zeros((cfg.SR, cfg.D), dtype=BF),
    }
    in_maps = []
    for c in range(n_cores):
        b = c // per_batch
        t0 = (c % per_batch) * cfg.T
        m = dict(common)
        m["x"] = np.ascontiguousarray(x[b, t0:t0 + cfg.T, :].T)
        in_maps.append(m)
    return in_maps


def gather_out(cfg, results, Bb, S, D, n_cores=8):
    per_batch = n_cores // Bb
    out = np.empty((Bb, S, D), dtype=np.float32)
    for c in range(n_cores):
        b = c // per_batch
        t0 = (c % per_batch) * cfg.T
        out[b, t0:t0 + cfg.T, :] = results[c]["y"].T
    return out


def kernel(L0_attn_x, L0_attn_Wq, L0_attn_bq, L0_attn_Wk, L0_attn_bk,
           L0_attn_Wv, L0_attn_bv, L0_attn_Wo, L0_attn_bo,
           L0_ffn_router_w, L0_ffn_up_w, L0_ffn_gate_w, L0_ffn_down_w,
           L0_ffn_shared_up_w, L0_ffn_shared_gate_w, L0_ffn_shared_down_w):
    x = np.asarray(L0_attn_x, dtype=np.float32)
    Bb, S, D = x.shape
    upw = np.asarray(L0_ffn_up_w)
    cfg = Cfg(T=(Bb * S) // 8, D=D, HF=upw.shape[2], Er=upw.shape[0],
              E=np.asarray(L0_ffn_router_w).shape[1], group=8 // Bb)
    in_maps = stage_inputs(cfg, x, L0_attn_Wq, L0_attn_bq, L0_attn_Wk,
                           L0_attn_bk, L0_attn_Wv, L0_attn_bv, L0_attn_Wo,
                           L0_attn_bo, L0_ffn_router_w, L0_ffn_up_w,
                           L0_ffn_gate_w, L0_ffn_down_w, L0_ffn_shared_up_w,
                           L0_ffn_shared_gate_w, L0_ffn_shared_down_w)
    nc = build_nc(cfg)
    res = run_bass_kernel_spmd(nc, in_maps, core_ids=list(range(8)))
    return gather_out(cfg, res.results, Bb, S, D)


# version 09
# speedup vs baseline: 2.5318x; 1.1040x over previous
from contextlib import ExitStack

import numpy as np
import ml_dtypes

import concourse.bass as bass
import concourse.mybir as mybir
from concourse.bass_utils import run_bass_kernel_spmd

F32 = mybir.dt.float32
BF16 = mybir.dt.bfloat16
I32 = mybir.dt.int32
AF = mybir.ActivationFunctionType
ALU = mybir.AluOpType
BF = ml_dtypes.bfloat16


class Cfg:
    def __init__(self, T=2048, D=1024, HF=4096, Er=7, E=8, group=4,
                 ln_eps=1e-5, attn_eps=1e-6):
        assert T % 512 == 0 and D % 128 == 0 and HF % 512 == 0
        self.T, self.D, self.HF, self.Er, self.E = T, D, HF, Er, E
        self.DC = D // 128
        self.TC = T // 128
        self.FT = T // 512
        self.HB = HF // 512
        self.NH = D // 64
        self.NP = D // 128
        self.group = group
        self.ln_eps, self.attn_eps = ln_eps, attn_eps
        self.C = max(128, ((3 * T // 16) + 127) // 128 * 128)
        self.SR = ((Er * self.C + 2 + 127) // 128) * 128


class Sem:
    def __init__(self, nc, name):
        self.h = nc.alloc_semaphore(name)
        self.n = 0

    def bump(self, amount=1):
        self.n += amount
        return self.n


class Builder:
    ENGS = ("t", "v", "s", "g", "y")

    def __init__(self, nc):
        self.nc = nc
        self.q = {e: [] for e in self.ENGS}
        self.sem = {k: Sem(nc, k) for k in
                    ("st", "sv", "ss", "scc", "sgd", "syd", "su", "sg2", "sd2",
                     "sw", "swb", "sx", "sconst", "sconsg", "se16", "sci", "syo",
                     "szf", "six0", "six1", "sxg", "srw", "sbg0", "sbg1")}
        self.cons = {}

    def wait(self, eng, sem, val):
        if val is None or val <= 0:
            return
        h = sem.h
        self.q[eng].append(lambda M: M.wait_ge(h, val))

    def wait_slot(self, eng, key):
        c = self.cons.get(key)
        if c is not None:
            self.wait(eng, c[0], c[1])

    def consumed(self, key, sem, val):
        self.cons[key] = (sem, val)

    def emit(self, eng, fn, inc=None, amount=1):
        if inc is None:
            self.q[eng].append(fn)
            return None
        val = inc.bump(amount)
        h = inc.h

        def L(M, fn=fn, h=h, amount=amount):
            fn(M).then_inc(h, amount)
        self.q[eng].append(L)
        return val

    def finalize(self):
        with self.nc.Block() as block:
            @block.tensor
            def _(E):
                for f in self.q["t"]:
                    f(E)

            @block.vector
            def _(E):
                for f in self.q["v"]:
                    f(E)

            @block.scalar
            def _(E):
                for f in self.q["s"]:
                    f(E)

            @block.gpsimd
            def _(E):
                for f in self.q["g"]:
                    f(E)

            @block.sync
            def _(E):
                for f in self.q["y"]:
                    f(E)


def build_nc(cfg: Cfg):
    nc = bass.Bass("TRN2", target_bir_lowering=False)
    T, D, HF, Er, E = cfg.T, cfg.D, cfg.HF, cfg.Er, cfg.E
    DC, TC, FT, HB, NH, NP = cfg.DC, cfg.TC, cfg.FT, cfg.HB, cfg.NH, cfg.NP
    WH = min(1024, T)
    NHALF = T // WH
    NQ = WH // 512
    OSW = min(512, D)
    NOS = D // OSW

    dx = nc.dram_tensor("x", [D, T], F32, kind="ExternalInput")
    dy = nc.dram_tensor("y", [D, T], F32, kind="ExternalOutput")
    dwq = nc.dram_tensor("wq", [D, D], F32, kind="ExternalInput")
    dwk = nc.dram_tensor("wk", [D, D], F32, kind="ExternalInput")
    dwv = nc.dram_tensor("wv", [D, D], F32, kind="ExternalInput")
    dwo = nc.dram_tensor("wo", [D, D], F32, kind="ExternalInput")
    dbq = nc.dram_tensor("bq", [1, D], F32, kind="ExternalInput")
    dbk = nc.dram_tensor("bk", [1, D], F32, kind="ExternalInput")
    dbv = nc.dram_tensor("bv", [1, D], F32, kind="ExternalInput")
    dbo = nc.dram_tensor("bo", [1, D], F32, kind="ExternalInput")
    drw = nc.dram_tensor("rw", [D, E], F32, kind="ExternalInput")
    dup = nc.dram_tensor("upw", [Er, D, HF], F32, kind="ExternalInput")
    dgate = nc.dram_tensor("gatew", [Er, D, HF], F32, kind="ExternalInput")
    ddown = nc.dram_tensor("downw", [Er, HF, D], F32, kind="ExternalInput")
    dsup = nc.dram_tensor("sup", [D, HF], F32, kind="ExternalInput")
    dsgate = nc.dram_tensor("sgate", [D, HF], F32, kind="ExternalInput")
    dsdown = nc.dram_tensor("sdown", [HF, D], F32, kind="ExternalInput")
    dltri = nc.dram_tensor("ltri", [128, 128], BF16, kind="ExternalInput")
    dltriT = nc.dram_tensor("ltriT", [TC, TC], F32, kind="ExternalInput")
    diTf = nc.dram_tensor("iTf", [TC, TC], F32, kind="ExternalInput")
    dslots = nc.dram_tensor("slots", [cfg.SR, D], BF16, kind="ExternalInput")
    di128 = nc.dram_tensor("i128", [128, 128], BF16, kind="ExternalInput")
    de16 = nc.dram_tensor("e16", [NH, NP * 128], BF16, kind="ExternalInput")
    dccin = nc.dram_tensor("cc_in", [128, NP * 64 + NP], F32)
    dccout = nc.dram_tensor("cc_out", [128, NP * 64 + NP], F32)

    with ExitStack() as top:
        ec = top.enter_context
        x_sb = ec(nc.sbuf_tensor("x_sb", [128, DC * T], F32))
        xln_sb = ec(nc.sbuf_tensor("xln_sb", [128, DC * T], BF16))
        rows = ec(nc.sbuf_tensor("rows", [128, T], F32))
        rstd_row = ec(nc.sbuf_tensor("rstd_row", [1, T], BF16))
        onesf = ec(nc.sbuf_tensor("onesf", [1, 128], F32))
        onesbf = ec(nc.sbuf_tensor("onesbf", [1, 512], BF16))
        onescol = ec(nc.sbuf_tensor("onescol", [128, 1], BF16))
        statw = ec(nc.sbuf_tensor("statw", [128, 1], BF16))
        ones128f = ec(nc.sbuf_tensor("ones128f", [128, 1], F32))
        rw_sb = ec(nc.sbuf_tensor("rw_sb", [128, DC * E], F32))
        srow = ec(nc.sbuf_tensor("srow", [1, E], F32))
        sb8 = ec(nc.sbuf_tensor("sb8", [128, E], F32))
        cols = ec(nc.sbuf_tensor("cols", [128, 4 * TC], F32))
        PS_A = ec(nc.psum_tensor("PS_A", [128, 512], F32))
        PS_B = ec(nc.psum_tensor("PS_B", [128, 512], F32))
        PS_C = ec(nc.psum_tensor("PS_C", [128, 512], F32))
        PS_D = ec(nc.psum_tensor("PS_D", [128, 512], F32))
        PS_W = ec(nc.psum_tensor("PS_W", [128, 2048], F32))
        PS = {"A": PS_A, "B": PS_B, "C": PS_C, "D": PS_D}

        B = Builder(nc)
        st, sv, ss = B.sem["st"], B.sem["sv"], B.sem["ss"]
        sgd, syd, scc = B.sem["sgd"], B.sem["syd"], B.sem["scc"]
        su, sg2, sd2, sw = B.sem["su"], B.sem["sg2"], B.sem["sd2"], B.sem["sw"]
        sx, sconst, se16 = B.sem["sx"], B.sem["sconst"], B.sem["se16"]
        sconsg = B.sem["sconsg"]
        sci, syo = B.sem["sci"], B.sem["syo"]

        def xs(dc, a, b):
            return x_sb[:, dc * T + a: dc * T + b]

        def xlns(dc, a, b):
            return xln_sb[:, dc * T + a: dc * T + b]

        def vq(fn):
            B.wait("v", sv, sv.n)
            return B.emit("v", fn, inc=sv)

        B.emit("v", lambda M: M.memset(onesf[:, :], 1.0))
        B.emit("v", lambda M: M.memset(onesbf[:, :], 1.0))
        B.emit("v", lambda M: M.memset(onescol[:, :], 1.0))
        B.emit("v", lambda M: M.memset(statw[:, :], 1.0 / D))
        v_consts = B.emit("v", lambda M: M.memset(ones128f[:, :], 1.0), inc=sv)

        src = dx.ap().rearrange("(a p) j -> p a j", p=128)
        dst = x_sb.ap().rearrange("p (a j) -> p a j", a=DC)
        y_xload = B.emit("y", lambda M, o=dst, i=src: M.dma_start(out=o, in_=i),
                         inc=sx, amount=16)
        rsrc = drw.ap().rearrange("(a p) j -> p a j", p=128)
        rdst = rw_sb.ap().rearrange("p (a j) -> p a j", a=DC)
        B.emit("g", lambda M, o=rdst, i=rsrc: M.dma_start(out=o, in_=i),
                      inc=sconsg, amount=16)
        CONSG_TOTAL = sconsg.n

        B.wait("t", sconsg, CONSG_TOTAL)
        B.wait("t", sv, v_consts)
        t_cs = None
        for dc in range(DC):
            t_cs = B.emit("t", lambda M, dc=dc: M.matmul(
                PS_B[0:1, 0:E], ones128f[:, 0:1], rw_sb[:, dc * E:(dc + 1) * E],
                start=(dc == 0), stop=(dc == DC - 1)),
                inc=(st if dc == DC - 1 else None))
        B.wait("v", st, t_cs)
        v_srow = B.emit("v", lambda M: M.tensor_copy(srow[:, :], PS_B[0:1, 0:E]), inc=sv)
        B.wait("t", sv, v_srow)
        t_sb8 = B.emit("t", lambda M: M.matmul(
            PS_B[:, 32:32 + E], onesf[0:1, 0:128], srow[0:1, :],
            start=True, stop=True), inc=st)
        B.wait("v", st, t_sb8)
        v_sb8 = B.emit("v", lambda M: M.tensor_copy(sb8[:, :], PS_B[:, 32:32 + E]), inc=sv)
        B.consumed("B", sv, v_sb8)

        def emit_ln(scr_sb, scr_off, sq_sb, sq_off):
            t_dc = 0
            for dc in range(DC):
                B.emit("s", lambda M, dc=dc: M.copy(xlns(dc, 0, T), xs(dc, 0, T)))
                B.wait("s", st, t_dc)
                v = B.emit("s", lambda M, dc=dc: M.activation(
                    sq_sb[:, sq_off: sq_off + T], xs(dc, 0, T), AF.Square), inc=ss)
                B.wait("t", ss, v)
                if dc == 0:
                    B.wait_slot("t", "Wstat")
                for ft in range(FT):
                    B.emit("t", lambda M, dc=dc, ft=ft: M.matmul(
                        PS_W[0:1, ft * 512:(ft + 1) * 512], statw[:, 0:1],
                        xlns(dc, ft * 512, (ft + 1) * 512),
                        start=(dc == 0), stop=(dc == DC - 1)))
                    t_dc = B.emit("t", lambda M, dc=dc, ft=ft: M.matmul(
                        PS_W[32:33, ft * 512:(ft + 1) * 512], statw[:, 0:1],
                        sq_sb[:, sq_off + ft * 512: sq_off + (ft + 1) * 512],
                        start=(dc == 0), stop=(dc == DC - 1)),
                        inc=(st if ft == FT - 1 else None))
            t_stats = t_dc
            B.wait("v", st, t_stats)
            vq(lambda M: M.tensor_copy(rows[0:1, :], PS_W[0:1, 0:T]))
            vq(lambda M: M.tensor_mul(rows[96:97, :], rows[0:1, :],
                                      rows[0:1, :]))
            v_var = vq(lambda M: M.scalar_tensor_tensor(
                rows[64:65, :], PS_W[32:33, 0:T], float(cfg.ln_eps),
                rows[96:97, :], ALU.add, ALU.subtract))
            B.consumed("Wstat", sv, v_var)
            B.wait("s", sv, v_var)
            s_sd = B.emit("s", lambda M: M.activation(
                rows[96:97, :], rows[64:65, :], AF.Sqrt), inc=ss)
            B.wait("v", ss, s_sd)
            def _rstd_lp(M):
                with nc.allow_low_precision(reason="bf16 rstd row"):
                    return M.reciprocal(rstd_row[0:1, :], rows[96:97, :])
            v_rstd = vq(_rstd_lp)
            B.wait("t", sv, v_rstd)
            names = ["A", "B", "C", "D"]
            for i in range(2 * FT):
                ft, which = i // 2, i % 2
                key = names[i % 4]
                ps = PS[key]
                B.wait_slot("t", key)
                if which == 0:
                    tv = B.emit("t", lambda M, ps=ps, ft=ft: M.matmul(
                        ps[:, 0:512], onesf[0:1, :],
                        rows[0:1, ft * 512:(ft + 1) * 512],
                        start=True, stop=True), inc=st)
                else:
                    tv = B.emit("t", lambda M, ps=ps, ft=ft: M.matmul(
                        ps[:, 0:512], onesbf[0:1, 0:128],
                        rstd_row[0:1, ft * 512:(ft + 1) * 512],
                        start=True, stop=True), inc=st)
                B.wait("v", st, tv)
                off = scr_off + T + which * T + ft * 512
                vv = B.emit("v", lambda M, ps=ps, off=off: M.tensor_copy(
                    scr_sb[:, off: off + 512], ps[:, 0:512]), inc=sv)
                B.consumed(key, sv, vv)
            v_ln = None
            for dc in range(DC):
                vq(lambda M, dc=dc: M.tensor_sub(
                    scr_sb[:, scr_off: scr_off + T], xs(dc, 0, T),
                    scr_sb[:, scr_off + T: scr_off + 2 * T]))
                v_ln = vq(lambda M, dc=dc: M.tensor_mul(
                    xlns(dc, 0, T), scr_sb[:, scr_off: scr_off + T],
                    scr_sb[:, scr_off + 2 * T: scr_off + 3 * T]))
            return v_ln, v_rstd

        def emit_router(v_rstd, logits_sb, mtok_sb):
            B.wait("t", sv, v_rstd)
            B.wait_slot("t", "B")
            for tc in range(TC):
                B.emit("t", lambda M, tc=tc: M.matmul(
                    PS_B[:, tc:tc + 1], rows[0:1, tc * 128:(tc + 1) * 128],
                    onesf[0:1, 0:1], start=True, stop=True))
                t_cols = B.emit("t", lambda M, tc=tc: M.matmul(
                    PS_B[:, TC + tc: TC + tc + 1],
                    rstd_row[0:1, tc * 128:(tc + 1) * 128],
                    onesbf[0:1, 0:1], start=True, stop=True),
                    inc=(st if tc == TC - 1 else None))
            B.wait("v", st, t_cols)
            v_cols = vq(lambda M: M.tensor_copy(
                cols[:, 0:2 * TC], PS_B[:, 0:2 * TC]))
            B.consumed("B", sv, v_cols)
            B.wait_slot("t", "Wstat")
            for tc in range(TC):
                for dc in range(DC):
                    t_raw = B.emit("t", lambda M, tc=tc, dc=dc: M.matmul(
                        PS_W[:, tc * E:(tc + 1) * E],
                        xs(dc, tc * 128, (tc + 1) * 128),
                        rw_sb[:, dc * E:(dc + 1) * E],
                        start=(dc == 0), stop=(dc == DC - 1)),
                        inc=(st if (tc == TC - 1 and dc == DC - 1) else None))
            B.wait("v", st, t_raw)
            B.wait("v", sv, v_cols)
            for tc in range(TC):
                vq(lambda M, tc=tc: M.tensor_scalar(
                    logits_sb[:, tc * E:(tc + 1) * E], sb8[:, :],
                    cols[:, tc:tc + 1], None, ALU.mult))
                vq(lambda M, tc=tc: M.tensor_sub(
                    logits_sb[:, tc * E:(tc + 1) * E], PS_W[:, tc * E:(tc + 1) * E],
                    logits_sb[:, tc * E:(tc + 1) * E]))
                vq(lambda M, tc=tc: M.tensor_scalar(
                    logits_sb[:, tc * E:(tc + 1) * E],
                    logits_sb[:, tc * E:(tc + 1) * E],
                    cols[:, TC + tc: TC + tc + 1], None, ALU.mult))
                vq(lambda M, tc=tc: M.tensor_reduce(
                    cols[:, 2 * TC + tc: 2 * TC + tc + 1],
                    logits_sb[:, tc * E:(tc + 1) * E], mybir.AxisListType.X, ALU.max))
            v_logit = vq(lambda M: M.tensor_scalar_mul(
                cols[:, 2 * TC: 3 * TC], cols[:, 2 * TC: 3 * TC], -1.0))
            B.wait("s", sv, v_logit)
            for tc in range(TC):
                s_exp = B.emit("s", lambda M, tc=tc: M.activation(
                    PS_W[:, tc * E:(tc + 1) * E], logits_sb[:, tc * E:(tc + 1) * E],
                    AF.Exp, bias=cols[:, 2 * TC + tc: 2 * TC + tc + 1], scale=1.0),
                    inc=(ss if tc == TC - 1 else None))
            B.wait("v", ss, s_exp)
            for tc in range(TC):
                vq(lambda M, tc=tc: M.tensor_reduce(
                    cols[:, 3 * TC + tc: 3 * TC + tc + 1],
                    PS_W[:, tc * E:(tc + 1) * E], mybir.AxisListType.X, ALU.add))
            v_se = vq(lambda M: M.reciprocal(
                cols[:, 3 * TC: 4 * TC], cols[:, 3 * TC: 4 * TC]))
            v_mtok = None
            for tc in range(TC):
                v_mtok = vq(lambda M, tc=tc: M.tensor_scalar(
                    mtok_sb[:, tc * E:(tc + 1) * E],
                    logits_sb[:, tc * E:(tc + 1) * E],
                    cols[:, 2 * TC + tc: 2 * TC + tc + 1], 0.0,
                    ALU.add, ALU.is_ge))
            B.consumed("Wstat", sv, v_mtok)
            return v_mtok

        def emit_ffn(v_gate, tag):
            C, SR = cfg.C, cfg.SR
            CB = C // 128
            ND = D // 512
            TRASH, ZROW = float(Er * C), float(Er * C + 1)
            ZB = (Er * C // 128) * 128
            with ExitStack() as ffn:
                fc = ffn.enter_context
                hid_sb = fc(nc.sbuf_tensor("hid" + tag, [128, 4 * T], BF16))
                U_sb = fc(nc.sbuf_tensor("U" + tag, [128, DC * 512], BF16))
                G_sb = fc(nc.sbuf_tensor("G" + tag, [128, DC * 512], BF16))
                U2_sb = fc(nc.sbuf_tensor("U2" + tag, [128, DC * 512], BF16))
                G2_sb = fc(nc.sbuf_tensor("G2" + tag, [128, DC * 512], BF16))
                D_sb = fc(nc.sbuf_tensor("Dw" + tag, [128, 4 * D], BF16))
                silu_sb = fc(nc.sbuf_tensor("silu" + tag, [128, 1024], BF16))
                logits_sb = fc(nc.sbuf_tensor("logits" + tag, [128, TC * E], F32))
                mtok_sb = fc(nc.sbuf_tensor("mtok" + tag, [128, TC * E], BF16))
                xg_tok = fc(nc.sbuf_tensor("xgt" + tag, [128, CB * D], BF16))
                xg_fm = fc(nc.sbuf_tensor("xgf" + tag, [128, DC * C], BF16))
                slotout = fc(nc.sbuf_tensor("sout" + tag, [128, CB * D], F32))
                xtt = fc(nc.sbuf_tensor("xtt" + tag, [128, D], BF16))
                rtt = fc(nc.sbuf_tensor("rtt" + tag, [128, D], BF16))
                pfx_sb = fc(nc.sbuf_tensor("pfx" + tag, [128, 2 * TC], F32))
                psc = fc(nc.sbuf_tensor("psc" + tag, [128, 4], F32))
                mpack = fc(nc.sbuf_tensor("mpack" + tag, [128, TC], BF16))
                selm = fc(nc.sbuf_tensor("selm" + tag, [128, TC], F32))
                tmp_sb = fc(nc.sbuf_tensor("tmpx" + tag, [128, TC], F32))
                orow_f = fc(nc.sbuf_tensor("orf" + tag, [128, TC], F32))
                orow_i = fc(nc.sbuf_tensor("ori" + tag, [128, TC], I32))
                brow_i = fc(nc.sbuf_tensor("bri" + tag, [128, TC], I32))
                ltri_sb = fc(nc.sbuf_tensor("ltri" + tag, [128, 128], BF16))
                ltriT_sb = fc(nc.sbuf_tensor("ltriT" + tag, [TC, TC], F32))
                iTf_sb = fc(nc.sbuf_tensor("iTf" + tag, [TC, TC], F32))
                i128_sb = fc(nc.sbuf_tensor("i128" + tag, [128, 128], BF16))

                B.wait("y", sv, v_gate)
                B.emit("y", lambda M: M.dma_start(
                    out=ltri_sb[:, :], in_=dltri[:, :]), inc=sconst, amount=16)
                B.emit("y", lambda M: M.dma_start(
                    out=ltriT_sb[:, :], in_=dltriT[:, :]), inc=sconst, amount=16)
                B.emit("y", lambda M: M.dma_start(
                    out=iTf_sb[:, :], in_=diTf[:, :]), inc=sconst, amount=16)
                B.emit("y", lambda M: M.dma_start(
                    out=i128_sb[:, :], in_=di128[:, :]), inc=sconst, amount=16)
                c_tot = sconst.n

                def wsrc(e):
                    if e < Er:
                        return (dup.ap()[e], dgate.ap()[e], ddown.ap()[e])
                    return (dsup.ap(), dsgate.ap(), dsdown.ap())

                UB, GB = [U_sb, U2_sb], [G_sb, G2_sb]
                order = [Er] + list(range(Er))
                r_units = [(e, hb) for e in range(Er) for hb in range(HB)]
                nr = len(r_units)
                units = [(Er, 0)]
                ri = 0
                for s_hb in range(1, HB):
                    units.append((Er, s_hb))
                    take = (nr * s_hb) // max(HB - 1, 1) - ri
                    units.extend(r_units[ri:ri + take])
                    ri += take
                units.extend(r_units[ri:])

                def load_ug(k):
                    e, hb = units[k]
                    up, gate, _ = wsrc(e)
                    buf = k % 2
                    sem = [su, sg2][buf]
                    s_u = up[:, hb * 512:(hb + 1) * 512].rearrange(
                        "(a p) j -> p a j", p=128)
                    d_u = UB[buf].ap().rearrange("p (a j) -> p a j", a=DC)
                    B.emit("g", lambda M, o=d_u, i=s_u: M.dma_start(out=o, in_=i),
                           inc=sem, amount=16)
                    s_g = gate[:, hb * 512:(hb + 1) * 512].rearrange(
                        "(a p) j -> p a j", p=128)
                    d_g = GB[buf].ap().rearrange("p (a j) -> p a j", a=DC)
                    B.emit("g", lambda M, o=d_g, i=s_g: M.dma_start(out=o, in_=i),
                           inc=sem, amount=16)
                    return sem.n

                def load_d(k):
                    e, hb = units[k]
                    _, _, down = wsrc(e)
                    s_d = down[hb * 512:(hb + 1) * 512, :].rearrange(
                        "(a p) j -> p a j", p=128)
                    d_d = D_sb.ap().rearrange("p (a j) -> p a j", a=4)
                    return B.emit("g", lambda M, o=d_d, i=s_d: M.dma_start(out=o, in_=i),
                                  inc=sd2, amount=16)

                B.wait("g", sv, v_gate)
                ugv = {0: load_ug(0)}
                if len(units) > 1:
                    ugv[1] = load_ug(1)
                dv = load_d(0)

                v_ln, v_rstd = emit_ln(hid_sb, 0, hid_sb, 3 * T)
                v_mtok = emit_router(v_rstd, logits_sb, mtok_sb)

                B.wait("g", B.sem["sbg0"], B.sem["sbg0"].n)
                B.wait("g", B.sem["sbg1"], B.sem["sbg1"].n)

                R = {"v_bi": None, "sc_tot": None,
                     "xg_ready": None}

                def emit_routing():
                    mtok3 = mtok_sb.ap().rearrange("p (t e) -> p t e", e=E)
                    vq(lambda M: M.memset(orow_f[:, :], 0.0))
                    vq(lambda M: M.memset(selm[:, :], 0.0))
                    B.wait("t", sconst, c_tot)
                    B.wait("t", sv, v_mtok)
                    for e in range(Er):
                        vm = vq(lambda M, e=e: M.tensor_copy(
                            mpack[:, :], mtok3[:, :, e]))
                        B.wait("t", sv, vm)
                        B.wait_slot("t", "B")
                        B.emit("t", lambda M: M.matmul(
                            PS_B[:, 0:TC], ltri_sb[:, :], mpack[:, :],
                            start=True, stop=True))
                        t1 = B.emit("t", lambda M: M.matmul(
                            PS_B[0:TC, 64:65], mpack[:, :], onescol[:, 0:1],
                            start=True, stop=True), inc=st)
                        B.wait("v", st, t1)
                        vq(lambda M: M.tensor_copy(pfx_sb[:, 0:TC], PS_B[:, 0:TC]))
                        v2 = vq(lambda M: M.tensor_copy(psc[0:TC, 0:1],
                                                        PS_B[0:TC, 64:65]))
                        B.wait("t", sv, v2)
                        t2 = B.emit("t", lambda M: M.matmul(
                            PS_B[0:TC, 96:97], ltriT_sb[:, :], psc[0:TC, 0:1],
                            start=True, stop=True), inc=st)
                        B.wait("v", st, t2)
                        v3 = vq(lambda M: M.tensor_copy(psc[0:TC, 1:2],
                                                        PS_B[0:TC, 96:97]))
                        B.wait("t", sv, v3)
                        t3 = B.emit("t", lambda M: M.matmul(
                            PS_B[0:1, 128:128 + TC], psc[0:TC, 1:2], iTf_sb[:, :],
                            start=True, stop=True), inc=st)
                        B.wait("v", st, t3)
                        v4 = vq(lambda M: M.tensor_copy(
                            pfx_sb[0:1, TC:2 * TC], PS_B[0:1, 128:128 + TC]))
                        B.wait("t", sv, v4)
                        t4 = B.emit("t", lambda M: M.matmul(
                            PS_B[:, 160:160 + TC], onesf[0:1, :],
                            pfx_sb[0:1, TC:2 * TC], start=True, stop=True), inc=st)
                        B.wait("v", st, t4)
                        v5 = vq(lambda M: M.tensor_copy(
                            pfx_sb[:, TC:2 * TC], PS_B[:, 160:160 + TC]))
                        B.consumed("B", sv, v5)
                        vq(lambda M, e=e: M.scalar_tensor_tensor(
                            tmp_sb[:, :], pfx_sb[:, 0:TC], float(e * C),
                            pfx_sb[:, TC:2 * TC], ALU.add, ALU.add))
                        vq(lambda M: M.tensor_mul(tmp_sb[:, :], tmp_sb[:, :],
                                                  mpack[:, :]))
                        vq(lambda M: M.tensor_add(orow_f[:, :], orow_f[:, :],
                                                  tmp_sb[:, :]))
                        vq(lambda M: M.tensor_add(selm[:, :], selm[:, :],
                                                  mpack[:, :]))
                    vq(lambda M: M.tensor_scalar(selm[:, :], selm[:, :], -1.0, 1.0,
                                                 ALU.mult, ALU.add))
                    vq(lambda M: M.scalar_tensor_tensor(
                        tmp_sb[:, :], selm[:, :], TRASH, orow_f[:, :],
                        ALU.mult, ALU.add))
                    v_oi = vq(lambda M: M.tensor_copy(orow_i[:, :], tmp_sb[:, :]))
                    vq(lambda M: M.scalar_tensor_tensor(
                        tmp_sb[:, :], selm[:, :], ZROW, orow_f[:, :],
                        ALU.mult, ALU.add))
                    v_bi = vq(lambda M: M.tensor_copy(brow_i[:, :], tmp_sb[:, :]))

                    six0 = B.sem["six0"]
                    scat_vals = []
                    for tc in range(TC):
                        par = tc % 2
                        B.wait_slot("t", "W%d" % par)
                        t_tr = None
                        for dc in range(DC):
                            t_tr = B.emit("t", lambda M, par=par, dc=dc, tc=tc:
                                          M.matmul(
                                PS_W[:, par * 1024 + dc * 128: par * 1024 + (dc + 1) * 128],
                                xlns(dc, tc * 128, (tc + 1) * 128), i128_sb[:, :],
                                start=True, stop=True),
                                inc=(st if dc == DC - 1 else None))
                        sb_src = [xtt, rtt][tc % 2]
                        B.wait("v", st, t_tr)
                        if tc >= 2:
                            B.wait("v", six0, scat_vals[tc - 2])
                        v_xt = vq(lambda M, par=par, sb_src=sb_src: M.tensor_copy(
                            sb_src[:, 0:D], PS_W[:, par * 1024: par * 1024 + D]))
                        B.consumed("W%d" % par, sv, v_xt)
                        B.wait("g", sv, max(v_xt, v_oi))
                        if scat_vals:
                            B.wait("g", six0, scat_vals[-1])
                        scat_vals.append(B.emit("g", lambda M, tc=tc, sb_src=sb_src:
                                                M.indirect_dma_start(
                            dslots[:, :],
                            bass.IndirectOffsetOnAxis(orow_i[:, tc:tc + 1], 0),
                            sb_src[:, 0:D], None),
                            inc=six0, amount=16))
                    sc_tot = six0.n
                    R["v_bi"] = v_bi
                    R["sc_tot"] = six0.n
                    issue_xg(routed[0])
                    R["xg_ready"] = sxg_prev


                t_xg_done = 0
                srw_prev = B.sem["srw"].n
                sxg_prev = B.sem["sxg"].n
                routed = [e for e in order if e < Er]

                def issue_xg(e):
                    nonlocal sxg_prev
                    B.wait("g", B.sem["six0"], R["sc_tot"])
                    B.wait("g", st, t_xg_done)
                    B.wait("g", B.sem["sxg"], sxg_prev)
                    sxg_prev = B.emit("g", lambda M, e=e: M.dma_start(
                        out=xg_tok.ap().rearrange("p (a j) -> p a j", a=CB),
                        in_=dslots[e * C:(e + 1) * C, :].rearrange(
                            "(a p) j -> p a j", p=128)),
                        inc=B.sem["sxg"], amount=16)

                for k, (e, hb) in enumerate(units):
                    Ub, Gb = UB[k % 2], GB[k % 2]
                    if e == Er:
                        if hb == 0:
                            B.wait("t", sv, v_ln)
                        B.wait("t", [su, sg2][k % 2], ugv[k])
                        tiles = [(h, f) for h in range(4) for f in range(FT)]
                        for i, (hc, ft) in enumerate(tiles):
                            kU = ["A", "C"][i % 2]
                            kG = ["B", "D"][i % 2]
                            psU, psG = PS[kU], PS[kG]
                            sl = i % 2
                            B.wait_slot("t", kU)
                            B.wait_slot("t", kG)
                            for dc in range(DC):
                                B.emit("t", lambda M, psU=psU, Ub=Ub, dc=dc, hc=hc, ft=ft:
                                       M.matmul(
                                    psU[:, 0:512],
                                    Ub[:, dc * 512 + hc * 128: dc * 512 + (hc + 1) * 128],
                                    xlns(dc, ft * 512, (ft + 1) * 512),
                                    start=(dc == 0), stop=(dc == DC - 1)))
                            for dc in range(DC):
                                tA = B.emit("t", lambda M, psG=psG, Gb=Gb, dc=dc, hc=hc, ft=ft:
                                            M.matmul(
                                    psG[:, 0:512],
                                    Gb[:, dc * 512 + hc * 128: dc * 512 + (hc + 1) * 128],
                                    xlns(dc, ft * 512, (ft + 1) * 512),
                                    start=(dc == 0), stop=(dc == DC - 1)),
                                    inc=(st if dc == DC - 1 else None))
                            B.wait("s", st, tA)
                            sS = B.emit("s", lambda M, psG=psG, sl=sl: M.activation(
                                silu_sb[:, sl * 512:(sl + 1) * 512], psG[:, 0:512],
                                AF.Silu), inc=ss)
                            B.wait("v", ss, sS)
                            vv = B.emit("v", lambda M, psU=psU, sl=sl, hc=hc, ft=ft:
                                        M.tensor_mul(
                                hid_sb[:, hc * T + ft * 512: hc * T + (ft + 1) * 512],
                                psU[:, 0:512],
                                silu_sb[:, sl * 512:(sl + 1) * 512]), inc=sv)
                            B.consumed(kU, sv, vv)
                            B.consumed(kG, sv, vv)
                            v_hid = vv
                        if k + 2 < len(units):
                            B.wait("g", st, tA)
                            ugv[k + 2] = load_ug(k + 2)
                        ugv.pop(k)
                        B.wait("t", sd2, dv)
                        B.wait("t", sv, v_hid)
                        for dco in range(DC):
                            for half in range(NHALF):
                                B.wait_slot("t", "W%d" % half)
                                for hc in range(4):
                                    for q in range(NQ):
                                        c0 = half * WH + q * 512
                                        tB = B.emit("t",
                                                    lambda M, dco=dco, hc=hc, c0=c0:
                                                    M.matmul(
                                            PS_W[:, c0:c0 + 512],
                                            D_sb[:, hc * D + dco * 128: hc * D + (dco + 1) * 128],
                                            hid_sb[:, hc * T + c0: hc * T + c0 + 512],
                                            start=(hc == 0), stop=(hc == 3)),
                                            inc=(st if (hc == 3 and q == NQ - 1) else None))
                                B.wait("v", st, tB)
                                vv = B.emit("v", lambda M, dco=dco, half=half:
                                            M.scalar_tensor_tensor(
                                    xs(dco, half * WH, (half + 1) * WH),
                                    PS_W[:, half * WH:(half + 1) * WH], 0.5,
                                    xs(dco, half * WH, (half + 1) * WH),
                                    ALU.mult, ALU.add), inc=sv)
                                B.consumed("W%d" % half, sv, vv)
                        if k + 1 < len(units):
                            B.wait("g", st, st.n)
                            dv = load_d(k + 1)
                        if k == 0:
                            emit_routing()
                        continue

                    if hb == 0:
                        B.wait("t", B.sem["sxg"], R["xg_ready"])
                        for a in range(CB):
                            par = a % 2
                            B.wait_slot("t", "W%d" % par)
                            for dc in range(DC):
                                t_xg_done = B.emit("t", lambda M, a=a, dc=dc, par=par:
                                                   M.matmul(
                                    PS_W[:, par * 1024 + dc * 128: par * 1024 + (dc + 1) * 128],
                                    xg_tok[:, a * D + dc * 128: a * D + (dc + 1) * 128],
                                    i128_sb[:, :], start=True, stop=True),
                                    inc=(st if dc == DC - 1 else None))
                            B.wait("v", st, t_xg_done)
                            dst3 = xg_fm.ap().rearrange(
                                "p (b j) -> p b j", j=C)[:, :, a * 128:(a + 1) * 128]
                            src3 = PS_W.ap().rearrange(
                                "p (b j) -> p b j", j=128)[:, par * 8: par * 8 + DC, :]
                            v_xgf = vq(lambda M, dst3=dst3, src3=src3: M.tensor_copy(
                                dst3, src3))
                            B.consumed("W%d" % par, sv, v_xgf)
                        ri = routed.index(e)
                        if ri + 1 < len(routed):
                            issue_xg(routed[ri + 1])
                            R["xg_ready"] = sxg_prev
                    B.wait("t", [su, sg2][k % 2], ugv[k])
                    if hb == 0:
                        B.wait("t", sv, v_xgf)
                    for hc in range(4):
                        kU = ["A", "C"][hc % 2]
                        kG = ["B", "D"][hc % 2]
                        psU, psG = PS[kU], PS[kG]
                        sl = hc % 2
                        B.wait_slot("t", kU)
                        B.wait_slot("t", kG)
                        for dc in range(DC):
                            B.emit("t", lambda M, psU=psU, Ub=Ub, dc=dc, hc=hc: M.matmul(
                                psU[:, 0:C],
                                Ub[:, dc * 512 + hc * 128: dc * 512 + (hc + 1) * 128],
                                xg_fm[:, dc * C:(dc + 1) * C],
                                start=(dc == 0), stop=(dc == DC - 1)))
                        for dc in range(DC):
                            tA = B.emit("t", lambda M, psG=psG, Gb=Gb, dc=dc, hc=hc:
                                        M.matmul(
                                psG[:, 0:C],
                                Gb[:, dc * 512 + hc * 128: dc * 512 + (hc + 1) * 128],
                                xg_fm[:, dc * C:(dc + 1) * C],
                                start=(dc == 0), stop=(dc == DC - 1)),
                                inc=(st if dc == DC - 1 else None))
                        B.wait("s", st, tA)
                        sS = B.emit("s", lambda M, psG=psG, sl=sl: M.activation(
                            silu_sb[:, sl * 512: sl * 512 + C], psG[:, 0:C],
                            AF.Silu), inc=ss)
                        B.wait("v", ss, sS)
                        vv = B.emit("v", lambda M, psU=psU, sl=sl, hc=hc:
                                    M.tensor_mul(
                            hid_sb[:, hc * C:(hc + 1) * C], psU[:, 0:C],
                            silu_sb[:, sl * 512: sl * 512 + C]), inc=sv)
                        B.consumed(kU, sv, vv)
                        B.consumed(kG, sv, vv)
                        v_hid = vv
                    if k + 2 < len(units):
                        B.wait("g", st, tA)
                        ugv[k + 2] = load_ug(k + 2)
                    ugv.pop(k)
                    B.wait("t", sd2, dv)
                    B.wait("t", sv, v_hid)
                    if hb == 0:
                        B.wait("v", B.sem["srw"], srw_prev)
                    regions = [(s_, d_) for s_ in range(CB)
                               for d_ in range(ND)]
                    for r, (sc_, dh) in enumerate(regions):
                        key = "ABCD"[r % 4]
                        ps = PS[key]
                        B.wait_slot("t", key)
                        for hc in range(4):
                            tB = B.emit("t",
                                        lambda M, ps=ps, sc_=sc_, dh=dh, hc=hc:
                                        M.matmul(
                                ps[:, 0:512],
                                hid_sb[:, hc * C + sc_ * 128: hc * C + sc_ * 128 + 128],
                                D_sb[:, hc * D + dh * 512: hc * D + dh * 512 + 512],
                                start=(hc == 0), stop=(hc == 3)),
                                inc=(st if hc == 3 else None))
                        B.wait("v", st, tB)
                        off = sc_ * D + dh * 512
                        if hb == 0:
                            vv = B.emit("v", lambda M, ps=ps, off=off:
                                        M.tensor_copy(
                                slotout[:, off:off + 512], ps[:, 0:512]),
                                inc=sv)
                        else:
                            vv = B.emit("v", lambda M, ps=ps, off=off:
                                        M.tensor_add(
                                slotout[:, off:off + 512],
                                slotout[:, off:off + 512], ps[:, 0:512]),
                                inc=sv)
                        B.consumed(key, sv, vv)
                        v_sout = vv
                    if k + 1 < len(units):
                        B.wait("g", st, st.n)
                        dv = load_d(k + 1)
                    if hb == HB - 1:
                        B.wait("g", sv, v_sout)
                        srw_prev = B.emit("g", lambda M, e=e: M.dma_start(
                            out=dslots[e * C:(e + 1) * C, :].rearrange(
                                "(a p) j -> p a j", p=128),
                            in_=slotout.ap().rearrange("p (a j) -> p a j", a=CB)),
                            inc=B.sem["srw"], amount=16)

                sbg = [B.sem["sbg0"], B.sem["sbg1"]]
                sbg_prev = [sbg[0].n, sbg[1].n]
                bb = [rtt, xtt]
                t_bw_par = [0, 0]
                t_bw = 0
                v_out = 0
                for tc in range(TC):
                    par = tc % 2
                    B.wait("g", B.sem["srw"], B.sem["srw"].n)
                    B.wait("g", sv, R["v_bi"])
                    B.wait("g", st, t_bw_par[par])
                    B.wait("g", sbg[par], sbg_prev[par])
                    sbg_prev[par] = B.emit("g", lambda M, tc=tc, par=par:
                                           M.indirect_dma_start(
                        bb[par][:, 0:D], None,
                        dslots[:, :],
                        bass.IndirectOffsetOnAxis(brow_i[:, tc:tc + 1], 0)),
                        inc=sbg[par], amount=16)
                    B.wait("v", sbg[par], sbg_prev[par])
                    v_sc = vq(lambda M, tc=tc, par=par: M.tensor_scalar(
                        bb[par][:, 0:D], bb[par][:, 0:D],
                        cols[:, 3 * TC + tc: 3 * TC + tc + 1], None, ALU.mult))
                    B.wait("t", sv, v_sc)
                    B.wait_slot("t", "W%d" % par)
                    for dc in range(DC):
                        t_bw = B.emit("t", lambda M, par=par, dc=dc: M.matmul(
                            PS_W[:, par * 1024 + dc * 128: par * 1024 + (dc + 1) * 128],
                            bb[par][:, dc * 128:(dc + 1) * 128],
                            i128_sb[:, :], start=True, stop=True),
                            inc=(st if dc == DC - 1 else None))
                    t_bw_par[par] = t_bw
                    B.wait("v", st, t_bw)
                    xdst = x_sb.ap().rearrange("p (b j) -> p b j", j=T)[
                        :, :, tc * 128:(tc + 1) * 128]
                    psrc = PS_W.ap().rearrange("p (b j) -> p b j", j=128)[
                        :, par * 8: par * 8 + DC, :]
                    v_out = vq(lambda M, xdst=xdst, psrc=psrc: M.tensor_add(
                        xdst, xdst, psrc))
                    B.consumed("W%d" % par, sv, v_out)
                B.consumed("Wstat", sv, v_out)
                return v_out

        def emit_attn(v_gate):
            with ExitStack() as att:
                ac = att.enter_context
                W0 = ac(nc.sbuf_tensor("W0", [128, DC * D], BF16))
                W1 = ac(nc.sbuf_tensor("W1", [128, DC * D], BF16))
                K_tile = ac(nc.sbuf_tensor("K_tile", [128, 2 * D], BF16))
                V_tile = ac(nc.sbuf_tensor("V_tile", [128, 2 * D], BF16))
                ascr = ac(nc.sbuf_tensor("ascr", [128, DC * T], BF16))
                recip_sb = ac(nc.sbuf_tensor("recip_sb", [NH, T], BF16))
                cc_sb = ac(nc.sbuf_tensor("cc_sb", [128, NP * 64 + NP], F32))
                kvbf_sb = ac(nc.sbuf_tensor("kvbf_sb", [128, NP * 64], BF16))
                zbd_sb = ac(nc.sbuf_tensor("zbd_sb", [128, DC * NH], BF16))
                e16_sb = ac(nc.sbuf_tensor("e16_sb", [NH, NP * 128], BF16))
                bias_at = ac(nc.sbuf_tensor("bias_at", [1, 2 * D], BF16))
                sc1 = ac(nc.sbuf_tensor("sc1a", [128, 512], BF16))
                sc2 = ac(nc.sbuf_tensor("sc2a", [128, 512], BF16))

                def load_w(dw, W_t, sem):
                    s_w = dw.ap().rearrange("(a p) j -> p a j", p=128)
                    d_w = W_t.ap().rearrange("p (a j) -> p a j", a=DC)
                    return B.emit("g", lambda M, o=d_w, i=s_w: M.dma_start(
                        out=o, in_=i), inc=sem, amount=16)

                B.wait("y", sv, v_gate)
                y_e16 = B.emit("y", lambda M: M.dma_start(
                    out=e16_sb[:, :], in_=de16[:, :]), inc=se16, amount=16)
                B.wait("g", sv, v_gate)
                w_k = load_w(dwk, W0, sw)
                w_v = load_w(dwv, W1, B.sem["swb"])
                B.emit("g", lambda M: M.dma_start(
                    out=bias_at[0:1, 0:D], in_=dbk[:, :]), inc=sconsg, amount=16)
                B.emit("g", lambda M: M.dma_start(
                    out=bias_at[0:1, D:2 * D], in_=dbv[:, :]), inc=sconsg, amount=16)
                bias_kv = sconsg.n

                v_ln2, _ = emit_ln(ascr, 0, ascr, 3 * T)
                v_ccz = vq(lambda M: M.memset(cc_sb[:, :], 0.0))

                B.wait("t", sw, w_k)
                B.wait("t", B.sem["swb"], w_v)
                B.wait("t", sconsg, bias_kv)
                B.wait("t", sv, v_ln2)
                B.wait_slot("t", "A")
                B.wait_slot("t", "B")
                B.wait_slot("t", "C")
                B.wait_slot("t", "Wstat")
                t_kvtc = [0] * TC
                for tc in range(TC):
                    half = tc % 2
                    B.wait_slot("t", "WpK")
                    tvk = 0
                    for os in range(NOS):
                        c0 = os * OSW
                        for dc in range(DC):
                            B.emit("t", lambda M, dc=dc, tc=tc, c0=c0: M.matmul(
                                PS_W[:, c0:c0 + OSW],
                                xlns(dc, tc * 128, (tc + 1) * 128),
                                W0[:, dc * D + c0: dc * D + c0 + OSW],
                                start=(dc == 0), stop=False))
                        tvk = B.emit("t", lambda M, c0=c0: M.matmul(
                            PS_W[:, c0:c0 + OSW], onesbf[0:1, 0:128],
                            bias_at[0:1, c0:c0 + OSW],
                            start=False, stop=True),
                            inc=(st if os == NOS - 1 else None))
                    vphi = 0
                    for os in range(NOS):
                        c0 = os * OSW
                        dk = K_tile[:, half * D + c0: half * D + c0 + OSW]
                        B.wait("v", st, tvk)
                        if tc >= 2:
                            B.wait("v", st, t_kvtc[tc - 2])
                        v1 = vq(lambda M, c0=c0: M.tensor_scalar_min(
                            sc1[:, 0:OSW], PS_W[:, c0:c0 + OSW], 0.0))
                        B.wait("s", sv, v1)
                        s1 = B.emit("s", lambda M: M.activation(
                            sc2[:, 0:OSW], sc1[:, 0:OSW], AF.Exp), inc=ss)
                        B.wait("v", ss, s1)
                        vphi = vq(lambda M, c0=c0, dk=dk: M.scalar_tensor_tensor(
                            dk, PS_W[:, c0:c0 + OSW], 0.0,
                            sc2[:, 0:OSW], ALU.max, ALU.add))
                    B.consumed("WpK", sv, vphi)
                    B.wait_slot("t", "WpV")
                    tvv = 0
                    for os in range(NOS):
                        c0 = D + os * OSW
                        for dc in range(DC):
                            B.emit("t", lambda M, dc=dc, tc=tc, c0=c0: M.matmul(
                                PS_W[:, c0:c0 + OSW],
                                xlns(dc, tc * 128, (tc + 1) * 128),
                                W1[:, dc * D + c0 - D: dc * D + c0 - D + OSW],
                                start=(dc == 0), stop=False))
                        tvv = B.emit("t", lambda M, c0=c0: M.matmul(
                            PS_W[:, c0:c0 + OSW], onesbf[0:1, 0:128],
                            bias_at[0:1, c0:c0 + OSW],
                            start=False, stop=True),
                            inc=(st if os == NOS - 1 else None))
                    vcv = 0
                    for os in range(NOS):
                        c0 = D + os * OSW
                        dv = V_tile[:, half * D + os * OSW: half * D + (os + 1) * OSW]
                        B.wait("v", st, tvv)
                        if tc >= 2:
                            B.wait("v", st, t_kvtc[tc - 2])
                        vcv = vq(lambda M, c0=c0, dv=dv: M.tensor_copy(
                            dv, PS_W[:, c0:c0 + OSW]))
                    B.consumed("WpV", sv, vcv)
                    B.wait("t", sv, vcv)
                    if tc > 0:
                        B.wait("t", sv, v_kvadd)
                    for p in range(NP):
                        b0 = half * D + p * 128
                        B.emit("t", lambda M, p=p, b0=b0: M.matmul(
                            PS_A[0:64, p * 64:(p + 1) * 64],
                            K_tile[:, b0: b0 + 64], V_tile[:, b0: b0 + 64],
                            start=True, stop=True))
                        B.emit("t", lambda M, p=p, b0=b0: M.matmul(
                            PS_B[64:128, p * 64:(p + 1) * 64],
                            K_tile[:, b0 + 64: b0 + 128],
                            V_tile[:, b0 + 64: b0 + 128],
                            start=True, stop=True))
                        t_kvtc[tc] = B.emit("t", lambda M, p=p, b0=b0: M.matmul(
                            PS_C[:, p:p + 1],
                            K_tile[:, b0: b0 + 128], onescol[:, 0:1],
                            start=True, stop=True),
                            inc=(st if p == NP - 1 else None))
                    B.wait("v", st, t_kvtc[tc])
                    if tc == 0:
                        B.wait("v", sv, v_ccz)
                    vq(lambda M: M.tensor_add(
                        cc_sb[0:64, 0:NP * 64], cc_sb[0:64, 0:NP * 64],
                        PS_A[0:64, 0:NP * 64]))
                    vq(lambda M: M.tensor_add(
                        cc_sb[64:128, 0:NP * 64], cc_sb[64:128, 0:NP * 64],
                        PS_B[64:128, 0:NP * 64]))
                    v_kvadd = vq(lambda M: M.tensor_add(
                        cc_sb[:, NP * 64: NP * 64 + NP],
                        cc_sb[:, NP * 64: NP * 64 + NP], PS_C[:, 0:NP]))
                t_kv = t_kvtc[TC - 1]
                v_cc = v_kvadd

                B.wait("g", st, t_kv)
                w_q = load_w(dwq, W0, sw)
                w_o = load_w(dwo, W1, B.sem["swb"])
                B.emit("g", lambda M: M.dma_start(
                    out=bias_at[0:1, 0:D], in_=dbq[:, :]), inc=sconsg, amount=16)
                B.emit("g", lambda M: M.dma_start(
                    out=bias_at[0:1, D:2 * D], in_=dbo[:, :]), inc=sconsg, amount=16)
                bias_qo = sconsg.n

                B.consumed("A", sv, v_cc)
                B.consumed("B", sv, v_cc)
                B.consumed("C", sv, v_cc)
                B.wait("y", sv, v_cc)
                y_ccin = B.emit("y", lambda M: M.dma_start(
                    out=dccin[:, :], in_=cc_sb[:, :]), inc=sci, amount=16)
                B.wait("g", sci, y_ccin)
                g_cc = B.emit("g", lambda M: M.collective_compute(
                    "AllReduce", ALU.add,
                    replica_groups=[list(range(0, cfg.group)),
                                    list(range(cfg.group, 2 * cfg.group))],
                    ins=[dccin.ap().opt()], outs=[dccout.ap().opt()]), inc=scc)
                B.wait("g", scc, g_cc)
                g_ccr = B.emit("g", lambda M: M.dma_start(
                    out=cc_sb[:, :], in_=dccout[:, :]), inc=sgd, amount=16)

                phiQ = ascr
                B.wait("t", sw, w_q)
                B.wait("t", sconsg, bias_qo)
                t_q = 0
                v_q = 0
                for i, (oc, ft) in enumerate(
                        [(o, f) for o in range(DC) for f in range(FT)]):
                    key = ["C", "D"][i % 2]
                    ps = PS[key]
                    B.wait_slot("t", key)
                    for dc in range(DC):
                        B.emit("t", lambda M, ps=ps, dc=dc, oc=oc, ft=ft: M.matmul(
                            ps[:, 0:512],
                            W0[:, dc * D + oc * 128: dc * D + (oc + 1) * 128],
                            xlns(dc, ft * 512, (ft + 1) * 512),
                            start=(dc == 0), stop=False))
                    t_q = B.emit("t", lambda M, ps=ps, oc=oc: M.matmul(
                        ps[:, 0:512], bias_at[0:1, oc * 128:(oc + 1) * 128],
                        onesbf[0:1, :], start=False, stop=True), inc=st)
                    B.wait("v", st, t_q)
                    v1 = vq(lambda M, ps=ps: M.tensor_scalar_min(
                        sc1[:, :], ps[:, 0:512], 0.0))
                    B.wait("s", sv, v1)
                    s1 = B.emit("s", lambda M: M.activation(
                        sc2[:, :], sc1[:, :], AF.Exp), inc=ss)
                    B.wait("v", ss, s1)
                    v_q = vq(lambda M, ps=ps, oc=oc, ft=ft: M.scalar_tensor_tensor(
                        phiQ[:, oc * T + ft * 512: oc * T + (ft + 1) * 512],
                        ps[:, 0:512], 0.0, sc2[:, :], ALU.max, ALU.add))
                    B.consumed(key, sv, v_q)

                B.wait("v", sgd, g_ccr)
                vq(lambda M: M.tensor_copy(
                    kvbf_sb[:, :], cc_sb[:, 0:NP * 64]))
                vq(lambda M: M.memset(zbd_sb[:, :], 0.0))
                v_kvz = 0
                for dc in range(DC):
                    vq(lambda M, dc=dc: M.tensor_copy(
                        zbd_sb[0:64, dc * NH + 2 * dc: dc * NH + 2 * dc + 1],
                        cc_sb[0:64, NP * 64 + dc: NP * 64 + dc + 1]))
                    v_kvz = vq(lambda M, dc=dc: M.tensor_copy(
                        zbd_sb[64:128, dc * NH + 2 * dc + 1: dc * NH + 2 * dc + 2],
                        cc_sb[64:128, NP * 64 + dc: NP * 64 + dc + 1]))

                B.wait("t", sv, max(v_kvz, v_q))
                B.wait_slot("t", "Wstat")
                t_nrm = 0
                for qq in range(FT):
                    for dc in range(DC):
                        t_nrm = B.emit("t", lambda M, qq=qq, dc=dc: M.matmul(
                            PS_W[0:NH, qq * 512:(qq + 1) * 512],
                            zbd_sb[:, dc * NH:(dc + 1) * NH],
                            phiQ[:, dc * T + qq * 512: dc * T + (qq + 1) * 512],
                            start=(dc == 0), stop=(dc == DC - 1)),
                            inc=(st if (qq == FT - 1 and dc == DC - 1) else None))
                B.wait("v", st, t_nrm)
                vq(lambda M: M.tensor_scalar(
                    PS_W[32:32 + NH, 0:T], PS_W[0:NH, 0:T],
                    float(cfg.attn_eps), None, ALU.add))

                def _recip_lp(M):
                    with nc.allow_low_precision(reason="bf16 attn norm recip"):
                        return M.reciprocal(recip_sb[:, :], PS_W[32:32 + NH, 0:T])
                v_rec = vq(_recip_lp)
                B.consumed("Wstat", sv, v_rec)

                B.wait("t", sv, v_rec)
                B.wait("t", se16, y_e16)
                okeys = [("A", "B"), ("C", "D")]
                v_fm = 0
                for i, (p, ft) in enumerate(
                        [(p, f) for p in range(NP) for f in range(FT)]):
                    kO, kR = okeys[i % 2]
                    psO, psR = PS[kO], PS[kR]
                    B.wait_slot("t", kO)
                    B.wait_slot("t", kR)
                    B.emit("t", lambda M, psO=psO, p=p, ft=ft: M.matmul(
                        psO[0:64, 0:512], kvbf_sb[0:64, p * 64:(p + 1) * 64],
                        phiQ[0:64, p * T + ft * 512: p * T + (ft + 1) * 512],
                        start=True, stop=True))
                    B.emit("t", lambda M, psO=psO, p=p, ft=ft: M.matmul(
                        psO[64:128, 0:512], kvbf_sb[64:128, p * 64:(p + 1) * 64],
                        phiQ[64:128, p * T + ft * 512: p * T + (ft + 1) * 512],
                        start=True, stop=True))
                    tv = B.emit("t", lambda M, psR=psR, p=p, ft=ft: M.matmul(
                        psR[:, 0:512], e16_sb[:, p * 128:(p + 1) * 128],
                        recip_sb[:, ft * 512:(ft + 1) * 512],
                        start=True, stop=True), inc=st)
                    B.wait("v", st, tv)
                    vq(lambda M, psR=psR: M.tensor_copy(
                        sc1[:, :], psR[:, 0:512]))
                    v_fm = vq(lambda M, psO=psO, p=p, ft=ft: M.tensor_mul(
                        xlns(p, ft * 512, (ft + 1) * 512), psO[:, 0:512],
                        sc1[:, :]))
                    B.consumed(kO, sv, v_fm)
                    B.consumed(kR, sv, v_fm)

                B.wait("t", B.sem["swb"], w_o)
                B.wait("t", sv, v_fm)
                v_out = 0
                for i, (oc, ft) in enumerate(
                        [(o, f) for o in range(DC) for f in range(FT)]):
                    key = ["C", "D"][i % 2]
                    ps = PS[key]
                    B.wait_slot("t", key)
                    for dc in range(DC):
                        B.emit("t", lambda M, ps=ps, dc=dc, oc=oc, ft=ft: M.matmul(
                            ps[:, 0:512],
                            W1[:, dc * D + oc * 128: dc * D + (oc + 1) * 128],
                            xlns(dc, ft * 512, (ft + 1) * 512),
                            start=(dc == 0), stop=False))
                    tv = B.emit("t", lambda M, ps=ps, oc=oc: M.matmul(
                        ps[:, 0:512], bias_at[0:1, D + oc * 128: D + (oc + 1) * 128],
                        onesbf[0:1, :], start=False, stop=True), inc=st)
                    B.wait("v", st, tv)
                    v_out = B.emit("v", lambda M, ps=ps, oc=oc, ft=ft: M.tensor_add(
                        xs(oc, ft * 512, (ft + 1) * 512),
                        xs(oc, ft * 512, (ft + 1) * 512), ps[:, 0:512]), inc=sv)
                    B.consumed(key, sv, v_out)
                return v_out

        B.wait("s", sx, y_xload)
        B.wait("t", sx, y_xload)
        v1 = emit_ffn(v_sb8, "_f1")
        B.wait("s", sv, v1)
        v2 = emit_attn(v1)
        B.wait("s", sv, v2)
        v3 = emit_ffn(v2, "_f2")

        B.wait("y", sv, v3)
        osrc = x_sb.ap().rearrange("p (a j) -> p a j", a=DC)
        odst = dy.ap().rearrange("(a p) j -> p a j", p=128)
        y_out = B.emit("y", lambda M, o=odst, i=osrc: M.dma_start(out=o, in_=i),
                       inc=syo, amount=16)
        B.wait("y", syo, y_out)

        B.finalize()
    return nc


def _host_constants(cfg):
    NH, NP, TC = cfg.NH, cfg.NP, cfg.TC
    i128 = np.eye(128, dtype=np.float32).astype(BF)
    e16 = np.zeros((NH, NP * 128), dtype=BF)
    for p in range(NP):
        e16[2 * p, p * 128: p * 128 + 64] = 1.0
        e16[2 * p + 1, p * 128 + 64: (p + 1) * 128] = 1.0
    ltri = np.triu(np.ones((128, 128), np.float32), 1).astype(BF)
    ltriT = np.triu(np.ones((TC, TC), np.float32), 1)
    iTf = np.eye(TC, dtype=np.float32)
    return i128, e16, ltri, ltriT, iTf


def stage_inputs(cfg, x, Wq, bq, Wk, bk, Wv, bv, Wo, bo, rw, upw, gatew, downw,
                 sup, sgate, sdown, n_cores=8):
    f32 = np.float32
    x = np.asarray(x, f32)
    Bb = x.shape[0]
    per_batch = n_cores // Bb
    i128, e16, ltri, ltriT, iTf = _host_constants(cfg)
    common = {
        "wq": np.ascontiguousarray(np.asarray(Wq, f32).T),
        "wk": np.ascontiguousarray(np.asarray(Wk, f32).T),
        "wv": np.ascontiguousarray(np.asarray(Wv, f32).T),
        "wo": np.ascontiguousarray(np.asarray(Wo, f32).T),
        "bq": np.asarray(bq, f32).reshape(1, -1),
        "bk": np.asarray(bk, f32).reshape(1, -1),
        "bv": np.asarray(bv, f32).reshape(1, -1),
        "bo": np.asarray(bo, f32).reshape(1, -1),
        "rw": np.ascontiguousarray(np.asarray(rw, f32)),
        "upw": np.ascontiguousarray(np.asarray(upw, f32)),
        "gatew": np.ascontiguousarray(np.asarray(gatew, f32)),
        "downw": np.ascontiguousarray(np.asarray(downw, f32)),
        "sup": np.ascontiguousarray(np.asarray(sup, f32)),
        "sgate": np.ascontiguousarray(np.asarray(sgate, f32)),
        "sdown": np.ascontiguousarray(np.asarray(sdown, f32)),
        "i128": i128, "e16": e16, "ltri": ltri, "ltriT": ltriT, "iTf": iTf,
        "slots": np.zeros((cfg.SR, cfg.D), dtype=BF),
    }
    in_maps = []
    for c in range(n_cores):
        b = c // per_batch
        t0 = (c % per_batch) * cfg.T
        m = dict(common)
        m["x"] = np.ascontiguousarray(x[b, t0:t0 + cfg.T, :].T)
        in_maps.append(m)
    return in_maps


def gather_out(cfg, results, Bb, S, D, n_cores=8):
    per_batch = n_cores // Bb
    out = np.empty((Bb, S, D), dtype=np.float32)
    for c in range(n_cores):
        b = c // per_batch
        t0 = (c % per_batch) * cfg.T
        out[b, t0:t0 + cfg.T, :] = results[c]["y"].T
    return out


def kernel(L0_attn_x, L0_attn_Wq, L0_attn_bq, L0_attn_Wk, L0_attn_bk,
           L0_attn_Wv, L0_attn_bv, L0_attn_Wo, L0_attn_bo,
           L0_ffn_router_w, L0_ffn_up_w, L0_ffn_gate_w, L0_ffn_down_w,
           L0_ffn_shared_up_w, L0_ffn_shared_gate_w, L0_ffn_shared_down_w):
    x = np.asarray(L0_attn_x, dtype=np.float32)
    Bb, S, D = x.shape
    upw = np.asarray(L0_ffn_up_w)
    cfg = Cfg(T=(Bb * S) // 8, D=D, HF=upw.shape[2], Er=upw.shape[0],
              E=np.asarray(L0_ffn_router_w).shape[1], group=8 // Bb)
    in_maps = stage_inputs(cfg, x, L0_attn_Wq, L0_attn_bq, L0_attn_Wk,
                           L0_attn_bk, L0_attn_Wv, L0_attn_bv, L0_attn_Wo,
                           L0_attn_bo, L0_ffn_router_w, L0_ffn_up_w,
                           L0_ffn_gate_w, L0_ffn_down_w, L0_ffn_shared_up_w,
                           L0_ffn_shared_gate_w, L0_ffn_shared_down_w)
    nc = build_nc(cfg)
    res = run_bass_kernel_spmd(nc, in_maps, core_ids=list(range(8)))
    return gather_out(cfg, res.results, Bb, S, D)
